# version 17
# baseline (speedup 1.0000x reference)
import numpy as np
import ml_dtypes
import concourse.bass as bass
import concourse.mybir as mybir
from concourse.bass_utils import run_bass_kernel_spmd

F32 = mybir.dt.float32
BF16 = mybir.dt.bfloat16
U32 = mybir.dt.uint32
AF = mybir.ActivationFunctionType
ALU = mybir.AluOpType
NPBF = ml_dtypes.bfloat16

T = 2048
D = 1024
NDC = 8
TGW = 512
NTG = T // TGW
S_FULL = 4096
NCORE = 8


class Prog:
    def __init__(self, nc):
        self.nc = nc
        self.blk = None
        self.sems = {}
        self.cnt = {}

    def s(self, name):
        if name not in self.sems:
            self.sems[name] = self.nc.alloc_semaphore(name=f"sem_{name}")
            self.cnt[name] = 0
        return self.sems[name]


def inc(pr, inst, name, n=1):
    inst.then_inc(pr.s(name), n)
    pr.cnt[name] += n
    return pr.cnt[name]


def ser(pr, eng, inst, name="chx"):
    inc(pr, inst, name)
    eng.wait_ge(pr.s(name), pr.cnt[name])
    return inst


def pipe(pr, jobs, banks, ev_eng="vector", mm="mm", ev="ev"):
    nb = len(banks)
    base_mm = pr.cnt[mm]
    base_ev = pr.cnt[ev]
    n = len(jobs)
    pr.s(mm); pr.s(ev)

    def tbody(t):
        for i, job in enumerate(jobs):
            if i >= nb:
                t.wait_ge(pr.s(ev), base_ev + i - nb + 1)
            if 'pre_mm' in job:
                job['pre_mm'](t)
            job['mm'](t, banks[i % nb]).then_inc(pr.s(mm), 1)

    def ebody(e):
        for i, job in enumerate(jobs):
            e.wait_ge(pr.s(mm), base_mm + i + 1)
            if 'pre_ev' in job:
                job['pre_ev'](e)
            job['ev'](e, banks[i % nb]).then_inc(pr.s(ev), 1)

    pr.blk.tensor(tbody)
    pr.cnt[mm] += n
    getattr(pr.blk, ev_eng)(ebody)
    pr.cnt[ev] += n
    return base_mm, base_ev


def dma_seq(pr, eng, items, sem, nsem=1):
    names = [sem] if nsem == 1 else [f"{sem}{i}" for i in range(nsem)]
    for n in names:
        pr.s(n)
    def semof(j, it):
        return it[3] if len(it) > 3 else names[j % nsem]
    for j, it in enumerate(items):
        pr.s(semof(j, it))
    def body(e):
        for j, it in enumerate(items):
            dst, src, waits = it[0], it[1], it[2]
            for (sn, v) in waits:
                e.wait_ge(pr.s(sn), v)
            e.dma_start(out=dst, in_=src).then_inc(pr.s(semof(j, it)), 16)
    getattr(pr.blk, eng)(body)
    for j, it in enumerate(items):
        pr.cnt[semof(j, it)] += 16
    return pr.cnt[names[0]]


def ceil_div(a, b):
    return (a + b - 1) // b


class SbAl:
    def __init__(self, nc):
        self.nc = nc
        self._cache = {}

    def t(self, name, shape, dtype):
        if name in self._cache:
            return self._cache[name]
        h = self.nc.alloc_sbuf_tensor("sb_" + name, shape, dtype)
        self._cache[name] = h
        return h

    def overlay(self, name, shape, dtype, base_name, byte_off=0):
        if name in self._cache:
            return self._cache[name]
        base = self.nc.lookup_mloc(self._cache[base_name]).addr + byte_off
        h = self.nc.alloc_sbuf_tensor_at("sb_" + name, shape, dtype, offset=base)
        self._cache[name] = h
        return h



def load_consts(pr, al, io):
    ones_bf = al.t("c_ones_bf", [128, 1], BF16)
    ones_f32 = al.t("c_ones_f32", [1, 128], F32)
    al.ones_bf = ones_bf
    al.ones_f32 = ones_f32
    dma_seq(pr, "sync", [
        (ones_bf[:, :], io("c_ones_bf"), []),
        (ones_f32[0:1, :], io("c_ones_f32"), []),
    ], "cst")
    return pr.cnt["cst"]


def emit_ln(pr, al, banks, *, x_sb=None, x_dc=None, out_tile, tag,
            affine_cols=None, cst_wait=0, out_f32=False, out_pre=None,
            gp_post=None, ev_post=None):
    if x_dc is None:
        x_dc = lambda dc: x_sb[:, dc * T:(dc + 1) * T]
    sq = al.t(f"lnsq_{tag}", [128, 2, T], BF16)
    xb = al.t(f"lnxb_{tag}", [128, 2, T], BF16)
    rows = al.t(f"lnrows_{tag}", [1, 4 * TGW], F32)
    bc = al.t(f"lnbc_{tag}", [128, 2 * T], F32)
    nrm = al.t(f"lnnrm_{tag}", [128, 4, TGW], F32)
    stats_ps = banks[:NTG]
    bbA, bbB = banks[NTG], banks[NTG + 1]
    for s in ("mm", "ac", "ev", "gp"):
        pr.s(s)
    mm0 = pr.cnt["mm"]; ac0 = pr.cnt["ac"]; ev0 = pr.cnt["ev"]

    def acb(a):
        a.wait_ge(pr.s("cst"), cst_wait)
        for dc in range(NDC):
            if dc >= 2:
                a.wait_ge(pr.s("mm"), mm0 + (dc - 1))
            inc(pr, a.activation(sq[:, dc % 2, :], x_dc(dc),
                                 AF.Square), "ac")
    pr.blk.scalar(acb)

    def evb(e):
        for dc in range(NDC):
            if dc >= 2:
                e.wait_ge(pr.s("mm"), mm0 + (dc - 1))
            inc(pr, e.tensor_copy(xb[:, dc % 2, :], x_dc(dc)), "ev")
    pr.blk.vector(evb)

    def tb(t):
        for dc in range(NDC):
            t.wait_ge(pr.s("ac"), ac0 + dc + 1)
            t.wait_ge(pr.s("ev"), ev0 + dc + 1)
            last = None
            for tg in range(NTG):
                t.matmul(stats_ps[tg][0:1, :], al.ones_bf[:, 0:1],
                         xb[:, dc % 2, tg * TGW:(tg + 1) * TGW],
                         start=(dc == 0), stop=(dc == NDC - 1),
                         skip_group_check=True)
                last = t.matmul(stats_ps[tg][64:65, :], al.ones_bf[:, 0:1],
                                sq[:, dc % 2, tg * TGW:(tg + 1) * TGW],
                                start=(dc == 0), stop=(dc == NDC - 1),
                                skip_group_check=True)
            last.then_inc(pr.s("mm"), 1)
            pr.cnt["mm"] += 1
    pr.blk.tensor(tb)
    mm_stats = pr.cnt["mm"]

    R = lambda i: rows[0:1, i * TGW:(i + 1) * TGW]
    for tg in range(NTG):
        def evfin(e, tg=tg):
            e.wait_ge(pr.s("mm"), mm_stats)
            if tg > 0:
                e.wait_ge(pr.s("ev"), pr._ln_bcast_done)
            e.tensor_scalar_mul(R(0), stats_ps[tg][0:1, :], 1.0 / D)
            e.tensor_scalar_mul(R(1), stats_ps[tg][64:65, :], 1.0 / D)
            e.tensor_tensor(R(2), R(0), R(0), ALU.mult)
            e.tensor_tensor(R(1), R(1), R(2), ALU.subtract)
            inc(pr, e.tensor_scalar_add(R(1), R(1), 1e-5), "ev")
        pr.blk.vector(evfin)
        evv = pr.cnt["ev"]
        def acfin(a, tg=tg):
            a.wait_ge(pr.s("ev"), evv)
            a.activation(R(2), R(1), AF.Ln)
            inc(pr, a.activation(R(2), R(2), AF.Exp, scale=-0.5), "ac")
        pr.blk.scalar(acfin)
        acv = pr.cnt["ac"]
        def evfin2(e, tg=tg):
            e.wait_ge(pr.s("ac"), acv)
            inc(pr, e.scalar_tensor_tensor(R(3), R(0), -1.0, R(2),
                                           ALU.mult, ALU.mult), "ev")
        pr.blk.vector(evfin2)
        evv2 = pr.cnt["ev"]
        def tbb(t, tg=tg):
            t.wait_ge(pr.s("ev"), evv2)
            t.matmul(bbA[0:128, 0:TGW], al.ones_f32[0:1, :], R(2),
                     start=True, stop=True)
            inc(pr, t.matmul(bbB[0:128, 0:TGW], al.ones_f32[0:1, :], R(3),
                             start=True, stop=True), "mm")
        pr.blk.tensor(tbb)
        mmb = pr.cnt["mm"]
        def evbc(e, tg=tg):
            e.wait_ge(pr.s("mm"), mmb)
            e.tensor_copy(bc[:, tg * TGW:(tg + 1) * TGW], bbA[:, 0:TGW])
            inc(pr, e.tensor_copy(bc[:, T + tg * TGW:T + (tg + 1) * TGW],
                                  bbB[:, 0:TGW]), "ev")
        pr.blk.vector(evbc)
        pr._ln_bcast_done = pr.cnt["ev"]
    bc_done = pr.cnt["ev"]

    gp0 = pr.cnt["gp"]
    def gpb(g):
        g.wait_ge(pr.s("ev"), bc_done)
        j = 0
        for dc in range(NDC):
            for tg in range(NTG):
                if j >= 4:
                    g.wait_ge(pr.s("ev"), ev_nrm_base + (j - 3))
                inc(pr, g.tensor_tensor(nrm[:, j % 4, :],
                                        x_dc(dc)[:, tg * TGW:(tg + 1) * TGW],
                                        bc[:, tg * TGW:(tg + 1) * TGW], ALU.mult), "gp")
                j += 1
                if gp_post is not None:
                    gp_post(g, j - 1, ev_nrm_base)
        if gp_post is not None:
            gp_post(g, NDC * NTG, ev_nrm_base)
    ev_nrm_base = pr.cnt["ev"]
    pr.blk.gpsimd(gpb)

    if affine_cols is None:
        ev_out_base = pr.cnt["ev"]
        def evn(e):
            j = 0
            for dc in range(NDC):
                for tg in range(NTG):
                    e.wait_ge(pr.s("gp"), gp0 + j + 1)
                    if out_pre is not None:
                        out_pre(e, j)
                    inc(pr, e.tensor_tensor(out_tile(dc, tg), nrm[:, j % 4, :],
                                            bc[:, T + tg * TGW:T + (tg + 1) * TGW],
                                            ALU.add), "ev")
                    if ev_post is not None:
                        ev_post(e, j)
                    j += 1
        pr.blk.vector(evn)
        return ev_out_base
    else:
        gcol, bcol = affine_cols
        nrm2 = al.t(f"lnnrm2_{tag}", [128, 4, TGW], F32)
        ev_out_base = pr.cnt["ev"]
        def evn(e):
            j = 0
            for dc in range(NDC):
                for tg in range(NTG):
                    e.wait_ge(pr.s("gp"), gp0 + j + 1)
                    e.tensor_tensor(nrm2[:, j % 4, :], nrm[:, j % 4, :],
                                    bc[:, T + tg * TGW:T + (tg + 1) * TGW],
                                    ALU.add)
                    inc(pr, e.tensor_scalar(out_tile(dc, tg),
                                            nrm2[:, j % 4, :],
                                            gcol[:, dc:dc + 1],
                                            bcol[:, dc:dc + 1],
                                            ALU.mult, ALU.add), "ev")
                    j += 1
        pr.blk.vector(evn)
        return ev_out_base


def emit_proj_fm(pr, al, banks, *, rhs_tile, wT, n_in, n_out, ev_fn,
                 ev_eng="vector", slab_cols=512, tag="", tgs=None,
                 ev_sem="ev"):
    if tgs is None:
        tgs = list(range(NTG))
    pr.s("mm"); pr.s(ev_sem)
    ndc = n_in // 128
    wre = wT.rearrange("(dc p) o -> p dc o", p=128)
    nslab = ceil_div(n_out, slab_cols)
    wslab = al.t(f"wsl_{ndc}x{slab_cols}", [128, 2, ndc, slab_cols], BF16)
    pr.s("wdma0"); pr.s("wdma1")
    base_w = [pr.cnt["wdma0"], pr.cnt["wdma1"]]
    base_mm = pr.cnt["mm"]
    slabs = []
    for s in range(nslab):
        c0 = s * slab_cols
        cw = min(slab_cols, n_out - c0)
        slabs.append((s, c0, cw, ceil_div(cw, 128)))
    jobs_before = [0]
    for (s, c0, cw, nrt) in slabs:
        jobs_before.append(jobs_before[-1] + nrt * len(tgs))
    items = []
    for (s, c0, cw, nrt) in slabs:
        if s >= 2:
            waits = [("mm", base_mm + jobs_before[s - 1])]
        else:
            waits = [("mm", base_mm)]
        items.append((wslab[:, s % 2, :, 0:cw], wre[:, :, c0:c0 + cw], waits))
    dma_seq(pr, "gpsimd", items, "wdma", nsem=2)
    jobs = []
    for (s, c0, cw, nrt) in slabs:
        for rti in range(nrt):
            r0 = c0 + rti * 128
            rw = min(128, n_out - r0)
            rt = r0 // 128
            for tg in tgs:
                def mk_mm(s=s, rti=rti, rw=rw, tg=tg):
                    def mmf(t, ps):
                        t.wait_ge(pr.s(f"wdma{s % 2}"),
                                  base_w[s % 2] + 16 * (s // 2 + 1))
                        last = None
                        for dc in range(ndc):
                            last = t.matmul(
                                ps[0:rw, 0:TGW],
                                wslab[:, s % 2, dc, rti * 128:rti * 128 + rw],
                                rhs_tile(dc, tg),
                                start=(dc == 0), stop=(dc == ndc - 1))
                        return last
                    return mmf
                def mk_ev(rt=rt, rw=rw, tg=tg):
                    return lambda e, ps: ev_fn(e, rt, rw, tg, ps)
                jobs.append({'mm': mk_mm(), 'ev': mk_ev()})
    return pipe(pr, jobs, banks, ev_eng=ev_eng, ev=ev_sem)


def ring_out(pr, al, *, n_jobs, tile_shape, dtype, dst_fn, ev_base,
             dsem, nring=4, ring_name=None, ev_sem="ev"):
    if ring_name is None:
        ring_name = f"ring_{'x'.join(map(str, tile_shape))}_{dtype}"
    ring = al.t(ring_name, [128, nring] + tile_shape, dtype)
    items = []
    for j in range(n_jobs):
        items.append((dst_fn(j), ring[:, j % nring],
                      [(ev_sem, ev_base + j + 1)]))
    names = [f"{dsem}{i}" for i in range(nring)]
    for n in names:
        pr.s(n)
    base_d = [pr.cnt[n] for n in names]
    dma_seq(pr, "sync", items, dsem, nsem=nring)

    def pre_ev(j):
        if j < nring:
            return [(names[i], base_d[i]) for i in range(nring)
                    if base_d[i] > 0]
        return [(names[j % nring], base_d[j % nring] + 16 * (j // nring))]
    return ring, pre_ev



def stage1(pr, al, banks, io):
    cstv = load_consts(pr, al, io)
    xstage = al.t("xstage", [128, NDC * T], F32)
    xln = al.t("xln", [128, NDC * T], BF16)
    x0 = io("x0")
    dma_seq(pr, "sync", [(xstage[:, dc * T:(dc + 1) * T], x0[dc], [])
                         for dc in range(NDC)], "xin")
    xin_done = pr.cnt["xin"]
    def w0(a):
        a.wait_ge(pr.s("xin"), xin_done)
    pr.blk.scalar(w0)
    pr.blk.vector(lambda e: e.wait_ge(pr.s("xin"), xin_done))
    emit_ln(pr, al, banks, x_sb=xstage,
            out_tile=lambda dc, tg: xln[:, dc * T + tg * TGW: dc * T + (tg + 1) * TGW],
            tag="ln1", cst_wait=cstv)

    def xt(dc, tg):
        return xln[:, dc * T + tg * TGW: dc * T + (tg + 1) * TGW]

    bq = al.t("bq_cols", [128, NDC], F32)
    bk = al.t("bk_cols", [128, NDC], F32)
    bvr = al.t("bv_row", [1, D], F32)
    dma_seq(pr, "sync", [(bq[:, :], io("bq_cols"), []),
                         (bk[:, :], io("bk_cols"), []),
                         (bvr[0:1, :], io("bv_row"), [])], "cstb")
    cst2 = pr.cnt["cstb"]

    ln_done = pr.cnt["ev"]
    pr.blk.tensor(lambda t: t.wait_ge(pr.s("ev"), ln_done))

    def do_proj(nm, bias, wname):
        dst = io(nm + "_d")
        njobs = NDC * NTG
        ev_base = pr.cnt["ev"]
        ring, pre_ev = ring_out(pr, al, n_jobs=njobs,
                                tile_shape=[TGW], dtype=BF16,
                                dst_fn=lambda j, dst=dst: dst[j // NTG, :,
                                                     (j % NTG) * TGW:(j % NTG + 1) * TGW],
                                ev_base=ev_base, dsem="dout")
        jcount = [0]
        def ev_fn(e, rt, rw, tg, ps, ring=ring, pre_ev=pre_ev, bias=bias, jc=jcount):
            j = jc[0]; jc[0] += 1
            for (sn, v) in pre_ev(j):
                e.wait_ge(pr.s(sn), v)
            return e.tensor_scalar_add(ring[:, j % 4], ps[0:128, :],
                                       bias[:, rt:rt + 1])
        def mkpre(a, cst2=cst2):
            a.wait_ge(pr.s("cstb"), cst2)
        pr.blk.vector(mkpre)
        emit_proj_fm(pr, al, banks[:4], rhs_tile=xt, wT=io(wname),
                     n_in=D, n_out=D, ev_fn=ev_fn, ev_eng="vector", tag=nm)

    do_proj("k0", bk, "wk0T")
    kv_snap = {"k": [pr.cnt.get(f"dout{i}", 0) for i in range(4)]}

    bvbc = al.t("bvbc", [128, D], F32)
    def tbv(t):
        t.wait_ge(pr.s("cstb"), cst2)
        t.matmul(banks[4][0:128, 0:TGW], al.ones_f32[0:1, :], bvr[0:1, 0:TGW],
                 start=True, stop=True)
        inc(pr, t.matmul(banks[5][0:128, 0:TGW], al.ones_f32[0:1, :],
                         bvr[0:1, TGW:2 * TGW], start=True, stop=True), "mm")
    pr.blk.tensor(tbv)
    mmv = pr.cnt["mm"]
    def evbv(e):
        e.wait_ge(pr.s("mm"), mmv)
        e.tensor_copy(bvbc[:, 0:TGW], banks[4][:, 0:TGW])
        inc(pr, e.tensor_copy(bvbc[:, TGW:2 * TGW], banks[5][:, 0:TGW]), "ev")
    pr.blk.vector(evbv)

    vdst = io("v0_d")
    wvre = io("wv0T").rearrange("(dc p) o -> p dc o", p=128)
    wslab = al.t(f"wsl_{NDC}x{TGW}", [128, 2, NDC, TGW], BF16)
    pr.s("wdma0"); pr.s("wdma1")
    base_w = [pr.cnt["wdma0"], pr.cnt["wdma1"]]
    base_mm = pr.cnt["mm"]
    items = [(wslab[:, s % 2, :, :], wvre[:, :, s * TGW:(s + 1) * TGW],
              [("mm", base_mm)]) for s in range(2)]
    dma_seq(pr, "gpsimd", items, "wdma", nsem=2)
    njobs = 2 * 16
    ev_base = pr.cnt["ev"]
    ring, pre_ev = ring_out(pr, al, n_jobs=njobs,
                            tile_shape=[TGW], dtype=BF16,
                            dst_fn=lambda j: vdst[j % 16, :,
                                                  (j // 16) * TGW:(j // 16 + 1) * TGW],
                            ev_base=ev_base, dsem="dout")
    jobs = []
    for s in range(2):
        for tt in range(16):
            j = s * 16 + tt
            def mk_mm(s=s, tt=tt):
                def mmf(t, ps):
                    t.wait_ge(pr.s(f"wdma{s % 2}"), base_w[s % 2] + 16)
                    last = None
                    for dc in range(NDC):
                        last = t.matmul(
                            ps[0:128, 0:TGW],
                            xln[:, dc * T + tt * 128: dc * T + (tt + 1) * 128],
                            wslab[:, s % 2, dc, :],
                            start=(dc == 0), stop=(dc == NDC - 1))
                    return last
                return mmf
            def mk_ev(s=s, tt=tt, j=j):
                def evf(e, ps):
                    for (sn, v) in pre_ev(j):
                        e.wait_ge(pr.s(sn), v)
                    return e.tensor_tensor(ring[:, j % 4], ps[0:128, :],
                                           bvbc[:, s * TGW:(s + 1) * TGW], ALU.add)
                return evf
            jobs.append({'mm': mk_mm(), 'ev': mk_ev()})
    pipe(pr, jobs, banks[:4])
    kv_snap["v"] = [pr.cnt.get(f"dout{i}", 0) for i in range(4)]
    do_proj("q0", bq, "wq0T")
    pr.s("ccs")
    def ccb(g):
        for i in range(4):
            if kv_snap["k"][i] > 0:
                g.wait_ge(pr.s(f"dout{i}"), kv_snap["k"][i])
        inc(pr, g.collective_compute(
            "AllGather", mybir.AluOpType.bypass, replica_groups=CC_GROUPS,
            ins=[io("k0_d__2d").opt()], outs=[io("k0cc").opt()]), "ccs")
        for i in range(4):
            if kv_snap["v"][i] > kv_snap["k"][i]:
                g.wait_ge(pr.s(f"dout{i}"), kv_snap["v"][i])
        inc(pr, g.collective_compute(
            "AllGather", mybir.AluOpType.bypass, replica_groups=CC_GROUPS,
            ins=[io("v0_d__2d").opt()], outs=[io("v0cc").opt()]), "ccs")
    pr.blk.gpsimd(ccb)



def _fmT(x):
    return np.ascontiguousarray(x.T.reshape(NDC, 128, x.shape[0]))


def _cols(v):
    return np.ascontiguousarray(v.reshape(NDC, 128).T)



def _l0_attn_consts(h):
    lg = np.float32(np.log(np.float32(0.95)))
    jls, slots = l0_jlists()
    gcol = np.zeros((128, NTG * 32), np.float32)
    tri = np.ones((NTG, 8, 128, 512), NPBF)
    for ig in range(NTG):
        i0 = h * T + TGW * ig
        for jc in range(32):
            delta = i0 - 128 * jc
            if delta <= -TGW:
                continue
            expo = (np.float32(i0) - (128 * jc + np.arange(128, dtype=np.float32)))
            gcol[:, ig * 32 + jc] = np.exp(expo * lg).astype(np.float32)
        for s, jc in enumerate(slots[ig]):
            delta = i0 - 128 * jc
            if -TGW < delta < 128:
                jj = np.arange(128)[:, None]
                ii = np.arange(512)[None, :]
                tri[ig, s] = (ii + delta >= jj).astype(NPBF)
    gii = np.broadcast_to(np.exp(np.arange(512, dtype=np.float32) * lg)[None, :],
                          (128, 512)).astype(np.float32)
    return gcol, np.ascontiguousarray(gii), tri


def _interleave_gu(g, u):
    out = np.empty((1024, 2816), np.float32)
    for k in range(11):
        out[:, 256 * k:256 * k + 128] = g[128 * k:128 * (k + 1)].T
        out[:, 256 * k + 128:256 * (k + 1)] = u[128 * k:128 * (k + 1)].T
    return out


def host_prepare3(ip, shared):
    gu = ip["L0_ffn_experts_gate_up"]
    dn = ip["L0_ffn_experts_down"]
    sg = ip["L0_ffn_shared_gate_w"]
    su = ip["L0_ffn_shared_up_w"]
    sd = ip["L0_ffn_shared_down_w"]
    guT = np.empty((10, 1024, 2816), np.float32)
    gdT = np.empty((10, 1408, 1024), np.float32)
    for e in range(8):
        guT[e] = _interleave_gu(gu[e, :1408], gu[e, 1408:])
        gdT[e] = np.ascontiguousarray(dn[e].T)
    guT[8] = _interleave_gu(sg[:1408], su[:1408])
    guT[9] = _interleave_gu(sg[1408:], su[1408:])
    gdT[8] = np.ascontiguousarray(sd[:, :1408].T)
    gdT[9] = np.ascontiguousarray(sd[:, 1408:].T)
    shared["guT"] = guT
    shared["gdT"] = gdT
    shared["gwsT"] = np.ascontiguousarray(
        np.concatenate([ip["L0_ffn_gate_w"].T,
                        ip["L0_ffn_shared_expert_gate_w"].T], axis=1))
    ew = [0, 1, 2, 3, 4, 5, 6, 7, 8, 8]
    sel = np.zeros((16, 10, 128), np.float32)
    for e in range(10):
        sel[ew[e], e, :] = 1.0
    shared["c_sel"] = sel
    shared["c_ident"] = np.eye(128, dtype=np.float32)


def _eo_perm():
    perm = np.empty(1024, np.int64)
    for h in range(16):
        for m in range(32):
            perm[h * 32 + m] = h * 64 + 2 * m
            perm[512 + h * 32 + m] = h * 64 + 2 * m + 1
    return perm


def _rope1_tables(h):
    m = (np.arange(128) % 32).astype(np.float32)
    inv_freq = (1.0 / (10000.0 ** (2.0 * m / 64.0))).astype(np.float32)
    t = (h * T + np.arange(T)).astype(np.float32)
    freqs = (t[None, :] * inv_freq[:, None]).astype(np.float32)
    sv = ((2.0 * m + 0.4 * 64.0) / (1.4 * 64.0)).astype(np.float32)
    power = ((t - S_FULL // 2) / 512.0).astype(np.float32)
    scale = (sv[:, None] ** power[None, :]).astype(np.float32)
    cos = np.cos(freqs.astype(np.float64)).astype(np.float32)
    sin = np.sin(freqs.astype(np.float64)).astype(np.float32)
    out = np.empty((128, 4, T), NPBF)
    out[:, 0] = (cos * scale).astype(NPBF)
    out[:, 1] = (sin * scale).astype(NPBF)
    out[:, 2] = (cos / scale).astype(NPBF)
    out[:, 3] = (sin / scale).astype(NPBF)
    return out


def host_prepare5(ip, shared, per_core):
    wqkv = ip["L1_attn_w_qkv"]
    perm = _eo_perm()
    shared["wq1T"] = np.ascontiguousarray(wqkv[0:1024][perm].T)
    shared["wk1T"] = np.ascontiguousarray(wqkv[1024:2048][perm].T)
    shared["wv1T"] = np.ascontiguousarray(wqkv[2048:3072].T)
    shared["wg1T"] = np.ascontiguousarray(ip["L1_attn_w_gated"].T)
    for c in range(NCORE):
        per_core[c]["rt1"] = _rope1_tables(c % 2)


def _l1_attn_consts(h):
    gammas = (1.0 - 2.0 ** (-5.0 - np.arange(16, dtype=np.float32))).astype(np.float32)
    jls, slots = l1_jlists()
    gcol = np.zeros((128, 16 * NTG * 32), np.float32)
    tri = np.ones((NTG, 8, 128, 512), NPBF)
    for hd in range(16):
        lg = np.float32(np.log(gammas[hd]))
        for ig in range(NTG):
            i0 = h * T + TGW * ig
            for jc in range(32):
                delta = i0 - 128 * jc
                if delta <= -TGW:
                    continue
                expo = (np.float32(i0) - (128 * jc + np.arange(128, dtype=np.float32)))
                with np.errstate(under="ignore"):
                    gcol[:, (hd * NTG + ig) * 32 + jc] =                         np.exp(expo * lg).astype(np.float32)
    for ig in range(NTG):
        i0 = h * T + TGW * ig
        for s, jc in enumerate(slots[ig]):
            delta = i0 - 128 * jc
            if -TGW < delta < 128:
                jj = np.arange(128)[:, None]
                ii = np.arange(512)[None, :]
                tri[ig, s] = (ii + delta >= jj).astype(NPBF)
    gii = np.zeros((128, 8 * TGW), np.float32)
    for hp in range(8):
        for half in range(2):
            g = gammas[2 * hp + half]
            vals = (np.exp(np.arange(512, dtype=np.float32) * np.float32(np.log(g)))
                    * np.float32(64 ** -0.5))
            gii[half * 64:(half + 1) * 64, hp * TGW:(hp + 1) * TGW] = vals[None, :]
    return gcol, gii, tri


def host_prepare67(ip, shared, per_core):
    shared["g1_cols"] = _cols(ip["L1_attn_gn_weight"])
    shared["b1_cols"] = _cols(ip["L1_attn_gn_bias"])
    shared["wp1T"] = np.ascontiguousarray(ip["L1_attn_w_proj"].T)
    for c in range(NCORE):
        gcol, gii, tri = _l1_attn_consts(c % 2)
        per_core[c]["gcol1"] = gcol
        per_core[c]["gii1"] = gii
        per_core[c]["tri1"] = tri


def host_prepare2(ip, shared, per_core):
    shared["wo0T"] = np.ascontiguousarray(ip["L0_attn_Wo"])
    shared["bo_cols"] = _cols(ip["L0_attn_bo"])
    hpo = np.zeros((128, 2), NPBF)
    hpo[0:64, 0] = 1
    hpo[64:128, 1] = 1
    shared["c_hp_ones"] = hpo
    hpoT = np.zeros((2, 128), np.float32)
    hpoT[0, 0:64] = 1
    hpoT[1, 64:128] = 1
    shared["c_hp_onesT"] = hpoT
    shared["g0_cols"] = _cols(ip["L0_attn_gamma"])
    shared["b0_cols"] = _cols(ip["L0_attn_beta"])
    for c in range(NCORE):
        h = c % 2
        gcol, gii, tri = _l0_attn_consts(h)
        per_core[c]["gcol0"] = gcol
        per_core[c]["gii0"] = gii
        per_core[c]["tri0"] = tri
        per_core[c].update(shared)


def exchange_kv(env, kname, vname, kfull, vfull):
    for c in range(NCORE):
        p = c ^ 1
        a, b = (c, p) if c % 2 == 0 else (p, c)
        env[c][kfull] = np.ascontiguousarray(
            np.stack([env[a][kname], env[b][kname]], axis=0))
        vtok = np.concatenate([env[a][vname], env[b][vname]], axis=0)
        env[c][vfull] = np.ascontiguousarray(vtok.transpose(1, 0, 2))


def _fperm():
    perm = np.empty(512, np.int64)
    for rt in range(4):
        half, pp = rt // 2, rt % 2
        for g in range(4):
            for m in range(32):
                perm[rt * 128 + 32 * g + m] = (4 * pp + g) * 64 + 32 * half + m
    return perm


def _ffn_rope_tables(h):
    m = (np.arange(128) % 32).astype(np.float32)
    inv_freq = (1.0 / (10000.0 ** (2.0 * m / 64.0))).astype(np.float32)
    t = (h * T + np.arange(T)).astype(np.float32)
    freqs = t[None, :] * inv_freq[:, None]
    base = ((2.0 * m + 0.4 * 64.0) / (1.4 * 64.0)).astype(np.float32)
    power = ((t - S_FULL // 2) / 256.0).astype(np.float32)
    with np.errstate(under="ignore"):
        scale = (base[:, None] ** power[None, :]).astype(np.float32)
    cos = np.cos(freqs.astype(np.float64)).astype(np.float32)
    sin = np.sin(freqs.astype(np.float64)).astype(np.float32)
    out = np.empty((128, 4, T), NPBF)
    out[:, 0] = (cos * scale).astype(NPBF)
    out[:, 1] = (sin * scale).astype(NPBF)
    out[:, 2] = (cos / scale).astype(NPBF)
    out[:, 3] = (sin / scale).astype(NPBF)
    return out


def _ffn_masks(h):
    msk = np.zeros((2, 8, 128, TGW), np.float32)
    jj = np.arange(128)[:, None]
    ii = np.arange(TGW)[None, :]
    for wcl in range(2):
        for jc in range(8):
            if jc < 4:
                vis = (128 * jc + jj) >= ii
                if wcl == 0 and h == 0:
                    vis = np.zeros_like(vis)
            else:
                vis = (128 * (jc - 4) + jj) <= ii
            msk[wcl, jc] = np.where(vis, 0.0, -1e30)
    return msk.astype(NPBF)


def _ffn_w1(w1, b1):
    A = np.zeros((1024, 2816), np.float32); A[:, :2730] = w1[:, :2730]
    G = np.zeros((1024, 2816), np.float32); G[:, :2730] = w1[:, 2730:]
    ba = np.zeros(2816, np.float32); ba[:2730] = b1[:2730]
    bg = np.zeros(2816, np.float32); bg[:2730] = b1[2730:]
    out = np.empty((2, 1024, 2816), np.float32)
    bc = np.empty((128, 2, 22), np.float32)
    for e in range(2):
        for k in range(11):
            col = e * 1408 + k * 128
            out[e][:, 256 * k:256 * k + 128] = A[:, col:col + 128]
            out[e][:, 256 * k + 128:256 * k + 256] = G[:, col:col + 128]
            bc[:, e, 2 * k] = ba[col:col + 128]
            bc[:, e, 2 * k + 1] = bg[col:col + 128]
    return out, bc


def host_prepare_ffn(ip, shared, per_core):
    perm = _fperm()
    for l in range(2):
        wqkv = ip["L1_ffn_wqkv"][l]
        shared[f"wfq{l}T"] = np.ascontiguousarray(wqkv[:, 0:512][:, perm])
        shared[f"wfk{l}T"] = np.ascontiguousarray(wqkv[:, 512:1024][:, perm])
        shared[f"wfv{l}T"] = np.ascontiguousarray(wqkv[:, 1024:1536])
        shared[f"wfo{l}T"] = np.ascontiguousarray(ip["L1_ffn_wo"][l])
        qs = ip["L1_ffn_q_scale"][l].astype(np.float32)
        ks = ip["L1_ffn_k_scale"][l].astype(np.float32)
        m = np.arange(128) % 32
        qc = np.zeros((128, 8), np.float32)
        qc[:, 0] = qs[m]; qc[:, 1] = qs[m]
        qc[:, 2] = qs[32 + m]; qc[:, 3] = qs[32 + m]
        qc[:, 4] = ks[m]; qc[:, 5] = ks[m]
        qc[:, 6] = ks[32 + m]; qc[:, 7] = ks[32 + m]
        shared[f"qksc{l}"] = qc
        qkr = np.zeros((1, 128), np.float32)
        qkr[0, 0:64] = qs; qkr[0, 64:128] = ks
        shared[f"qkr{l}"] = qkr
        w1T, b1c = _ffn_w1(ip["L1_ffn_w1"][l], ip["L1_ffn_b1"][l])
        shared[f"fw1T{l}"] = w1T
        shared[f"fb1c{l}"] = b1c
        w2p = np.zeros((2816, 1024), np.float32)
        w2p[:2730] = ip["L1_ffn_w2"][l]
        shared[f"fw2T{l}"] = np.ascontiguousarray(w2p.reshape(2, 1408, 1024))
        shared[f"fb2c{l}"] = _cols(ip["L1_ffn_b2"][l])
        shared[f"flnc{l}"] = np.concatenate(
            [_cols(ip["L1_ffn_ln_g"][l]), _cols(ip["L1_ffn_ln_b"][l])], 1)
    shared["c_ident_bf"] = np.eye(128, dtype=NPBF)
    hb4 = np.zeros((128, 4), NPBF)
    hb4T = np.zeros((4, 128), np.float32)
    for g in range(4):
        hb4[32 * g:32 * g + 32, g] = 1
        hb4T[g, 32 * g:32 * g + 32] = 1
    shared["c_hb4"] = hb4
    shared["c_hb4T"] = hb4T
    for c in range(NCORE):
        per_core[c]["rtf"] = _ffn_rope_tables(c % 2)
        per_core[c]["mskf"] = _ffn_masks(c % 2)


def host_prepare(inputs):
    ip = {k: np.asarray(v) for k, v in inputs.items()}
    shared = {}
    shared["c_ones_bf"] = np.ones((128, 1), NPBF)
    shared["c_ones_f32"] = np.ones((1, 128), np.float32)
    shared["wq0T"] = np.ascontiguousarray(ip["L0_attn_Wq"])
    shared["wk0T"] = np.ascontiguousarray(ip["L0_attn_Wk"])
    shared["wv0T"] = np.ascontiguousarray(ip["L0_attn_Wv"])
    shared["bq_cols"] = _cols(ip["L0_attn_bq"])
    shared["bk_cols"] = _cols(ip["L0_attn_bk"])
    shared["bv_row"] = ip["L0_attn_bv"][None, :].astype(np.float32)
    per_core = []
    x = ip["L0_attn_x"]
    for c in range(NCORE):
        b, h = c // 2, c % 2
        d = dict(shared)
        d["x0"] = _fmT(x[b, h * T:(h + 1) * T])
        d["c_isodd"] = np.full((128, 1), float(h), np.float32)
        per_core.append(d)
    host_prepare3(ip, shared)
    host_prepare2(ip, shared, per_core)
    host_prepare5(ip, shared, per_core)
    host_prepare67(ip, shared, per_core)
    host_prepare_ffn(ip, shared, per_core)
    for nm in ("wq0T", "wk0T", "wv0T", "wo0T", "wq1T", "wk1T", "wv1T",
               "wg1T", "wp1T", "guT", "gdT",
               "wfq0T", "wfk0T", "wfv0T", "wfo0T", "fw1T0", "fw2T0",
               "wfq1T", "wfk1T", "wfv1T", "wfo1T", "fw1T1", "fw2T1"):
        shared[nm] = np.asarray(shared[nm]).astype(NPBF)
    for c in range(NCORE):
        per_core[c].update(shared)
    return per_core



BUFS = {
    "x0": ([NDC, 128, T], np.float32),
    "c_ones_bf": ([128, 1], NPBF),
    "c_ones_f32": ([1, 128], np.float32),
    "wq0T": ([D, D], NPBF),
    "wk0T": ([D, D], NPBF),
    "wv0T": ([D, D], NPBF),
    "bq_cols": ([128, NDC], np.float32),
    "bk_cols": ([128, NDC], np.float32),
    "bv_row": ([1, D], np.float32),
    "q0_d": ([NDC, 128, T], NPBF),
    "k0_d": ([NDC, 128, T], NPBF),
    "v0_d": ([16, 128, D], NPBF),
    "k_full": ([2, NDC, 128, T], NPBF),
    "v_full": ([128, S_FULL // 128, D], NPBF),
    "gcol0": ([128, 4 * 32], np.float32),
    "gii0": ([128, 512], np.float32),
    "tri0": ([4, 8, 128, 512], NPBF),
    "c_hp_ones": ([128, 2], NPBF),
    "wo0T": ([D, D], NPBF),
    "bo_cols": ([128, NDC], np.float32),
    "o0_d": ([NDC, 128, T], np.float32),
    "st0_d": ([2, 16], np.float32),
    "st0_p": ([2, 16], np.float32),
    "c_hp_onesT": ([2, 128], np.float32),
    "g0_cols": ([128, NDC], np.float32),
    "b0_cols": ([128, NDC], np.float32),
    "x1_d": ([NDC, 128, T], np.float32),
    "xln2_d": ([NDC, 128, T], NPBF),
    "x2_d": ([NDC, 128, T], np.float32),
    "guT": ([10, D, 2816], NPBF),
    "gdT": ([10, 1408, D], NPBF),
    "gwsT": ([D, 9], np.float32),
    "c_sel": ([16, 10, 128], np.float32),
    "c_ident": ([128, 128], np.float32),
    "wq1T": ([D, D], NPBF),
    "wk1T": ([D, D], NPBF),
    "wv1T": ([D, D], NPBF),
    "wg1T": ([D, D], NPBF),
    "rt1": ([128, 4, T], NPBF),
    "q1_d": ([NDC, 128, T], NPBF),
    "k1_d": ([NDC, 128, T], NPBF),
    "v1_d": ([16, 128, D], NPBF),
    "gated_d": ([NDC, 128, T], NPBF),
    "k1_full": ([2, NDC, 128, T], NPBF),
    "v1_full": ([128, S_FULL // 128, D], NPBF),
    "gcol1": ([128, 16 * NTG * 32], np.float32),
    "gii1": ([128, 8 * TGW], np.float32),
    "tri1": ([4, 8, 128, 512], NPBF),
    "y1_d": ([NDC, 128, T], np.float32),
    "st1_d": ([2, 16], np.float32),
    "st1_p": ([2, 16], np.float32),
    "g1_cols": ([128, NDC], np.float32),
    "b1_cols": ([128, NDC], np.float32),
    "wp1T": ([D, D], NPBF),
    "x3_d": ([NDC, 128, T], np.float32),
    "z_d": ([NDC, 128, T], np.float32),
    "zbf_d": ([NDC, 128, T], NPBF),
    "c_ident_bf": ([128, 128], NPBF),
    "c_hb4": ([128, 4], NPBF),
    "c_hb4T": ([4, 128], np.float32),
    "rtf": ([128, 4, T], NPBF),
    "mskf": ([2, 8, 128, TGW], NPBF),
    "out_d": ([NDC, 128, T], np.float32),
    "z3_0d": ([NDC, 128, T], np.float32),
    "z3bf_0d": ([NDC, 128, T], NPBF),
    "c_isodd": ([128, 1], np.float32),
    "zlast": ([128, NDC], np.float32),
    "z2last": ([128, NDC], np.float32),
    "z3last": ([128, NDC], np.float32),
}
for _l in range(2):
    BUFS.update({
        f"wfq{_l}T": ([D, 512], NPBF),
        f"wfk{_l}T": ([D, 512], NPBF),
        f"wfv{_l}T": ([D, 512], NPBF),
        f"wfo{_l}T": ([512, D], NPBF),
        f"qksc{_l}": ([128, 8], np.float32),
        f"qkr{_l}": ([1, 128], np.float32),
        f"fw1T{_l}": ([2, D, 2816], NPBF),
        f"fw2T{_l}": ([2, 1408, D], NPBF),
        f"fb1c{_l}": ([128, 2, 22], np.float32),
        f"fb2c{_l}": ([128, NDC], np.float32),
        f"flnc{_l}": ([128, 2 * NDC], np.float32),
        f"qf{_l}_d": ([4, 128, T], NPBF),
        f"kf{_l}_d": ([4, 128, T], NPBF),
        f"vf{_l}_d": ([16, 128, 520], NPBF),
        f"kb{_l}_x": ([128, 4, TGW], NPBF),
        f"vb{_l}_x": ([128, 4, 520], NPBF),
        f"zg8_{_l}": ([128, NDC], NPBF),
        f"zg32_{_l}": ([128, NDC], np.float32),
        f"z2_{_l}d": ([NDC, 128, T], np.float32),
        f"z2bf_{_l}d": ([NDC, 128, T], NPBF),
    })

MLDT = {np.float32: F32, NPBF: BF16, np.uint32: U32}


def run_stage(emit_fn, reads, writes, env_list, nexec=NCORE, rename=None):
    rename = rename or {}
    nc = bass.Bass("TRN2", target_bir_lowering=False)
    pr = Prog(nc)
    al = SbAl(nc)
    handles = {}
    for nm in reads:
        shape, dt = BUFS[nm]
        handles[nm] = nc.dram_tensor(nm, list(shape), MLDT[dt],
                                     kind="ExternalInput").ap()
    for nm in writes:
        shape, dt = BUFS[nm]
        handles[nm] = nc.dram_tensor(nm, list(shape), MLDT[dt],
                                     kind="ExternalOutput").ap()
    banks = [nc.alloc_psum_tensor(f"bank{i}", [128, 512], F32) for i in range(8)]
    with nc.Block() as blk:
        pr.blk = blk
        emit_fn(pr, al, banks, lambda nm: handles[rename.get(nm, nm)])
        def closing(s):
            for nm in list(pr.cnt):
                if (nm.startswith("dout") or nm.startswith("xout")
                        or nm.startswith("zo") or nm.startswith("zc")
                        or nm.startswith("rdo") or nm.startswith("fd")
                        or nm.startswith("vdo") or nm.startswith("dbf")
                        ) and pr.cnt[nm] > 0:
                    s.wait_ge(pr.s(nm), pr.cnt[nm])
        pr.blk.sync(closing)
    in_maps = []
    for c in range(NCORE):
        m = {}
        for nm in reads:
            shape, dt = BUFS[nm]
            a = np.asarray(env_list[c][nm])
            assert tuple(a.shape) == tuple(shape), (nm, a.shape, shape)
            m[nm] = a.astype(dt) if a.dtype != dt else a
        in_maps.append(m)
    res = run_bass_kernel_spmd(nc, in_maps, core_ids=list(range(NCORE)))
    return [r for r in res.results]



def _ghost_cols(env, src, dst, dt):
    for c in range(NCORE):
        if c % 2 == 0:
            env[c][dst] = np.zeros((128, NDC), dt)
        else:
            env[c][dst] = np.ascontiguousarray(
                np.asarray(env[c ^ 1][src])[:, :, T - 1].T).astype(dt)


def _kvb_exchange(env, l):
    for c in range(NCORE):
        p = c ^ 1
        env[c][f"kb{l}_x"] = np.ascontiguousarray(
            np.asarray(env[p][f"kf{l}_d"])[:, :, T - TGW:].transpose(1, 0, 2))
        env[c][f"vb{l}_x"] = np.ascontiguousarray(
            np.asarray(env[p][f"vf{l}_d"])[12:16].transpose(1, 0, 2))


def kernel_multilaunch(**inputs):
    env = [dict(p) for p in host_prepare(inputs)]

    def rs(fn, reads, writes, rename=None):
        outs = run_stage(fn, reads, writes, env, rename=rename)
        for c in range(NCORE):
            env[c].update(outs[c])

    CST = ["c_ones_bf", "c_ones_f32"]
    rs(stage1, ["x0", "wq0T", "wk0T", "wv0T", "bq_cols", "bk_cols",
                "bv_row"] + CST, ["q0_d", "k0_d", "v0_d"])
    exchange_kv(env, "k0_d", "v0_d", "k_full", "v_full")
    rs(stage2, ["q0_d", "k_full", "v_full", "c_hp_ones", "gcol0", "gii0",
                "tri0", "wo0T", "bo_cols"] + CST, ["o0_d", "st0_d"])
    for c in range(NCORE):
        env[c]["st0_p"] = env[c ^ 1]["st0_d"]
    rs(stage3, ["x0", "o0_d", "st0_d", "st0_p", "c_hp_onesT", "g0_cols",
                "b0_cols"] + CST, ["x1_d", "xln2_d"])
    rs(stage4, ["x1_d", "xln2_d", "guT", "gdT", "gwsT", "c_sel",
                "c_ident"] + CST, ["x2_d"])
    rs(stage5, ["x2_d", "wq1T", "wk1T", "wv1T", "wg1T", "rt1"] + CST,
       ["q1_d", "k1_d", "v1_d", "gated_d"])
    exchange_kv(env, "k1_d", "v1_d", "k1_full", "v1_full")
    rs(stage6, ["q1_d", "k1_full", "v1_full", "gcol1", "gii1", "tri1",
                "c_hp_ones"] + CST, ["y1_d", "st1_d"])
    for c in range(NCORE):
        env[c]["st1_p"] = env[c ^ 1]["st1_d"]
    rs(stage7, ["x2_d", "y1_d", "st1_d", "st1_p", "gated_d", "g1_cols",
                "b1_cols", "wp1T", "c_hp_onesT"] + CST,
       ["x3_d", "z_d", "zbf_d"])

    zf32, zbf = "z_d", "zbf_d"
    for l in range(2):
        _ghost_cols(env, zbf, f"zg8_{l}", NPBF)
        rs(lambda pr, al, banks, io, l=l: stage_ffn_qkv(pr, al, banks, io, l),
           [zbf, f"zg8_{l}", "rtf", f"qksc{l}", f"wfq{l}T", f"wfk{l}T",
            f"wfv{l}T", "c_hb4", "c_hb4T"] + CST,
           [f"qf{l}_d", f"kf{l}_d", f"vf{l}_d"],
           rename={"zin": zbf, "zg": f"zg8_{l}", "qksc": f"qksc{l}",
                   "wfqT": f"wfq{l}T", "wfkT": f"wfk{l}T",
                   "wfvT": f"wfv{l}T", "qf_d": f"qf{l}_d",
                   "kf_d": f"kf{l}_d", "vf_d": f"vf{l}_d"})
        _kvb_exchange(env, l)
        rs(stage_ffn_attn,
           [zf32, f"qf{l}_d", f"kf{l}_d", f"vf{l}_d", f"kb{l}_x",
            f"vb{l}_x", "mskf", "c_ident_bf", f"qkr{l}", f"wfo{l}T"] + CST,
           [f"z2_{l}d", f"z2bf_{l}d"],
           rename={"z32in": zf32, "qf_d": f"qf{l}_d", "kf_d": f"kf{l}_d",
                   "vf_d": f"vf{l}_d", "kb_x": f"kb{l}_x",
                   "vb_x": f"vb{l}_x", "qkr": f"qkr{l}",
                   "wfoT": f"wfo{l}T", "z2out": f"z2_{l}d",
                   "z2bfout": f"z2bf_{l}d"})
        _ghost_cols(env, f"z2_{l}d", f"zg32_{l}", np.float32)
        final = (l == 1)
        zout = "out_d" if final else "z3_0d"
        reads = [f"z2_{l}d", f"zg32_{l}", f"flnc{l}", f"fb1c{l}",
                 f"fb2c{l}", f"fw1T{l}", f"fw2T{l}"] + CST
        rename = {"z32in": f"z2_{l}d", "zg32": f"zg32_{l}",
                  "lncols": f"flnc{l}", "fb1cols": f"fb1c{l}",
                  "fb2cols": f"fb2c{l}", "fw1T": f"fw1T{l}",
                  "fw2T": f"fw2T{l}", "zout": zout,
                  "zbfout": "z3bf_0d"}
        writes = [zout] if final else [zout, "z3bf_0d"]
        if final:
            reads = reads + ["x3_d"]
        rs(lambda pr, al, banks, io, f=final: stage_ffn_ffn(pr, al, banks,
                                                            io, final=f),
           reads, writes, rename=rename)
        zf32, zbf = "z3_0d", "z3bf_0d"

    x = np.asarray(inputs["L0_attn_x"])
    out = np.empty(x.shape, np.float32)
    for c in range(NCORE):
        b, h = c // 2, c % 2
        out[b, h * T:(h + 1) * T] = env[c]["out_d"].reshape(D, T).T
    return out



CC_GROUPS = [[0, 1], [2, 3], [4, 5], [6, 7]]

EXTERNAL_INPUTS = [
    "x0", "wq0T", "wk0T", "wv0T", "bq_cols", "bk_cols", "bv_row",
    "c_ones_bf", "c_ones_f32", "c_hp_ones", "c_hp_onesT",
    "gcol0", "gii0", "tri0", "wo0T", "bo_cols", "g0_cols", "b0_cols",
    "guT", "gdT", "gwsT", "c_sel", "c_ident",
    "wq1T", "wk1T", "wv1T", "wg1T", "rt1",
    "gcol1", "gii1", "tri1", "g1_cols", "b1_cols", "wp1T",
    "rtf", "mskf", "c_ident_bf", "c_hb4", "c_hb4T", "c_isodd",
] + [f"{nm}{l}{sfx}" for l in range(2) for (nm, sfx) in
     (("wfq", "T"), ("wfk", "T"), ("wfv", "T"), ("wfo", "T"),
      ("qksc", ""), ("qkr", ""), ("fw1T", ""), ("fw2T", ""),
      ("fb1c", ""), ("fb2c", ""), ("flnc", ""))]

FUSED_INTERNALS = [
    "q0_d", "k0_d", "v0_d", "o0_d", "st0_d", "x1_d", "xln2_d", "x2_d",
    "q1_d", "k1_d", "v1_d", "gated_d", "y1_d", "st1_d",
    "x3_d", "z_d", "zbf_d", "z3_0d", "z3bf_0d",
] + [f"{nm}{l}{sfx}" for l in range(2) for (nm, sfx) in
     (("qf", "_d"), ("kf", "_d"), ("vf", "_d"), ("z2_", "d"),
      ("z2bf_", "d"), ("zg8_", ""), ("zg32_", ""))]


class RecAl(SbAl):

    def __init__(self, nc, rec):
        super().__init__(nc)
        self.rec = rec
        self.step = "*"
        self.base_of = {}

    def _charge(self, name):
        name = self.base_of.get(name, name)
        self.rec[name][3].add(self.step)

    def t(self, name, shape, dtype):
        if name in self._cache:
            self._charge(name)
            return self._cache[name]
        bpp = int(np.prod(shape[1:])) * (2 if dtype == BF16 else 4)
        r = self.rec.setdefault(name, [list(shape), dtype, bpp, set()])
        assert r[0] == list(shape) and r[1] == dtype, (name, shape, r)
        r[3].add(self.step)
        h = self.nc.alloc_sbuf_tensor_at("sb_" + name, shape, dtype, offset=0)
        self._cache[name] = h
        return h

    def overlay(self, name, shape, dtype, base_name, byte_off=0):
        if name in self._cache:
            self._charge(name)
            return self._cache[name]
        base_name = self.base_of.get(base_name, base_name)
        self.base_of[name] = base_name
        self.rec[base_name][3].add(self.step)
        return super().overlay(name, shape, dtype, base_name, byte_off)


def _place(rec, align=256):
    occ = {}

    def check_steps(ss):
        if "*" in ss:
            return set(occ) | {"*"}
        return set(ss) | {"*"}

    def rec_steps(ss):
        return {"*"} if "*" in ss else set(ss)

    def fits(off, size, chk):
        for st in chk:
            for (lo, hi) in occ.get(st, []):
                if off < hi and lo < off + size:
                    return False
        return True

    pos = {}
    total = 0
    for nm in sorted(rec, key=lambda n: (-rec[n][2], n)):
        shape, dt, bpp, ss = rec[nm]
        size = (bpp + align - 1) // align * align
        chk = check_steps(ss)
        cands = {0}
        for st in chk:
            for (lo, hi) in occ.get(st, []):
                cands.add((hi + align - 1) // align * align)
        off = min(o for o in sorted(cands) if fits(o, size, chk))
        pos[nm] = off
        for st in rec_steps(ss):
            occ.setdefault(st, []).append((off, off + size))
        total = max(total, off + size)
    return pos, total


def _mk_handles(nc, dbg=()):
    h = {}
    for nm in EXTERNAL_INPUTS:
        shape, dt = BUFS[nm]
        h[nm] = nc.dram_tensor(nm, list(shape), MLDT[dt],
                               kind="ExternalInput").ap()
    for nm in FUSED_INTERNALS:
        shape, dt = BUFS[nm]
        kind = "ExternalOutput" if nm in dbg else "Internal"
        h[nm] = nc.dram_tensor(nm, list(shape), MLDT[dt], kind=kind).ap()
    shape, dt = BUFS["out_d"]
    h["out_d"] = nc.dram_tensor("out_d", list(shape), MLDT[dt],
                                kind="ExternalOutput").ap()

    def internal(nm, shape, dt):
        kind = "ExternalOutput" if nm in dbg else "Internal"
        t = nc.dram_tensor(nm, shape, dt, kind=kind)
        h[nm] = t.ap()
        return t

    for l, knm, vnm in ((0, "k0", "v0"), (1, "k1", "v1")):
        kcc = internal(f"{knm}cc", [2 * NDC * 128, T], BF16)
        vcc = internal(f"{vnm}cc", [2 * 16 * 128, D], BF16)
        h[f"{knm}ccr"] = kcc.ap().rearrange("(r d p) t -> r d p t",
                                            r=2, d=NDC, p=128)
        h[f"{vnm}ccp"] = vcc.ap().rearrange("(t p) d -> p t d", p=128)
    for snm in ("st0", "st1"):
        scc = internal(f"{snm}cc", [4, 16], F32)
        h[f"{snm}cc_a"] = scc.ap()[0:2, :]
        h[f"{snm}cc_b"] = scc.ap()[2:4, :]
    for g in range(4):
        internal(f"zl{g}", [128, NDC], F32)
        internal(f"zgt{g}", [2 * 128, NDC], F32)
        internal(f"zcb{g}", [NDC * 128, 1], F32)
        internal(f"zgc{g}", [2 * NDC * 128, 1], F32)
    for l in range(2):
        internal(f"kbb{l}", [4 * 128, TGW], BF16)
        kbcc = internal(f"kbcc{l}", [2 * 4 * 128, TGW], BF16)
        vbcc = internal(f"vbcc{l}", [2 * 4 * 128, 520], BF16)
        h[f"kb{l}_x"] = kbcc.ap()[0:4 * 128, :].rearrange(
            "(c p) t -> p c t", c=4, p=128)
        h[f"vb{l}_x"] = vbcc.ap()[0:4 * 128, :].rearrange(
            "(c p) f -> p c f", c=4, p=128)
    h["k0_d__2d"] = h["k0_d"].rearrange("a p t -> (a p) t")
    h["k1_d__2d"] = h["k1_d"].rearrange("a p t -> (a p) t")
    h["v0_d__2d"] = h["v0_d"].rearrange("a p t -> (a p) t")
    h["v1_d__2d"] = h["v1_d"].rearrange("a p t -> (a p) t")
    return h


def _boundary(pr, al, io):
    stamp = al.t("b_stamp", [1, 1], F32)
    pr.s("stg")
    waits = [(nm, pr.cnt[nm]) for nm in sorted(pr.cnt)
             if pr.cnt[nm] > 0 and nm != "stg"]

    def sb(s):
        for nm, v in waits:
            s.wait_ge(pr.s(nm), v)
        s.dma_start(out=stamp[0:1, 0:1],
                    in_=io("c_ones_f32")[0:1, 0:1]).then_inc(pr.s("stg"), 16)
    pr.blk.sync(sb)
    pr.cnt["stg"] += 16
    tgt = pr.cnt["stg"]
    for eng in ("tensor", "vector", "scalar", "gpsimd"):
        getattr(pr.blk, eng)(lambda e, tgt=tgt: e.wait_ge(pr.s("stg"), tgt))


def _final_wait(pr, al, io):
    waits = [(nm, pr.cnt[nm]) for nm in sorted(pr.cnt) if pr.cnt[nm] > 0]

    def sb(s):
        for nm, v in waits:
            s.wait_ge(pr.s(nm), v)
    pr.blk.sync(sb)


def cc_step(pairs):
    def emit(pr, al, banks, io):
        pr.s("ccs")

        def gb(g):
            for (src, dst) in pairs:
                inc(pr, g.collective_compute(
                    "AllGather", mybir.AluOpType.bypass,
                    replica_groups=CC_GROUPS,
                    ins=[io(src).opt()], outs=[io(dst).opt()]), "ccs")
        pr.blk.gpsimd(gb)
    return emit


GHOST_OLD = False
ZL_WRITES = True
MOE_OLD = False
ATTN_OLD = False


def cc_col_step_old(src_name, g):
    def emit(pr, al, banks, io):
        pr.s("ccb"); pr.s("ccs")

        def gb(g_):
            g_.dma_start(
                out=io(f"zcb{g}"),
                in_=io(src_name).rearrange("d p t -> (d p) t")[:, T - 1:T],
                allow_slow_non_contiguous=True,
            ).then_inc(pr.s("ccb"), 16)
            pr.cnt["ccb"] += 16
            g_.wait_ge(pr.s("ccb"), pr.cnt["ccb"])
            inc(pr, g_.collective_compute(
                "AllGather", mybir.AluOpType.bypass, replica_groups=CC_GROUPS,
                ins=[io(f"zcb{g}").opt()], outs=[io(f"zgc{g}").opt()]), "ccs")
        pr.blk.gpsimd(gb)
    return emit


def gh_step_old(g, dst_name, to_bf16):
    def emit(pr, al, banks, io):
        g32 = al.t("gh_g32", [128, NDC], F32)
        gbf = al.t("gh_gbf", [128, NDC], BF16)
        iso = al.t("gh_iso", [128, 1], F32)
        src = io(f"zgc{g}")
        dma_seq(pr, "sync",
                [(g32[:, dc:dc + 1], src[dc * 128:(dc + 1) * 128, :], [])
                 for dc in range(NDC)] +
                [(iso[:, :], io("c_isodd"), [])], "ghl")
        ghl = pr.cnt["ghl"]
        pr.s("ghe")

        def eb(e):
            e.wait_ge(pr.s("ghl"), ghl)
            ser(pr, e, e.tensor_scalar_mul(g32[:, :], g32[:, :], iso[:, 0:1]))
            if to_bf16:
                ser(pr, e, e.tensor_copy(gbf[:, :], g32[:, :]))
            inc(pr, e.tensor_copy(g32[:, 0:1], g32[:, 0:1]), "ghe")
        pr.blk.vector(eb)
        tile = gbf if to_bf16 else g32
        dma_seq(pr, "sync", [(io(dst_name), tile[:, :],
                              [("ghe", pr.cnt["ghe"])])], "gho")
    return emit


def cc_last_step(g):
    def emit(pr, al, banks, io):
        pr.s("ccs")

        def gb(g_):
            inc(pr, g_.collective_compute(
                "AllGather", mybir.AluOpType.bypass, replica_groups=CC_GROUPS,
                ins=[io(f"zl{g}").opt()], outs=[io(f"zgt{g}").opt()]), "ccs")
        pr.blk.gpsimd(gb)
    return emit


def cc_kb_step(l):
    def emit(pr, al, banks, io):
        pr.s("ccb"); pr.s("ccs")

        def gb(g_):
            g_.dma_start(
                out=io(f"kbb{l}"),
                in_=io(f"kf{l}_d").rearrange("c p t -> (c p) t")[:, T - TGW:T]
            ).then_inc(pr.s("ccb"), 16)
            pr.cnt["ccb"] += 16
            g_.wait_ge(pr.s("ccb"), pr.cnt["ccb"])
            inc(pr, g_.collective_compute(
                "AllGather", mybir.AluOpType.bypass, replica_groups=CC_GROUPS,
                ins=[io(f"kbb{l}").opt()], outs=[io(f"kbcc{l}").opt()]), "ccs")
            inc(pr, g_.collective_compute(
                "AllGather", mybir.AluOpType.bypass, replica_groups=CC_GROUPS,
                ins=[io(f"vf{l}_d")[12:16].rearrange("c p f -> (c p) f").opt()],
                outs=[io(f"vbcc{l}").opt()]), "ccs")
        pr.blk.gpsimd(gb)
    return emit


def gh_step(g, dst_name, to_bf16):
    def emit(pr, al, banks, io):
        al.t("xstage", [128, NDC * T], F32)
        g32 = al.t("gh_g32", [128, NDC], F32)
        gbf = al.t("gh_gbf", [128, NDC], BF16)
        iso = al.t("gh_iso", [128, 1], F32)
        dma_seq(pr, "sync",
                [(g32[:, :], io(f"zgt{g}")[0:128, :], []),
                 (iso[:, :], io("c_isodd"), [])], "ghl")
        ghl = pr.cnt["ghl"]
        pr.s("ghe")

        def eb(e):
            e.wait_ge(pr.s("ghl"), ghl)
            ser(pr, e, e.tensor_scalar_mul(g32[:, :], g32[:, :], iso[:, 0:1]))
            if to_bf16:
                ser(pr, e, e.tensor_copy(gbf[:, :], g32[:, :]))
            inc(pr, e.tensor_copy(g32[:, 0:1], g32[:, 0:1]), "ghe")
        pr.blk.vector(eb)
        tile = gbf if to_bf16 else g32
        dma_seq(pr, "sync", [(io(dst_name), tile[:, :],
                              [("ghe", pr.cnt["ghe"])])], "gho")
    return emit


def fused_steps():
    steps = [
        (stage1, {}),
        (stage2, {"k_full": "k0ccr", "v_full": "v0ccp"}),
        (cc_step([("st0_d", "st0cc")]), {}),
        (stage3, {"st0_d": "st0cc_a", "st0_p": "st0cc_b"}),
        (stage4, {}),
        (stage5, {}),
        (stage6, {"k1_full": "k1ccr", "v1_full": "v1ccp"}),
        (cc_step([("st1_d", "st1cc")]), {}),
        (stage7, {"st1_d": "st1cc_a", "st1_p": "st1cc_b", "zlast": "zl0"}),
    ]
    zf32, zbf = "z_d", "zbf_d"
    ghsrc8 = {0: "z_d", 1: "z3_0d"}
    for l in range(2):
        gq = 2 * l
        if GHOST_OLD:
            steps.append((cc_col_step_old(ghsrc8[l], [0, 2][l]), {}))
            steps.append((gh_step_old([0, 2][l], f"zg8_{l}", True), {}))
        else:
            steps.append((cc_last_step([0, 2][l]), {}))
            steps.append((gh_step([0, 2][l], f"zg8_{l}", True), {}))
        steps.append((lambda pr, al, banks, io, l=l:
                      stage_ffn_qkv(pr, al, banks, io, l),
                      {"zin": zbf, "zg": f"zg8_{l}", "qksc": f"qksc{l}",
                       "wfqT": f"wfq{l}T", "wfkT": f"wfk{l}T",
                       "wfvT": f"wfv{l}T", "qf_d": f"qf{l}_d",
                       "kf_d": f"kf{l}_d", "vf_d": f"vf{l}_d"}))
        steps.append((cc_kb_step(l), {}))
        steps.append((lambda pr, al, banks, io, zr=(l == 1):
                      stage_ffn_attn(pr, al, banks, io, z_resident=zr),
                      {"z32in": zf32, "qf_d": f"qf{l}_d", "kf_d": f"kf{l}_d",
                       "vf_d": f"vf{l}_d", "kb_x": f"kb{l}_x",
                       "vb_x": f"vb{l}_x", "qkr": f"qkr{l}",
                       "wfoT": f"wfo{l}T", "z2out": f"z2_{l}d",
                       "z2last": ["zl1", "zl3"][l],
                       "z2bfout": f"z2bf_{l}d"}))
        if GHOST_OLD:
            steps.append((cc_col_step_old(f"z2_{l}d", [1, 3][l]), {}))
            steps.append((gh_step_old([1, 3][l], f"zg32_{l}", False), {}))
        else:
            steps.append((cc_last_step([1, 3][l]), {}))
            steps.append((gh_step([1, 3][l], f"zg32_{l}", False), {}))
        final = (l == 1)
        zout = "out_d" if final else "z3_0d"
        steps.append((lambda pr, al, banks, io, f=final:
                      stage_ffn_ffn(pr, al, banks, io, final=f),
                      {"z32in": f"z2_{l}d", "zg32": f"zg32_{l}",
                       "lncols": f"flnc{l}", "fb1cols": f"fb1c{l}",
                       "fb2cols": f"fb2c{l}", "fw1T": f"fw1T{l}",
                       "fw2T": f"fw2T{l}", "zout": zout,
                       "z3last": "zl2",
                       "zbfout": "z3bf_0d"}))
        zf32, zbf = "z3_0d", "z3bf_0d"
    return steps


def build_fused(placement=None, rec=None, dbg=(), dbg_copy=None):
    nc = bass.Bass("TRN2", target_bir_lowering=False)
    pr = Prog(nc)
    handles = _mk_handles(nc, dbg=dbg)
    banks = [nc.alloc_psum_tensor(f"bank{i}", [128, 512], F32)
             for i in range(8)]
    if rec is not None:
        al = RecAl(nc, rec)
    else:
        al = SbAl(nc)
        pos, total, prec = placement
        al.t("mega", [128, (total + 3) // 4], F32)
        for nm, (shape, dt, bpp, ss) in prec.items():
            al.overlay(nm, shape, dt, "mega", byte_off=pos[nm])
    steps = fused_steps()
    with nc.Block() as blk:
        pr.blk = blk
        for i, (fn, rename) in enumerate(steps):
            io = lambda nm, rename=rename: handles[rename.get(nm, nm)]
            if i > 0:
                if rec is not None:
                    al.step = "*"
                _boundary(pr, al, io)
            if rec is not None:
                al.step = i
            fn(pr, al, banks, io)
        if rec is not None:
            al.step = "*"
        if dbg_copy:
            pr.s("dbgc")
            waits = [(nm, pr.cnt[nm]) for nm in sorted(pr.cnt)
                     if pr.cnt[nm] > 0]

            def cb(g, waits=waits):
                for nm, v in waits:
                    g.wait_ge(pr.s(nm), v)
                for nm, (shape, dt) in dbg_copy.items():
                    t = nc.dram_tensor("dbg_" + nm, list(shape), dt,
                                       kind="ExternalOutput")
                    g.dma_start(out=t.ap(), in_=handles[nm]).then_inc(
                        pr.s("dbgc"), 16)
                    pr.cnt["dbgc"] += 16
            pr.blk.gpsimd(cb)
        _final_wait(pr, al, lambda nm: handles[nm])
    return nc


def kernel(**inputs):
    env = [dict(p) for p in host_prepare(inputs)]
    rec = {}
    build_fused(rec=rec)
    pos, total = _place(rec)
    nc = build_fused(placement=(pos, total, rec))
    in_maps = []
    for c in range(NCORE):
        m = {}
        for nm in EXTERNAL_INPUTS:
            shape, dt = BUFS[nm]
            a = np.asarray(env[c][nm])
            assert tuple(a.shape) == tuple(shape), (nm, a.shape, shape)
            m[nm] = a.astype(dt) if a.dtype != dt else a
        in_maps.append(m)
    res = run_bass_kernel_spmd(nc, in_maps, core_ids=list(range(NCORE)))
    x = np.asarray(inputs["L0_attn_x"])
    out = np.empty(x.shape, np.float32)
    for c in range(NCORE):
        b, h = c // 2, c % 2
        out[b, h * T:(h + 1) * T] = res.results[c]["out_d"].reshape(D, T).T
    return out



def l0_jlists():
    lg = np.log(0.95)
    jls, slots = [], []
    for g in range(NTG):
        js = set()
        sl = set()
        for half in range(2):
            i0 = half * T + g * TGW
            for jc in range(S_FULL // 128):
                if jc * 128 > i0 + TGW - 1:
                    continue
                dmin = i0 - (jc * 128 + 127)
                if dmin > 0 and np.exp(dmin * lg) < 1e-40:
                    continue
                js.add(jc)
                delta = i0 - jc * 128
                if -TGW < delta < 128:
                    sl.add(jc)
        jls.append(sorted(js))
        slots.append(sorted(sl))
    return jls, slots


def l1_jlists():
    jls, slots = [], []
    for g in range(NTG):
        js = set()
        sl = set()
        for half in range(2):
            i0 = half * T + g * TGW
            for jc in range(S_FULL // 128):
                if jc * 128 > i0 + TGW - 1:
                    continue
                js.add(jc)
                delta = i0 - jc * 128
                if -TGW < delta < 128:
                    sl.add(jc)
        jls.append(sorted(js))
        slots.append(sorted(sl))
    return jls, slots


def emit_attn(pr, al, banks, io, *, q_name, k_name, v_name, y_sb, jls, slots,
              gcol_name, gcol_per_head, gii_name, tri_name, tag,
              stats_stc=None):
    for s in ("amS", "aac", "aev", "agp", "akv", "amY", "aevY", "amsk"):
        pr.s(s)
    q_sb = al.t(f"q_{tag}", [128, NDC, T], BF16)
    khp = al.t("a_khp", [128, 2, S_FULL], BF16)
    vhp = al.t("a_vhp", [128, 2, S_FULL // 128, 128], BF16)
    sbf = al.t("a_sbf", [128, 3, 2, TGW], BF16)
    trim = al.t("a_trim", [128, 8, TGW], BF16)
    gcw = 16 * NTG * 32 if gcol_per_head else NTG * 32
    gcol = al.t(f"a_gcol_{tag}", [128, gcw], F32)
    giw = 8 * TGW if gcol_per_head else TGW
    gii = al.t(f"a_gii_{tag}", [128, giw], F32)
    NJC = S_FULL // 128

    dma_seq(pr, "sync", [
        (q_sb[:, :, :], io(q_name).rearrange("h p t -> p h t"), []),
        (gcol[:, :], io(gcol_name), []),
        (gii[:, :], io(gii_name), []),
    ], "astat")
    stat_loads = pr.cnt["astat"]

    SA = [banks[0], banks[2], banks[6]]
    SB = [banks[1], banks[3], banks[7]]
    YB = [banks[4], banks[5]]

    jobs = [(ig, hp) for ig in range(NTG) for hp in range(8)]
    njobs = len(jobs)
    base_mY = pr.cnt["amY"]
    base_evY = pr.cnt["aevY"]

    pr.s("amsk"); pr.s("akv0"); pr.s("akv1")
    base_msk = pr.cnt["amsk"]
    base_kv = [pr.cnt["akv0"], pr.cnt["akv1"]]
    HT = S_FULL // 2
    items = []
    for ig in range(NTG):
        J0 = ig * 8
        w = [("amY", base_mY + J0)] if ig > 0 else []
        items.append((trim[:, :, :],
                      io(tri_name)[ig].rearrange("s p t -> p s t"), w, "amsk"))
        for hp in range(8):
            J = J0 + hp
            w = [("amY", base_mY + J - 1)] if J >= 2 else [("astat", stat_loads)]
            items.append((khp[:, J % 2, 0:HT], io(k_name)[0, hp], list(w),
                          f"akv{J % 2}"))
            items.append((khp[:, J % 2, HT:S_FULL], io(k_name)[1, hp], list(w),
                          f"akv{J % 2}"))
            items.append((vhp[:, J % 2],
                          io(v_name)[:, :, hp * 128:(hp + 1) * 128],
                          list(w), f"akv{J % 2}"))
    dma_seq(pr, "sync", items, "akv_unused")

    acb = [0] * njobs; evb = [0] * njobs; gpb = [0] * njobs; msb = [0] * njobs
    a_ac = pr.cnt["aac"]; a_ev = pr.cnt["aev"]; a_gp = pr.cnt["agp"]
    a_mS = pr.cnt["amS"]
    for J, (ig, hp) in enumerate(jobs):
        L = len(jls[ig])
        M = len([jc for jc in jls[ig] if jc in slots[ig]])
        msb[J] = a_mS; acb[J] = a_ac; evb[J] = a_ev; gpb[J] = a_gp
        a_mS += L; a_ac += L; a_ev += L; a_gp += 2 * M

    def tbody(t):
        t.wait_ge(pr.s("astat"), stat_loads)
        for J, (ig, hp) in enumerate(jobs):
            jl = jls[ig]
            L = len(jl)
            t.wait_ge(pr.s(f"akv{J % 2}"), base_kv[J % 2] + 48 * (J // 2 + 1))
            if J >= 2:
                t.wait_ge(pr.s("aevY"), base_evY + J - 1)

            def emit_pv(i):
                jc = jl[i]
                t.wait_ge(pr.s("aac"), acb[J] + i + 1)
                t.wait_ge(pr.s("aev"), evb[J] + i + 1)
                if jc in slots[ig]:
                    nmask = len([x for x in jl[:i + 1] if x in slots[ig]])
                    t.wait_ge(pr.s("agp"), gpb[J] + 2 * nmask)
                t.matmul(YB[J % 2][0:64, :],
                         vhp[:, J % 2, jc, 0:64],
                         sbf[:, i % 3, 0, :],
                         start=(i == 0), stop=False, skip_group_check=True)
                return t.matmul(YB[J % 2][64:128, :],
                                vhp[:, J % 2, jc, 64:128],
                                sbf[:, i % 3, 1, :],
                                start=(i == 0), stop=(i == L - 1),
                                skip_group_check=True)

            for i, jc in enumerate(jl):
                if i >= 3:
                    t.wait_ge(pr.s("aac"), acb[J] + i - 2)
                    t.wait_ge(pr.s("aev"), evb[J] + i - 2)
                t.matmul(SA[i % 3][:, :], khp[0:64, J % 2, jc * 128:(jc + 1) * 128],
                         q_sb[0:64, hp, ig * TGW:(ig + 1) * TGW],
                         start=True, stop=True)
                t.matmul(SB[i % 3][:, :],
                         khp[64:128, J % 2, jc * 128:(jc + 1) * 128],
                         q_sb[64:128, hp, ig * TGW:(ig + 1) * TGW],
                         start=True, stop=True).then_inc(pr.s("amS"), 1)
                if i >= 2:
                    emit_pv(i - 2)
            if L >= 2:
                emit_pv(L - 2)
            emit_pv(L - 1).then_inc(pr.s("amY"), 1)
    pr.blk.tensor(tbody)
    pr.cnt["amS"] = a_mS
    pr.cnt["amY"] = base_mY + njobs

    def acbody(a):
        for J, (ig, hp) in enumerate(jobs):
            jl = jls[ig]
            for i, jc in enumerate(jl):
                a.wait_ge(pr.s("amS"), msb[J] + i + 1)
                if gcol_per_head:
                    col = gcol[:, ((2 * hp) * NTG + ig) * 32 + jc:
                               ((2 * hp) * NTG + ig) * 32 + jc + 1]
                else:
                    col = gcol[:, ig * 32 + jc: ig * 32 + jc + 1]
                a.activation(sbf[:, i % 3, 0, :], SA[i % 3][:, :], AF.Copy,
                             scale=col).then_inc(pr.s("aac"), 1)
    pr.blk.scalar(acbody)
    pr.cnt["aac"] = a_ac

    def evbody(e):
        for J, (ig, hp) in enumerate(jobs):
            jl = jls[ig]
            if not ATTN_OLD:
                e.wait_ge(pr.s("amsk"), base_msk + 16 * (ig + 1))
            for i, jc in enumerate(jl):
                e.wait_ge(pr.s("amS"), msb[J] + i + 1)
                if gcol_per_head:
                    col = gcol[:, ((2 * hp + 1) * NTG + ig) * 32 + jc:
                               ((2 * hp + 1) * NTG + ig) * 32 + jc + 1]
                else:
                    col = gcol[:, ig * 32 + jc: ig * 32 + jc + 1]
                e.tensor_scalar(sbf[:, i % 3, 1, :], SB[i % 3][:, :], col, None,
                                ALU.mult).then_inc(pr.s("aev"), 1)
                if (not ATTN_OLD) and jc in slots[ig]:
                    s = slots[ig].index(jc)
                    e.wait_ge(pr.s("aac"), acb[J] + i + 1)
                    e.tensor_tensor(sbf[:, i % 3, 0, :], sbf[:, i % 3, 0, :],
                                    trim[:, s, :], ALU.mult).then_inc(
                                        pr.s("agp"), 1)
                    e.tensor_tensor(sbf[:, i % 3, 1, :], sbf[:, i % 3, 1, :],
                                    trim[:, s, :], ALU.mult).then_inc(
                                        pr.s("agp"), 1)
            if J >= 1:
                emit_yev(e, J - 1)
        emit_yev(e, njobs - 1)
    sqy = (al.t(f"sqy_{tag}", [128, 2, TGW], F32)
           if stats_stc is not None else None)

    def emit_yev(e, J):
        ig, hp = jobs[J]
        e.wait_ge(pr.s("amY"), base_mY + J + 1)
        if gcol_per_head:
            gt = gii[:, hp * TGW:(hp + 1) * TGW]
        else:
            gt = gii[:, 0:TGW]
        yt = y_sb[:, hp * T + ig * TGW: hp * T + (ig + 1) * TGW]
        if stats_stc is None:
            e.tensor_tensor(yt, YB[J % 2][:, :], gt,
                            ALU.mult).then_inc(pr.s("aevY"), 1)
        else:
            e.tensor_tensor(yt, YB[J % 2][:, :], gt, ALU.mult)
            e.reduce_sum(stats_stc[:, 0, hp, ig:ig + 1], yt,
                         mybir.AxisListType.X)
            e.tensor_tensor(sqy[:, J % 2, :], yt, yt, ALU.mult)
            e.reduce_sum(stats_stc[:, 1, hp, ig:ig + 1], sqy[:, J % 2, :],
                         mybir.AxisListType.X).then_inc(pr.s("aevY"), 1)
    pr.blk.vector(evbody)
    pr.cnt["aev"] = a_ev
    pr.cnt["aevY"] = base_evY + njobs

    if ATTN_OLD:
        def gpbody(g):
            for J, (ig, hp) in enumerate(jobs):
                jl = jls[ig]
                g.wait_ge(pr.s("amsk"), base_msk + 16 * (ig + 1))
                for i, jc in enumerate(jl):
                    if jc not in slots[ig]:
                        continue
                    s = slots[ig].index(jc)
                    g.wait_ge(pr.s("aac"), acb[J] + i + 1)
                    g.wait_ge(pr.s("aev"), evb[J] + i + 1)
                    g.tensor_tensor(sbf[:, i % 2, 0, :], sbf[:, i % 2, 0, :],
                                    trim[:, s, :], ALU.mult).then_inc(
                                        pr.s("agp"), 1)
                    g.tensor_tensor(sbf[:, i % 2, 1, :], sbf[:, i % 2, 1, :],
                                    trim[:, s, :], ALU.mult).then_inc(
                                        pr.s("agp"), 1)
        pr.blk.gpsimd(gpbody)
    pr.cnt["agp"] = a_gp



def emit_stats_pass(pr, al, banks, io, *, src_name, dst_name, cst_wait):
    for s in ("mm", "ac", "ev"):
        pr.s(s)
    src = io(src_name)
    ot = al.t("st_ot", [128, 2, TGW], F32)
    ob = al.t("st_ob", [128, 2, TGW], BF16)
    osq = al.t("st_osq", [128, 2, TGW], BF16)
    hp_ones = al.t("st_hpones", [128, 2], BF16)
    stats_sb = al.t("st_out", [2, 16], F32)
    dma_seq(pr, "sync", [(hp_ones[:, :], io("c_hp_ones"), [])], "cstc")
    cst2 = pr.cnt["cstc"]
    bank = banks[6]
    items = []
    for rt in range(NDC):
        for tg in range(NTG):
            j = rt * NTG + tg
            w = [("ac", None), ]
            items.append((ot[:, j % 2, :], src[rt, :, tg * TGW:(tg + 1) * TGW], j))
    ac0 = pr.cnt["ac"]; ev0 = pr.cnt["ev"]; mm0 = pr.cnt["mm"]
    pr.s("sst0"); pr.s("sst1")
    base_sst = [pr.cnt["sst0"], pr.cnt["sst1"]]
    def dmab(s):
        for (dst, srcap, j) in items:
            if j >= 2:
                s.wait_ge(pr.s("mm"), mm0 + (j - 1))
            s.dma_start(out=dst, in_=srcap).then_inc(pr.s(f"sst{j % 2}"), 16)
    pr.blk.sync(dmab)
    for (dst, srcap, j) in items:
        pr.cnt[f"sst{j % 2}"] += 16
    def acb(a):
        a.wait_ge(pr.s("cstc"), cst2)
        for j in range(len(items)):
            a.wait_ge(pr.s(f"sst{j % 2}"), base_sst[j % 2] + 16 * (j // 2 + 1))
            if j >= 2:
                a.wait_ge(pr.s("mm"), mm0 + (j - 1))
            inc(pr, a.activation(osq[:, j % 2, :], ot[:, j % 2, :], AF.Square), "ac")
    pr.blk.scalar(acb)
    def evb(e):
        for j in range(len(items)):
            e.wait_ge(pr.s(f"sst{j % 2}"), base_sst[j % 2] + 16 * (j // 2 + 1))
            if j >= 2:
                e.wait_ge(pr.s("mm"), mm0 + (j - 1))
            inc(pr, e.tensor_copy(ob[:, j % 2, :], ot[:, j % 2, :]), "ev")
            if j % NTG == NTG - 1:
                emit_red(e, j // NTG)
    def emit_red(e, rt):
        e.wait_ge(pr.s("mm"), mm0 + NTG * (rt + 1))
        inc(pr, e.reduce_sum(stats_sb[0:2, rt:rt + 1], bank[0:2, :],
                             mybir.AxisListType.X), "rsum")
        inc(pr, e.reduce_sum(stats_sb[0:2, 8 + rt:9 + rt], bank[64:66, :],
                             mybir.AxisListType.X), "rsum")
    pr.s("rsum")
    rsum0 = pr.cnt["rsum"]
    pr.blk.vector(evb)
    def tb(t):
        t.wait_ge(pr.s("cstc"), cst2)
        for j in range(len(items)):
            rt, tg = j // NTG, j % NTG
            t.wait_ge(pr.s("ac"), ac0 + j + 1)
            t.wait_ge(pr.s("ev"), ev0 + j + 1)
            if tg == 0 and rt > 0:
                t.wait_ge(pr.s("rsum"), rsum0 + 2 * rt)
            t.matmul(bank[0:2, :], hp_ones[:, :], ob[:, j % 2, :],
                     start=(tg == 0), stop=(tg == NTG - 1), skip_group_check=True)
            last = t.matmul(bank[64:66, :], hp_ones[:, :], osq[:, j % 2, :],
                            start=(tg == 0), stop=(tg == NTG - 1),
                            skip_group_check=True)
            last.then_inc(pr.s("mm"), 1)
            pr.cnt["mm"] += 1
    pr.blk.tensor(tb)
    rs_done = pr.cnt["rsum"]
    dma_seq(pr, "sync", [(io(dst_name), stats_sb[:, :], [("rsum", rs_done)])],
            "dout")


def emit_stats_final(pr, al, banks, io, *, stc, dst_name, tag, ev_wait):
    hp = al.t("st_hpones", [128, 2], BF16)
    stf = al.t(f"stf_{tag}", [128, 2, 8, 1], F32)
    stlo = al.t(f"stlo_{tag}", [128, 2, 8, 1], F32)
    stfb = al.t(f"stfb_{tag}", [128, 2, 2, 8, 1], BF16)
    sto = al.t(f"sto_{tag}", [2, 16], F32)
    dma_seq(pr, "sync", [(hp[:, :], io("c_hp_ones"), [])], "cstc")
    cst2 = pr.cnt["cstc"]
    pr.s("stv"); pr.s("stm")

    def evf(e):
        for (sn, v) in ev_wait:
            e.wait_ge(pr.s(sn), v)
        ser(pr, e, e.reduce_sum(stf[:, 0], stc[:, 0], mybir.AxisListType.X),
            name="chx_st")
        ser(pr, e, e.reduce_sum(stf[:, 1], stc[:, 1], mybir.AxisListType.X),
            name="chx_st")
        ser(pr, e, e.tensor_copy(stfb[:, 0], stf[:, :]), name="chx_st")
        ser(pr, e, e.tensor_tensor(stlo[:, :], stf[:, :], stfb[:, 0],
                                   ALU.subtract), name="chx_st")
        inc(pr, e.tensor_copy(stfb[:, 1], stlo[:, :]), "stv")
    pr.blk.vector(evf)
    stv = pr.cnt["stv"]

    def tbf(t):
        t.wait_ge(pr.s("stv"), stv)
        t.wait_ge(pr.s("cstc"), cst2)
        t.matmul(banks[7][0:2, 0:16], hp[:, :], stfb[:, 0],
                 start=True, stop=False, skip_group_check=True)
        inc(pr, t.matmul(banks[7][0:2, 0:16], hp[:, :], stfb[:, 1],
                         start=False, stop=True, skip_group_check=True), "stm")
    pr.blk.tensor(tbf)
    stm = pr.cnt["stm"]

    def evf2(e):
        e.wait_ge(pr.s("stm"), stm)
        inc(pr, e.tensor_copy(sto[:, :], banks[7][0:2, 0:16]), "stv")
    pr.blk.vector(evf2)
    dma_seq(pr, "sync", [(io(dst_name), sto[:, :],
                          [("stv", pr.cnt["stv"])])], "dout")


def stage2(pr, al, banks, io):
    cstv = load_consts(pr, al, io)
    al.t("xstage", [128, NDC * T], F32)
    y_sb = al.t("y_attn", [128, NDC * T], BF16)
    jls, slots = l0_jlists()
    emit_attn(pr, al, banks, io, q_name="q0_d", k_name="k_full", v_name="v_full",
              y_sb=y_sb, jls=jls, slots=slots, gcol_name="gcol0",
              gcol_per_head=False, gii_name="gii0", tri_name="tri0", tag="l0")
    attn_done_mm = pr.cnt["aevY"]
    pr.blk.tensor(lambda t: t.wait_ge(pr.s("aevY"), attn_done_mm))
    bo = al.t("bo_cols", [128, NDC], F32)
    dma_seq(pr, "sync", [(bo[:, :], io("bo_cols"), [])], "cstb")
    cst2 = pr.cnt["cstb"]
    pr.blk.vector(lambda e: e.wait_ge(pr.s("cstb"), cst2))
    dst = io("o0_d")
    for s in ("mm", "ev"):
        pr.s(s)
    ev_base = pr.cnt["ev"]
    ring, pre_ev = ring_out(pr, al, n_jobs=NDC * NTG, tile_shape=[TGW], dtype=F32,
                            dst_fn=lambda j: dst[j // NTG, :,
                                                 (j % NTG) * TGW:(j % NTG + 1) * TGW],
                            ev_base=ev_base, dsem="dout", ring_name="ring_f32")
    jc2 = [0]
    stc = al.t("stc_st0", [128, 2, 8, 4], F32)
    sqj = al.t("sqj_st0", [128, 2, TGW], F32)
    def ev_fn(e, rt, rw, tg, ps):
        j = jc2[0]; jc2[0] += 1
        for (sn, v) in pre_ev(j):
            e.wait_ge(pr.s(sn), v)
        r = ring[:, j % 4]
        e.tensor_scalar_add(r, ps[0:128, :], bo[:, rt:rt + 1])
        e.reduce_sum(stc[:, 0, rt, tg:tg + 1], r, mybir.AxisListType.X)
        e.tensor_tensor(sqj[:, j % 2, :], r, r, ALU.mult)
        return e.reduce_sum(stc[:, 1, rt, tg:tg + 1], sqj[:, j % 2, :],
                            mybir.AxisListType.X)
    emit_proj_fm(pr, al, banks[:4],
                 rhs_tile=lambda dc, tg: y_sb[:, dc * T + tg * TGW:
                                              dc * T + (tg + 1) * TGW],
                 wT=io("wo0T"), n_in=D, n_out=D, ev_fn=ev_fn, tag="wo0")
    emit_stats_final(pr, al, banks, io, stc=stc, dst_name="st0_d", tag="st0",
                     ev_wait=[("ev", pr.cnt["ev"])])



def emit_gn_apply(pr, al, banks, io, *, st_own, st_pair, g_cols, b_cols,
                  src_name, x_sb, nelem, tag, cst_wait):
    for s in ("mm", "ac", "ev", "gst"):
        pr.s(s)
    sto = al.t(f"gn_sto_{tag}", [2, 16], F32)
    stp = al.t(f"gn_stp_{tag}", [2, 16], F32)
    srow = al.t(f"gn_rows_{tag}", [2, 3 * 16], F32)
    cols = al.t(f"gn_cols_{tag}", [128, 4 * NDC], F32)
    gcols = al.t(f"gn_g_{tag}", [128, NDC], F32)
    bcols = al.t(f"gn_b_{tag}", [128, NDC], F32)
    hpT = al.t("c_hp_onesT", [2, 128], F32)
    ot = al.t(f"gn_ot_{tag}", [128, 2, TGW], F32)
    gt = al.t(f"gn_gt_{tag}", [128, 2, TGW], F32)
    dma_seq(pr, "sync", [
        (sto[:, :], io(st_own), []),
        (stp[:, :], io(st_pair), []),
        (gcols[:, :], io(g_cols), []),
        (bcols[:, :], io(b_cols), []),
        (hpT[:, :], io("c_hp_onesT"), []),
    ], "gst")
    gst0 = pr.cnt["gst"]
    SR = lambda i: srow[0:2, i * 16:(i + 1) * 16]
    def evst(e):
        e.wait_ge(pr.s("gst"), gst0)
        ser(pr, e, e.tensor_tensor(srow[0:2, 0:16], sto[:, :], stp[:, :],
                                   ALU.add))
        ser(pr, e, e.tensor_scalar_mul(SR(1), SR(0), 1.0 / nelem))
        ser(pr, e, e.tensor_tensor(SR(2)[:, 0:8], SR(1)[:, 0:8],
                                   SR(1)[:, 0:8], ALU.mult))
        ser(pr, e, e.tensor_tensor(SR(1)[:, 8:16], SR(1)[:, 8:16],
                                   SR(2)[:, 0:8], ALU.subtract))
        inc(pr, e.tensor_scalar_add(SR(1)[:, 8:16], SR(1)[:, 8:16], 1e-5), "ev")
    pr.blk.vector(evst)
    evv = pr.cnt["ev"]
    def acst(a):
        a.wait_ge(pr.s("ev"), evv)
        ser(pr, a, a.activation(SR(2)[:, 8:16], SR(1)[:, 8:16], AF.Ln),
            name="chxa")
        inc(pr, a.activation(SR(2)[:, 8:16], SR(2)[:, 8:16], AF.Exp, scale=-0.5),
            "ac")
    pr.blk.scalar(acst)
    acv = pr.cnt["ac"]
    def evst2(e):
        e.wait_ge(pr.s("ac"), acv)
        e.tensor_copy(SR(0)[:, 0:8], SR(2)[:, 8:16])
        inc(pr, e.tensor_tensor(SR(0)[:, 8:16], SR(1)[:, 0:8], SR(2)[:, 8:16],
                                ALU.mult), "ev")
    pr.blk.vector(evst2)
    evv2 = pr.cnt["ev"]
    def tbc(t):
        t.wait_ge(pr.s("ev"), evv2)
        inc(pr, t.matmul(banks[6][0:128, 0:16], hpT[:, :], SR(0),
                         start=True, stop=True), "mm")
    pr.blk.tensor(tbc)
    mmv = pr.cnt["mm"]
    def evst3(e):
        e.wait_ge(pr.s("mm"), mmv)
        ser(pr, e, e.tensor_copy(cols[:, 0:16], banks[6][0:128, 0:16]))
        e.tensor_tensor(cols[:, 16:24], cols[:, 0:8], gcols[:, :], ALU.mult)
        ser(pr, e, e.tensor_tensor(cols[:, 24:32], cols[:, 8:16], gcols[:, :],
                                   ALU.mult))
        inc(pr, e.tensor_tensor(cols[:, 24:32], bcols[:, :], cols[:, 24:32],
                                ALU.subtract), "ev")
    pr.blk.vector(evst3)
    cols_done = pr.cnt["ev"]
    scol = cols[:, 16:24]
    bcol = cols[:, 24:32]

    src = io(src_name)
    pr.s("gno0"); pr.s("gno1")
    base_o = [pr.cnt["gno0"], pr.cnt["gno1"]]
    ev0 = pr.cnt["ev"]
    items = []
    for j in range(NDC * NTG):
        rt, tg = j // NTG, j % NTG
        w = [("ev", ev0 + 2 * (j - 2) + 1)] if j >= 2 else []
        items.append((ot[:, j % 2, :], src[rt, :, tg * TGW:(tg + 1) * TGW], w,
                      f"gno{j % 2}"))
    dma_seq(pr, "sync", items, "gno_unused")
    def evg(e):
        e.wait_ge(pr.s("ev"), cols_done)
        for j in range(NDC * NTG):
            rt, tg = j // NTG, j % NTG
            e.wait_ge(pr.s(f"gno{j % 2}"), base_o[j % 2] + 16 * (j // 2 + 1))
            inc(pr, e.tensor_scalar(gt[:, 0, :], ot[:, j % 2, :],
                                    scol[:, rt:rt + 1], bcol[:, rt:rt + 1],
                                    ALU.mult, ALU.add), "ev")
            xs = x_sb[:, rt * T + tg * TGW: rt * T + (tg + 1) * TGW]
            inc(pr, e.tensor_tensor(xs, xs, gt[:, 0, :], ALU.add), "ev")
    pr.blk.vector(evg)
    return pr.cnt["ev"]


def stage3(pr, al, banks, io):
    cstv = load_consts(pr, al, io)
    xstage = al.t("xstage", [128, NDC * T], F32)
    res_done = emit_gn_apply(pr, al, banks, io, st_own="st0_d", st_pair="st0_p",
                             g_cols="g0_cols", b_cols="b0_cols", src_name="o0_d",
                             x_sb=xstage, nelem=64 * S_FULL, tag="gn0",
                             cst_wait=cstv)
    pr.blk.scalar(lambda a: a.wait_ge(pr.s("ev"), res_done))
    pr.blk.vector(lambda e: e.wait_ge(pr.s("ev"), res_done))
    xln = al.t("xw", [128, NDC * T], BF16)
    emit_ln(pr, al, banks, x_sb=xstage,
            out_tile=lambda dc, tg: xln[:, dc * T + tg * TGW: dc * T + (tg + 1) * TGW],
            tag="ln", cst_wait=cstv)



DBG_FF = False

NEXP = 10
NK = 11

def stage4(pr, al, banks, io):
    cstv = load_consts(pr, al, io)
    for s in ("mg", "mu", "md", "ag", "eu", "ed", "mrt", "ert", "mwb", "ewb"):
        pr.s(s)
    xacc = al.t("xstage", [128, NDC * T], F32)
    xln = al.t("xw", [128, NDC * T], BF16)
    wgu = al.t("wgu", [128, NDC, 2816], BF16)
    wgd = al.t("wgd", [128, NK, 1024], BF16)
    h = al.t("moe_h", [128, NK, TGW], BF16)
    gbf = al.t("moe_g", [128, 2, TGW], BF16)
    tmpu = al.t("moe_tmpu", [128, TGW], BF16)
    wbc = al.t("moe_wbc", [128, T], F32)
    gws = al.t("moe_gws", [128, NDC, 9], BF16)
    rout = al.t("moe_rout", [128, 16, 9], F32)
    w9 = al.t("moe_w9", [9, T], F32)
    sel = al.t("c_sel", [16, NEXP, 128], F32)
    ident = al.t("c_ident", [128, 128], F32)

    dma_seq(pr, "sync", [(sel[:, :, :], io("c_sel"), []),
                         (ident[:, :], io("c_ident"), [])], "xin")
    xin_done = pr.cnt["xin"]
    dma_seq(pr, "gpsimd", [(gws[:, :, :], io("gwsT").rearrange(
        "(dc p) o -> p dc o", p=128), [])], "mgw")
    gw_done = pr.cnt["mgw"]

    def trt(t):
        t.wait_ge(pr.s("xin"), xin_done)
        t.wait_ge(pr.s("mgw"), gw_done)
        for tt in range(16):
            bank = banks[6 + tt % 2]
            if tt >= 2:
                t.wait_ge(pr.s("ert"), tt - 1)
            last = None
            for dc in range(NDC):
                last = t.matmul(bank[0:128, 0:9],
                                xln[:, dc * T + tt * 128: dc * T + (tt + 1) * 128],
                                gws[:, dc, :], start=(dc == 0), stop=(dc == NDC - 1))
            inc(pr, last, "mrt")
    pr.blk.tensor(trt)
    def evrt(e):
        for tt in range(16):
            bank = banks[6 + tt % 2]
            e.wait_ge(pr.s("mrt"), tt + 1)
            inc(pr, e.tensor_copy(rout[:, tt, :], bank[0:128, 0:9]), "ert")
    pr.blk.vector(evrt)
    rt_done = pr.cnt["ert"]

    rows = al.t("moe_rows", [128, 4], F32)
    eq8 = al.t("moe_eq8", [128, 16], F32)
    for tt in range(16):
        sc = rout[:, tt, 0:8]
        def ev1(e, tt=tt, sc=sc):
            e.wait_ge(pr.s("ert"), rt_done)
            if tt > 0:
                e.wait_ge(pr.s("ag"), pr._sm_ac)
            inc(pr, e.reduce_max(rows[:, 0:1], sc, mybir.AxisListType.X,
                                 negate=True), "eu")
        pr.blk.vector(ev1)
        ev_a = pr.cnt["eu"]
        def ac1(a, tt=tt, sc=sc):
            a.wait_ge(pr.s("eu"), ev_a)
            a.activation(sc, sc, AF.Exp, bias=rows[:, 0:1])
            inc(pr, a.activation(rout[:, tt, 8:9], rout[:, tt, 8:9],
                                 AF.Sigmoid), "ag")
        pr.blk.scalar(ac1)
        pr._sm_ac = pr.cnt["ag"]
        def ev2(e, tt=tt, sc=sc):
            e.wait_ge(pr.s("ag"), pr._sm_ac)
            ser(pr, e, e.reduce_sum(rows[:, 0:1], sc, mybir.AxisListType.X))
            ser(pr, e, e.reciprocal(rows[:, 1:2], rows[:, 0:1]))
            ser(pr, e, e.tensor_scalar_mul(sc, sc, rows[:, 1:2]))
            ser(pr, e, e.reduce_max(rows[:, 2:3], sc,
                                    mybir.AxisListType.X))
            ser(pr, e, e.tensor_scalar(eq8[:, 0:8], sc, rows[:, 2:3], None,
                                       ALU.is_equal))
            ser(pr, e, e.scalar_tensor_tensor(eq8[:, 8:16], eq8[:, 0:8], -1e30,
                                              sc, ALU.mult, ALU.add))
            ser(pr, e, e.reduce_max(rows[:, 3:4], eq8[:, 8:16],
                                    mybir.AxisListType.X))
            ser(pr, e, e.tensor_scalar(eq8[:, 0:8], sc, rows[:, 3:4], None,
                                       ALU.is_ge))
            inc(pr, e.tensor_tensor(sc, sc, eq8[:, 0:8], ALU.mult), "eu")
        pr.blk.vector(ev2)
    sm_done = pr.cnt["eu"]

    def ttr(t):
        t.wait_ge(pr.s("eu"), sm_done)
        for tt in range(16):
            bank = banks[6 + tt % 2]
            if tt >= 2:
                t.wait_ge(pr.s("ewb"), w9b + tt - 1)
            inc(pr, t.transpose(bank[0:9, 0:128], rout[:, tt, :], ident[:, :]),
                "mwb")
    w9b = pr.cnt["ewb"]
    mwb0 = pr.cnt["mwb"]
    pr.blk.tensor(ttr)
    def evtr(e):
        for tt in range(16):
            bank = banks[6 + tt % 2]
            e.wait_ge(pr.s("mwb"), mwb0 + tt + 1)
            inc(pr, e.tensor_copy(w9[0:9, tt * 128:(tt + 1) * 128],
                                  bank[0:9, 0:128]), "ewb")
    pr.blk.vector(evtr)
    w9_done = pr.cnt["ewb"]

    guT = io("guT")
    gdT = io("gdT")
    ew = [0, 1, 2, 3, 4, 5, 6, 7, 8, 8]
    njg = 2 * NK * NTG
    mg0 = pr.cnt["mg"]; mu0 = pr.cnt["mu"]; md0 = pr.cnt["md"]
    ag0 = pr.cnt["ag"]; eu0 = pr.cnt["eu"]; ed0 = pr.cnt["ed"]
    mwb1 = pr.cnt["mwb"]; ewb1 = pr.cnt["ewb"]

    pr.s("mgu"); pr.s("mgd")
    mgu0 = pr.cnt["mgu"]; mgd0 = pr.cnt["mgd"]
    itemsA, itemsB = [], []
    for e in range(NEXP):
        wu = [("mu", mu0 + NTG * NK * e)] if e > 0 else []
        wd = [("md", md0 + 32 * e)] if e > 0 else []
        gre = guT[e].rearrange("(dc p) o -> p dc o", p=128)
        dre = gdT[e].rearrange("(kc p) o -> p kc o", p=128)
        itemsA.append((wgu[:, :, 0:1408], gre[:, :, 0:1408], list(wu), "mgu"))
        itemsB.append((wgu[:, :, 1408:2816], gre[:, :, 1408:2816],
                       list(wu), "mgu"))
        itemsA.append((wgd[:, :, 0:512], dre[:, :, 0:512], list(wd), "mgd"))
        itemsB.append((wgd[:, :, 512:1024], dre[:, :, 512:1024],
                       list(wd), "mgd"))
    if MOE_OLD:
        inter = []
        for i in range(len(itemsA)):
            inter.append(itemsA[i]); inter.append(itemsB[i])
        dma_seq(pr, "gpsimd", inter, "mgu_unusedA")
    else:
        dma_seq(pr, "gpsimd", itemsA, "mgu_unusedA")
        dma_seq(pr, "sync", itemsB, "mgu_unusedB")

    def tmoe(t):
        for e in range(NEXP):
            if e == 0:
                t.wait_ge(pr.s("ewb"), w9_done)
            for tg in range(NTG):
                gj = e * NTG + tg
                if gj >= 2:
                    t.wait_ge(pr.s("ewb"), ewb1 + gj - 1)
                inc(pr, t.matmul(banks[6 + gj % 2][0:128, 0:TGW],
                                 sel[0:9, ew[e], :],
                                 w9[0:9, tg * TGW:(tg + 1) * TGW],
                                 start=True, stop=True), "mwb")
            t.wait_ge(pr.s("mgu"), mgu0 + 32 * (e + 1))
            t.wait_ge(pr.s("mgd"), mgd0 + 32 * (e + 1))
            for tg in range(NTG):
                for k in range(NK):
                    for half in range(2):
                        jidx = ((e * NTG + tg) * NK + k) * 2 + half
                        bank = banks[jidx % 4]
                        pj = jidx - 4
                        if pj >= 0:
                            if pj % 2 == 0:
                                t.wait_ge(pr.s("ag"), ag0 + pj // 2 + 1)
                            else:
                                t.wait_ge(pr.s("eu"), eu0 + pj // 2 + 1)
                        last = None
                        for dc in range(NDC):
                            last = t.matmul(
                                bank[0:128, 0:TGW],
                                wgu[:, dc, (2 * k + half) * 128:
                                    (2 * k + half + 1) * 128],
                                xln[:, dc * T + tg * TGW: dc * T + (tg + 1) * TGW],
                                start=(dc == 0), stop=(dc == NDC - 1))
                        inc(pr, last, "mg" if half == 0 else "mu")
                for rt in range(NDC):
                    didx = (e * NTG + tg) * NDC + rt
                    bank = banks[4 + didx % 2]
                    if didx >= 2:
                        t.wait_ge(pr.s("ed"), ed0 + didx - 1)
                    t.wait_ge(pr.s("eu"), eu0 + (e * NTG + tg + 1) * NK)
                    last = None
                    for kc in range(NK):
                        last = t.matmul(bank[0:128, 0:TGW],
                                        wgd[:, kc, rt * 128:(rt + 1) * 128],
                                        h[:, kc, :],
                                        start=(kc == 0), stop=(kc == NK - 1))
                    inc(pr, last, "md")
    pr.blk.tensor(tmoe)

    def amoe(a):
        for e in range(NEXP):
            for tg in range(NTG):
                for k in range(NK):
                    gidx = (e * NTG + tg) * NK + k
                    jidx = gidx * 2
                    a.wait_ge(pr.s("mg"), mg0 + gidx + 1)
                    if gidx >= 2:
                        a.wait_ge(pr.s("eu"), eu0 + gidx - 1)
                    inc(pr, a.activation(gbf[:, k % 2, :], banks[jidx % 4][0:128, 0:TGW],
                                         AF.Silu), "ag")
    pr.blk.scalar(amoe)

    def vmoe(e):
        e.wait_ge(pr.s("xin"), xin_done)
        for ei in range(NEXP):
            for tg in range(NTG):
                gj = ei * NTG + tg
                e.wait_ge(pr.s("mwb"), mwb1 + gj + 1)
                inc(pr, e.tensor_copy(wbc[:, tg * TGW:(tg + 1) * TGW],
                                      banks[6 + gj % 2][0:128, 0:TGW]), "ewb")
            for tg in range(NTG):
                for k in range(NK):
                    gidx = (ei * NTG + tg) * NK + k
                    jidx = gidx * 2 + 1
                    if k == 0 and (ei, tg) != (0, 0):
                        e.wait_ge(pr.s("md"), md0 + (ei * NTG + tg) * NDC)
                    e.wait_ge(pr.s("mu"), mu0 + gidx + 1)
                    e.wait_ge(pr.s("ewb"), ewb1 + ei * NTG + NTG)
                    e.tensor_tensor(tmpu[:, :], banks[jidx % 4][0:128, 0:TGW],
                                    wbc[:, tg * TGW:(tg + 1) * TGW], ALU.mult)
                    e.wait_ge(pr.s("ag"), ag0 + gidx + 1)
                    inc(pr, e.tensor_tensor(h[:, k, :], tmpu[:, :], gbf[:, k % 2, :],
                                            ALU.mult), "eu")
                for rt in range(NDC):
                    didx = (ei * NTG + tg) * NDC + rt
                    e.wait_ge(pr.s("md"), md0 + didx + 1)
                    xs = xacc[:, rt * T + tg * TGW: rt * T + (tg + 1) * TGW]
                    inc(pr, e.tensor_tensor(xs, xs, banks[4 + didx % 2][0:128, 0:TGW],
                                            ALU.add), "ed")
    pr.blk.vector(vmoe)




def emit_rope_proj(pr, al, banks, io, *, w_name, out_name, tabA, tabB, sub_b,
                   tag, ln_done):
    xln = al.t("xw", [128, NDC * T], BF16)
    qes = al.t("rp_qes", [128, 4, T], BF16)
    qob = al.t("rp_qob", [128, 2, TGW], BF16)
    rs = al.t("rp_rs", [128, 4, TGW], BF16)
    ring = al.t("rp_ring", [128, 4, TGW], BF16)
    dst = io(out_name)
    for s in ("ro", "rg", "rv"):
        pr.s(s)
    rg0 = pr.cnt["rg"]; rv0 = pr.cnt["rv"]
    pr.s("rdo0"); pr.s("rdo1")
    base_rd = [pr.cnt["rdo0"], pr.cnt["rdo1"]]

    def combine(e, jo):
        e.wait_ge(pr.s("rg"), rg0 + jo + 1)
        if jo >= 2:
            e.wait_ge(pr.s(f"rdo{jo % 2}"), base_rd[jo % 2] + 128 * (jo // 2))
        elif base_rd[jo % 2] > 0:
            e.wait_ge(pr.s(f"rdo{jo % 2}"), base_rd[jo % 2])
        sl = (2 * jo) % 4
        e.tensor_tensor(ring[:, sl, :], rs[:, 0, :], rs[:, 1, :], ALU.add)
        inc(pr, e.tensor_tensor(ring[:, sl + 1, :], rs[:, 2, :], rs[:, 3, :],
                                ALU.subtract), "rv")

    jc = [0]
    def ev_fn(e, rt, rw, tg, ps):
        j = jc[0]; jc[0] += 1
        if j == 0 and rg0 > 0:
            e.wait_ge(pr.s("rg"), rg0)
        if j < 16:
            return e.tensor_copy(qes[:, rt, tg * TGW:(tg + 1) * TGW],
                                 ps[0:128, :])
        jo = j - 16
        if jo >= 1:
            combine(e, jo - 1)
        if jo >= 2:
            e.wait_ge(pr.s("rg"), rg0 + (jo - 1))
        return e.tensor_copy(qob[:, jo % 2, :], ps[0:128, :])

    def dispatch(dc, tg):
        return xln[:, dc * T + tg * TGW: dc * T + (tg + 1) * TGW]

    pr.blk.tensor(lambda t: t.wait_ge(pr.s("ev"), ln_done))
    base_mm, base_ev = emit_proj_fm(pr, al, banks[:4], rhs_tile=dispatch,
                                    wT=io(w_name), n_in=D, n_out=D,
                                    ev_fn=ev_fn, tag=tag)
    def vfin(e):
        combine(e, 15)
    pr.blk.vector(vfin)
    def gpb(g):
        for jo in range(16):
            rt, tg = jo // NTG, jo % NTG
            g.wait_ge(pr.s("ev"), base_ev + 16 + jo + 1)
            if jo >= 1:
                g.wait_ge(pr.s("rv"), rv0 + jo)
            A = tabA[:, tg * TGW:(tg + 1) * TGW]
            B = tabB[:, tg * TGW:(tg + 1) * TGW]
            qe = qes[:, rt, tg * TGW:(tg + 1) * TGW]
            qo = qob[:, jo % 2, :]
            g.tensor_tensor(rs[:, 0, :], qo, A, ALU.mult)
            g.tensor_tensor(rs[:, 1, :], qe, B, ALU.mult)
            g.tensor_tensor(rs[:, 2, :], qe, A, ALU.mult)
            inc(pr, g.tensor_tensor(rs[:, 3, :], qo, B, ALU.mult), "rg")
    pr.blk.gpsimd(gpb)
    items = []
    for jo in range(16):
        rt, tg = jo // NTG, jo % NTG
        sl = (2 * jo) % 4
        for g4 in range(4):
            h = 4 * rt + g4
            w = [("rv", rv0 + jo + 1)]
            items.append((dst[h // 2, (h % 2) * 64 + 32:(h % 2) * 64 + 64,
                              tg * TGW:(tg + 1) * TGW],
                          ring[32 * g4:32 * g4 + 32, sl, :], w, f"rdo{jo % 2}"))
            items.append((dst[h // 2, (h % 2) * 64:(h % 2) * 64 + 32,
                              tg * TGW:(tg + 1) * TGW],
                          ring[32 * g4:32 * g4 + 32, sl + 1, :], w,
                          f"rdo{jo % 2}"))
    dma_seq(pr, "sync", items, "rdo_unused")


def stage5(pr, al, banks, io):
    cstv = load_consts(pr, al, io)
    xstage = al.t("xstage", [128, NDC * T], F32)
    xln = al.t("xw", [128, NDC * T], BF16)
    emit_ln(pr, al, banks, x_sb=xstage,
            out_tile=lambda dc, tg: xln[:, dc * T + tg * TGW: dc * T + (tg + 1) * TGW],
            tag="ln", cst_wait=cstv)
    ln_done = pr.cnt["ev"]

    def xt(dc, tg):
        return xln[:, dc * T + tg * TGW: dc * T + (tg + 1) * TGW]

    pr.blk.tensor(lambda t: t.wait_ge(pr.s("ev"), ln_done))

    wvre = io("wv1T").rearrange("(dc p) o -> p dc o", p=128)
    wslab = al.t(f"wsl_{NDC}x{TGW}", [128, 2, NDC, TGW], BF16)
    pr.s("wdma0"); pr.s("wdma1")
    base_w = [pr.cnt["wdma0"], pr.cnt["wdma1"]]
    base_mm = pr.cnt["mm"]
    items = [(wslab[:, s % 2, :, :], wvre[:, :, s * TGW:(s + 1) * TGW],
              [("mm", base_mm)]) for s in range(2)]
    dma_seq(pr, "gpsimd", items, "wdma", nsem=2)
    vdst = io("v1_d")
    ev_base = pr.cnt["ev"]
    ring, pre_ev = ring_out(pr, al, n_jobs=32, tile_shape=[TGW], dtype=BF16,
                            dst_fn=lambda j: vdst[j % 16, :,
                                                  (j // 16) * TGW:(j // 16 + 1) * TGW],
                            ev_base=ev_base, dsem="dout")
    jobs = []
    for s in range(2):
        for tt in range(16):
            j = s * 16 + tt
            def mk_mm(s=s, tt=tt):
                def mmf(t, ps):
                    t.wait_ge(pr.s(f"wdma{s % 2}"), base_w[s % 2] + 16)
                    last = None
                    for dc in range(NDC):
                        last = t.matmul(
                            ps[0:128, 0:TGW],
                            xln[:, dc * T + tt * 128: dc * T + (tt + 1) * 128],
                            wslab[:, s % 2, dc, :],
                            start=(dc == 0), stop=(dc == NDC - 1))
                    return last
                return mmf
            def mk_ev(j=j):
                def evf(e, ps):
                    for (sn, v) in pre_ev(j):
                        e.wait_ge(pr.s(sn), v)
                    return e.tensor_copy(ring[:, j % 4], ps[0:128, :])
                return evf
            jobs.append({'mm': mk_mm(), 'ev': mk_ev()})
    pipe(pr, jobs, banks[:4])

    kv_snap = {"v": [pr.cnt.get(f"dout{i}", 0) for i in range(4)]}

    tabs = al.t("rp_tabs", [128, 4, T], BF16)
    dma_seq(pr, "sync", [(tabs[:, :, :], io("rt1"), [])], "cstb")
    tab_done = pr.cnt["cstb"]
    pr.blk.gpsimd(lambda g: g.wait_ge(pr.s("cstb"), tab_done))
    emit_rope_proj(pr, al, banks, io, w_name="wk1T", out_name="k1_d",
                   tabA=tabs[:, 2], tabB=tabs[:, 3], sub_b=True, tag="k1",
                   ln_done=ln_done)
    kv_snap["k"] = [pr.cnt.get(f"rdo{i}", 0) for i in range(2)]

    dstg = io("gated_d")
    ev_base_g = pr.cnt["ac"]
    ringg, pre_evg = ring_out(pr, al, n_jobs=NDC * NTG, tile_shape=[TGW],
                              dtype=BF16,
                              dst_fn=lambda j: dstg[j // NTG, :,
                                                    (j % NTG) * TGW:(j % NTG + 1) * TGW],
                              ev_base=ev_base_g, dsem="dout", ev_sem="ac")
    jcg = [0]
    def evg_fn(a, rt, rw, tg, ps):
        j = jcg[0]; jcg[0] += 1
        for (sn, v) in pre_evg(j):
            a.wait_ge(pr.s(sn), v)
        return a.activation(ringg[:, j % 4], ps[0:128, :], AF.Silu)
    emit_proj_fm(pr, al, banks[:4], rhs_tile=xt, wT=io("wg1T"), n_in=D, n_out=D,
                 ev_fn=evg_fn, ev_eng="scalar", tag="g1", ev_sem="ac")

    pr.s("ccs")
    def ccb(g):
        for i in range(2):
            if kv_snap["k"][i] > 0:
                g.wait_ge(pr.s(f"rdo{i}"), kv_snap["k"][i])
        inc(pr, g.collective_compute(
            "AllGather", mybir.AluOpType.bypass, replica_groups=CC_GROUPS,
            ins=[io("k1_d__2d").opt()], outs=[io("k1cc").opt()]), "ccs")
        for i in range(4):
            if kv_snap["v"][i] > 0:
                g.wait_ge(pr.s(f"dout{i}"), kv_snap["v"][i])
        inc(pr, g.collective_compute(
            "AllGather", mybir.AluOpType.bypass, replica_groups=CC_GROUPS,
            ins=[io("v1_d__2d").opt()], outs=[io("v1cc").opt()]), "ccs")
    pr.blk.gpsimd(ccb)

    emit_rope_proj(pr, al, banks, io, w_name="wq1T", out_name="q1_d",
                   tabA=tabs[:, 0], tabB=tabs[:, 1], sub_b=True, tag="q1",
                   ln_done=ln_done)



def stage6(pr, al, banks, io):
    cstv = load_consts(pr, al, io)
    y_sb = al.t("y_attn1", [128, NDC * T], BF16)
    al.t("xstage", [128, NDC * T], F32)
    stc = al.t("stc_st1", [128, 2, 8, 4], F32)
    jls, slots = l1_jlists()
    emit_attn(pr, al, banks, io, q_name="q1_d", k_name="k1_full",
              v_name="v1_full", y_sb=y_sb, jls=jls, slots=slots,
              gcol_name="gcol1", gcol_per_head=True, gii_name="gii1",
              tri_name="tri1", tag="l1", stats_stc=stc)
    y_done = pr.cnt["aevY"]
    emit_stats_final(pr, al, banks, io, stc=stc, dst_name="st1_d", tag="st1",
                     ev_wait=[("aevY", y_done)])


def emit_gn_cols(pr, al, banks, io, *, st_own, st_pair, g_cols, b_cols, nelem,
                 tag):
    for s in ("mm", "ac", "ev", "gst"):
        pr.s(s)
    sto = al.t(f"gn_sto_{tag}", [2, 16], F32)
    stp = al.t(f"gn_stp_{tag}", [2, 16], F32)
    srow = al.t(f"gn_rows_{tag}", [2, 3 * 16], F32)
    cols = al.t(f"gn_cols_{tag}", [128, 4 * NDC], F32)
    gcols = al.t(f"gn_g_{tag}", [128, NDC], F32)
    bcols = al.t(f"gn_b_{tag}", [128, NDC], F32)
    hpT = al.t("c_hp_onesT", [2, 128], F32)
    dma_seq(pr, "sync", [
        (sto[:, :], io(st_own), []),
        (stp[:, :], io(st_pair), []),
        (gcols[:, :], io(g_cols), []),
        (bcols[:, :], io(b_cols), []),
        (hpT[:, :], io("c_hp_onesT"), []),
    ], "gst")
    gst0 = pr.cnt["gst"]
    SR = lambda i: srow[0:2, i * 16:(i + 1) * 16]
    def evst(e):
        e.wait_ge(pr.s("gst"), gst0)
        ser(pr, e, e.tensor_tensor(srow[0:2, 0:16], sto[:, :], stp[:, :],
                                   ALU.add))
        ser(pr, e, e.tensor_scalar_mul(SR(1), SR(0), 1.0 / nelem))
        ser(pr, e, e.tensor_tensor(SR(2)[:, 0:8], SR(1)[:, 0:8],
                                   SR(1)[:, 0:8], ALU.mult))
        ser(pr, e, e.tensor_tensor(SR(1)[:, 8:16], SR(1)[:, 8:16],
                                   SR(2)[:, 0:8], ALU.subtract))
        inc(pr, e.tensor_scalar_add(SR(1)[:, 8:16], SR(1)[:, 8:16], 1e-5), "ev")
    pr.blk.vector(evst)
    evv = pr.cnt["ev"]
    def acst(a):
        a.wait_ge(pr.s("ev"), evv)
        ser(pr, a, a.activation(SR(2)[:, 8:16], SR(1)[:, 8:16], AF.Ln),
            name="chxa")
        inc(pr, a.activation(SR(2)[:, 8:16], SR(2)[:, 8:16], AF.Exp, scale=-0.5),
            "ac")
    pr.blk.scalar(acst)
    acv = pr.cnt["ac"]
    def evst2(e):
        e.wait_ge(pr.s("ac"), acv)
        e.tensor_copy(SR(0)[:, 0:8], SR(2)[:, 8:16])
        inc(pr, e.tensor_tensor(SR(0)[:, 8:16], SR(1)[:, 0:8], SR(2)[:, 8:16],
                                ALU.mult), "ev")
    pr.blk.vector(evst2)
    evv2 = pr.cnt["ev"]
    def tbc(t):
        t.wait_ge(pr.s("ev"), evv2)
        inc(pr, t.matmul(banks[6][0:128, 0:16], hpT[:, :], SR(0),
                         start=True, stop=True), "mm")
    pr.blk.tensor(tbc)
    mmv = pr.cnt["mm"]
    def evst3(e):
        e.wait_ge(pr.s("mm"), mmv)
        ser(pr, e, e.tensor_copy(cols[:, 0:16], banks[6][0:128, 0:16]))
        e.tensor_tensor(cols[:, 16:24], cols[:, 0:8], gcols[:, :], ALU.mult)
        ser(pr, e, e.tensor_tensor(cols[:, 24:32], cols[:, 8:16], gcols[:, :],
                                   ALU.mult))
        inc(pr, e.tensor_tensor(cols[:, 24:32], bcols[:, :], cols[:, 24:32],
                                ALU.subtract), "ev")
    pr.blk.vector(evst3)
    return cols[:, 16:24], cols[:, 24:32], pr.cnt["ev"]


def stage7(pr, al, banks, io):
    cstv = load_consts(pr, al, io)
    xstage = al.t("xstage", [128, NDC * T], F32)
    ybf = al.t("y_attn1", [128, NDC * T], BF16)
    gv = al.t("xw", [128, NDC * T], BF16)
    gat = al.t("gat7", [128, NDC * T], BF16)
    dma_seq(pr, "sync", [(gat[:, dc * T:(dc + 1) * T], io("gated_d")[dc], [])
                         for dc in range(NDC)], "xin")
    xin_done = pr.cnt["xin"]
    scol, bcol, cols_done = emit_gn_cols(pr, al, banks, io, st_own="st1_d",
                                         st_pair="st1_p", g_cols="g1_cols",
                                         b_cols="b1_cols", nelem=64 * S_FULL,
                                         tag="gn1")
    gtile = al.t("gn_gt_gn1", [128, 2, TGW], F32)
    def evg(e):
        e.wait_ge(pr.s("ev"), cols_done)
        e.wait_ge(pr.s("xin"), xin_done)
        for j in range(NDC * NTG):
            rt, tg = j // NTG, j % NTG
            ysl = ybf[:, rt * T + tg * TGW: rt * T + (tg + 1) * TGW]
            inc(pr, e.tensor_scalar(gtile[:, 0, :], ysl,
                                    scol[:, rt:rt + 1], bcol[:, rt:rt + 1],
                                    ALU.mult, ALU.add), "ev")
            sl = gv[:, rt * T + tg * TGW: rt * T + (tg + 1) * TGW]
            inc(pr, e.tensor_tensor(sl, gtile[:, 0, :],
                                    gat[:, rt * T + tg * TGW: rt * T + (tg + 1) * TGW],
                                    ALU.mult), "ev")
    pr.blk.vector(evg)
    gv_done = pr.cnt["ev"]
    pr.blk.tensor(lambda t: t.wait_ge(pr.s("ev"), gv_done))
    jc = [0]
    def ev_fn(e, rt, rw, tg, ps):
        xs = xstage[:, rt * T + tg * TGW: rt * T + (tg + 1) * TGW]
        return e.tensor_tensor(xs, xs, ps[0:128, :], ALU.add)
    emit_proj_fm(pr, al, banks[:4],
                 rhs_tile=lambda dc, tg: gv[:, dc * T + tg * TGW:
                                            dc * T + (tg + 1) * TGW],
                 wT=io("wp1T"), n_in=D, n_out=D, ev_fn=ev_fn, tag="wp1")
    x3_done = pr.cnt["ev"]
    x3 = io("x3_d")
    dma_seq(pr, "sync", [(x3[dc], xstage[:, dc * T:(dc + 1) * T],
                          [("ev", x3_done)]) for dc in range(NDC)], "dout")
    al.overlay("lnsq_ln7", [128, 2, T], BF16, "gat7", 0)
    al.overlay("lnxb_ln7", [128, 2, T], BF16, "gat7", 8192)
    al.overlay("lnbc_ln7", [128, 2 * T], F32, "gat7", 16384)
    pr.blk.scalar(lambda a: a.wait_ge(pr.s("ev"), x3_done))
    pr.blk.vector(lambda e: e.wait_ge(pr.s("ev"), x3_done))
    zr = al.t("zr32", [128, 4, TGW], F32)
    for i in range(4):
        pr.s(f"zo{i}"); pr.s(f"zc{i}")
    zo_base = [pr.cnt[f"zo{i}"] for i in range(4)]
    zc_base = [pr.cnt[f"zc{i}"] for i in range(4)]
    def out_pre(e, j):
        if j >= 4:
            e.wait_ge(pr.s(f"zo{j % 4}"), zo_base[j % 4] + 16 * (j // 4))
            e.wait_ge(pr.s(f"zc{j % 4}"), zc_base[j % 4] + 16 * (j // 4))
    def gp_post(g, j, evb):
        def cast(jt):
            dc, tg = jt // NTG, jt % NTG
            g.wait_ge(pr.s("ev"), evb + jt + 1)
            g.dma_start(out=io("zbf_d")[dc, :, tg * TGW:(tg + 1) * TGW],
                        in_=zr[:, jt % 4, :]).then_inc(pr.s(f"zc{jt % 4}"), 16)
            pr.cnt[f"zc{jt % 4}"] += 16
        if 1 <= j < NDC * NTG:
            cast(j - 1)
        elif j == NDC * NTG:
            cast(j - 1)
    zlcol = al.t("zlcol", [128, NDC], F32)
    pr.s("zlc")

    def ev_post(e, j):
        if j % NTG == NTG - 1:
            dc = j // NTG
            e.wait_ge(pr.s("ev"), pr.cnt["ev"])
            inc(pr, e.tensor_copy(zlcol[:, dc:dc + 1],
                                  zr[:, j % 4, TGW - 1:TGW]), "zlc")
    if not ZL_WRITES:
        ev_post = None
    zlc0 = pr.cnt["zlc"]
    evb_ln = emit_ln(pr, al, banks, x_sb=xstage,
                     out_tile=lambda dc, tg: zr[:, (dc * NTG + tg) % 4, :],
                     tag="ln7", cst_wait=cstv, out_pre=out_pre, gp_post=gp_post,
                     ev_post=ev_post)
    items_f = []
    for j in range(NDC * NTG):
        dc, tg = j // NTG, j % NTG
        w = [("ev", evb_ln + j + 1)]
        items_f.append((io("z_d")[dc, :, tg * TGW:(tg + 1) * TGW],
                        zr[:, j % 4, :], w, f"zo{j % 4}"))
    dma_seq(pr, "sync", items_f, "zo_unused")
    if ZL_WRITES:
        dma_seq(pr, "sync", [(io("zlast"), zlcol[:, :],
                              [("zlc", zlc0 + NDC)])], "dout")



def stage_ffn_qkv(pr, al, banks, io, l):
    cstv = load_consts(pr, al, io)
    zbf = al.t("xw", [128, NDC * T], BF16)
    zstage = al.t("xstage", [128, NDC * T], F32)
    zsh = al.t("zsh", [128, 4, T], BF16)
    zg = al.t("zg8", [128, NDC], BF16)
    zitems = ([(zbf[:, dc * T:(dc + 1) * T], io("zin")[dc], [])
               for dc in range(NDC)] if l == 0 else [])
    dma_seq(pr, "sync", zitems + [(zg[:, :], io("zg"), [])], "xin")
    xin_done = pr.cnt["xin"]
    def evsh(e):
        e.wait_ge(pr.s("xin"), xin_done)
        if l != 0:
            for dc in range(NDC):
                e.tensor_copy(zbf[:, dc * T:(dc + 1) * T],
                              zstage[:, dc * T:(dc + 1) * T])
        for dch in range(4):
            e.tensor_copy(zsh[:, dch, 1:T], zbf[:, (4 + dch) * T:(5 + dch) * T - 1])
            inc(pr, e.tensor_copy(zsh[:, dch, 0:1], zg[:, 4 + dch:5 + dch]), "ev")
    pr.blk.vector(evsh)
    sh_done = pr.cnt["ev"]

    def xt(dc, tg):
        if dc < 4:
            return zbf[:, dc * T + tg * TGW: dc * T + (tg + 1) * TGW]
        return zsh[:, dc - 4, tg * TGW:(tg + 1) * TGW]

    tabs = al.t("rp_tabs", [128, 4, T], BF16)
    qk_cols = al.t("qkcols", [128, 8], F32)
    hb4 = al.t("c_hb4", [128, 4], BF16)
    hb4T = al.t("c_hb4T", [4, 128], F32)
    dma_seq(pr, "sync", [(tabs[:, :, :], io("rtf"), []),
                         (qk_cols[:, :], io("qksc"), []),
                         (hb4[:, :], io("c_hb4"), []),
                         (hb4T[:, :], io("c_hb4T"), [])], "cstb")
    cst2 = pr.cnt["cstb"]
    pr.blk.tensor(lambda t: (t.wait_ge(pr.s("ev"), sh_done),
                             t.wait_ge(pr.s("cstb"), cst2)))
    pr.blk.scalar(lambda a: a.wait_ge(pr.s("cstb"), cst2))
    pr.blk.gpsimd(lambda g: g.wait_ge(pr.s("cstb"), cst2))

    wslab = al.t("wsl_qk", [128, NDC, TGW], BF16)
    qsq = al.t("qk_sq", [128, 2, TGW], BF16)
    rr = al.t("qk_rr", [4, 2, TGW], F32)
    rbc = al.t("qk_rbc", [128, 2, TGW], F32)
    pre = al.t("qk_pre", [128, 4, TGW], BF16)
    rs = al.t("rp_rs", [128, 4, TGW], BF16)
    ring = al.t("rp_ring", [128, 4, TGW], BF16)
    for s in ("fq", "fa", "fe", "fg", "fv", "fv2", "fw"):
        pr.s(s)

    for qk in range(2):
        nm = "q" if qk == 0 else "k"
        dst = io(f"{nm}f_d")
        wre = io(f"wf{nm}T").rearrange("(dc p) o -> p dc o", p=128)
        base_w = pr.cnt["fw"]
        dma_seq(pr, "gpsimd", [(wslab[:, :, :], wre[:, :, 0:TGW],
                                [("fq", pr.cnt["fq"])])], "fw")
        mm0 = pr.cnt["fq"]; fa0 = pr.cnt["fa"]; fe0 = pr.cnt["fe"]
        gp0 = pr.cnt["fg"]; fv0 = pr.cnt["fv"]; v20 = pr.cnt["fv2"]
        pr.s("fd0"); pr.s("fd1")
        base_fd = [pr.cnt["fd0"], pr.cnt["fd1"]]
        scol = qk_cols[:, 4 * qk:4 * qk + 4]
        jobs = [(pp, tg) for pp in range(2) for tg in range(NTG)]
        NJ = len(jobs)
        Aq = 0 if qk == 0 else 2
        def tb(t, mm0=mm0, fa0=fa0, fe0=fe0, fv0=fv0, base_w=base_w):
            t.wait_ge(pr.s("fw"), base_w + 16)
            for J, (pp, tg) in enumerate(jobs):
                if J >= 2:
                    t.wait_ge(pr.s("fv"), fv0 + 2 * J - 2)
                    t.wait_ge(pr.s("fa"), fa0 + 3 * J - 4)
                    t.wait_ge(pr.s("fe"), fe0 + 3 * J - 5)
                last = None
                for half in range(2):
                    rt = pp + 2 * half
                    for dc in range(NDC):
                        last = t.matmul(banks[2 * half + J % 2][0:128, 0:TGW],
                                        wslab[:, dc, rt * 128:(rt + 1) * 128],
                                        xt(dc, tg), start=(dc == 0),
                                        stop=(dc == NDC - 1))
                inc(pr, last, "fq")
                t.wait_ge(pr.s("fa"), fa0 + 3 * J + 2)
                t.matmul(banks[4 + J % 2][0:4, 0:TGW], hb4[:, :],
                         qsq[:, 0, :], start=True, stop=False,
                         skip_group_check=True)
                inc(pr, t.matmul(banks[4 + J % 2][0:4, 0:TGW], hb4[:, :],
                                 qsq[:, 1, :], start=False, stop=True,
                                 skip_group_check=True), "fq")
                t.wait_ge(pr.s("fe"), fe0 + 3 * J + 2)
                inc(pr, t.matmul(banks[6 + J % 2][0:128, 0:TGW], hb4T[:, :],
                                 rr[0:4, J % 2, :], start=True, stop=True), "fq")
        pr.blk.tensor(tb)
        def ab(a, mm0=mm0, fa0=fa0, fe0=fe0):
            for J in range(NJ):
                a.wait_ge(pr.s("fq"), mm0 + 3 * J + 1)
                if J >= 1:
                    a.wait_ge(pr.s("fq"), mm0 + 3 * J - 1)
                inc(pr, a.activation(qsq[:, 0, :], banks[J % 2][0:128, 0:TGW],
                                     AF.Square), "fa")
                inc(pr, a.activation(qsq[:, 1, :], banks[2 + J % 2][0:128, 0:TGW],
                                     AF.Square), "fa")
                a.wait_ge(pr.s("fe"), fe0 + 3 * J + 1)
                a.activation(rr[0:4, J % 2, :], rr[0:4, J % 2, :], AF.Ln)
                inc(pr, a.activation(rr[0:4, J % 2, :], rr[0:4, J % 2, :],
                                     AF.Exp, scale=0.5), "fa")
        pr.blk.scalar(ab)
        def emit_comb(e, J):
            pp, tg = jobs[J]
            e.wait_ge(pr.s("fg"), gp0 + J + 1)
            if J >= 2:
                e.wait_ge(pr.s(f"fd{J % 2}"), base_fd[J % 2] + 128 * (J // 2))
            elif base_fd[J % 2] > 0:
                e.wait_ge(pr.s(f"fd{J % 2}"), base_fd[J % 2])
            sl = (2 * J) % 4
            e.tensor_tensor(ring[:, sl, :], rs[:, 0, :], rs[:, 1, :], ALU.add)
            inc(pr, e.tensor_tensor(ring[:, sl + 1, :], rs[:, 2, :],
                                    rs[:, 3, :], ALU.subtract), "fv2")
        def vb(e, mm0=mm0, fa0=fa0, fe0=fe0, gp0=gp0):
            for J, (pp, tg) in enumerate(jobs):
                e.wait_ge(pr.s("fq"), mm0 + 3 * J + 2)
                inc(pr, e.tensor_copy(rr[0:4, J % 2, :],
                                      banks[4 + J % 2][0:4, 0:TGW]), "fe")
                e.wait_ge(pr.s("fa"), fa0 + 3 * J + 3)
                e.tensor_scalar_max(rr[0:4, J % 2, :], rr[0:4, J % 2, :], 1e-12)
                inc(pr, e.reciprocal(rr[0:4, J % 2, :], rr[0:4, J % 2, :]), "fe")
                e.wait_ge(pr.s("fq"), mm0 + 3 * J + 3)
                inc(pr, e.tensor_copy(rbc[:, J % 2, :],
                                      banks[6 + J % 2][0:128, 0:TGW]), "fe")
                for half in range(2):
                    if J >= 2:
                        e.wait_ge(pr.s("fg"), gp0 + J - 1)
                    inc(pr, e.scalar_tensor_tensor(
                        pre[:, 2 * half + J % 2, :],
                        banks[2 * half + J % 2][0:128, 0:TGW],
                        scol[:, pp + 2 * half: pp + 2 * half + 1],
                        rbc[:, J % 2, :], ALU.mult, ALU.mult), "fv")
                if J >= 1:
                    emit_comb(e, J - 1)
            emit_comb(e, NJ - 1)
        pr.blk.vector(vb)
        def gb(g, gp0=gp0, fv0=fv0, v20=v20):
            for J, (pp, tg) in enumerate(jobs):
                g.wait_ge(pr.s("fv"), fv0 + 2 * (J + 1))
                if J >= 1:
                    g.wait_ge(pr.s("fv2"), v20 + J)
                A = tabs[:, Aq, tg * TGW:(tg + 1) * TGW]
                B = tabs[:, Aq + 1, tg * TGW:(tg + 1) * TGW]
                lo = pre[:, 0 + J % 2, :]
                hi = pre[:, 2 + J % 2, :]
                g.tensor_tensor(rs[:, 0, :], hi, A, ALU.mult)
                g.tensor_tensor(rs[:, 1, :], lo, B, ALU.mult)
                g.tensor_tensor(rs[:, 2, :], lo, A, ALU.mult)
                inc(pr, g.tensor_tensor(rs[:, 3, :], hi, B, ALU.mult), "fg")
        pr.blk.gpsimd(gb)
        items = []
        for J, (pp, tg) in enumerate(jobs):
            sl = (2 * J) % 4
            for g4 in range(4):
                h = 4 * pp + g4
                w = [("fv2", v20 + J + 1)]
                items.append((dst[h // 2, (h % 2) * 64 + 32:(h % 2) * 64 + 64,
                                  tg * TGW:(tg + 1) * TGW],
                              ring[32 * g4:32 * g4 + 32, sl, :], w,
                              f"fd{J % 2}"))
                items.append((dst[h // 2, (h % 2) * 64:(h % 2) * 64 + 32,
                                  tg * TGW:(tg + 1) * TGW],
                              ring[32 * g4:32 * g4 + 32, sl + 1, :], w,
                              f"fd{J % 2}"))
        dma_seq(pr, "sync", items, "fd_unused")

    vringv = al.t("vring520", [128, 4, 520], BF16)
    pr.blk.vector(lambda e: [e.memset(vringv[:, i, 64::65], 1.0)
                             for i in range(4)])
    wre = io("wfvT").rearrange("(dc p) o -> p dc o", p=128)
    base_w = pr.cnt["fw"]
    dma_seq(pr, "gpsimd", [(wslab[:, :, :], wre[:, :, 0:TGW],
                            [("fq", pr.cnt["fq"])])], "fw")
    vdst = io("vf_d")
    mm0 = pr.cnt["fq"]
    fa_t = pr.cnt["fa"]; fv_t = pr.cnt["fv"]; fe_t = pr.cnt["fe"]
    for i in range(4):
        pr.s(f"vdo{i}")
    base_vd = [pr.cnt[f"vdo{i}"] for i in range(4)]
    ev0v = pr.cnt["fe"]
    def tv(t):
        t.wait_ge(pr.s("fw"), base_w + 16)
        t.wait_ge(pr.s("fa"), fa_t)
        t.wait_ge(pr.s("fv"), fv_t)
        for tt in range(16):
            if tt >= 2:
                t.wait_ge(pr.s("fe"), ev0v + tt - 1)
            last = None
            for dc in range(NDC):
                last = t.matmul(banks[tt % 2][0:128, 0:TGW],
                                zbf[:, dc * T + tt * 128: dc * T + (tt + 1) * 128]
                                if dc < 4 else
                                zsh[:, dc - 4, tt * 128:(tt + 1) * 128],
                                wslab[:, dc, :], start=(dc == 0),
                                stop=(dc == NDC - 1))
            inc(pr, last, "fq")
    pr.blk.tensor(tv)
    def vv(e):
        for tt in range(16):
            e.wait_ge(pr.s("fq"), mm0 + tt + 1)
            if tt >= 4:
                e.wait_ge(pr.s(f"vdo{tt % 4}"),
                          base_vd[tt % 4] + 16 * (tt // 4))
            dstv = vringv[:, tt % 4, :]
            inc(pr, e.tensor_copy(
                bass.AP(dstv.tensor, dstv.offset,
                        [list(dstv.ap[0]), [65, 8], [1, 64]]),
                banks[tt % 2][0:128, 0:TGW]), "fe")
    pr.blk.vector(vv)
    items = [(vdst[tt], vringv[:, tt % 4, :],
              [("fe", ev0v + tt + 1)], f"vdo{tt % 4}") for tt in range(16)]
    dma_seq(pr, "sync", items, "vdo_unused")


def stage_ffn_attn(pr, al, banks, io, z_resident=False):
    cstv = load_consts(pr, al, io)
    zstage = al.t("xstage", [128, NDC * T], F32)
    qf = al.t("fa_qf", [128, 4, T], BF16)
    kf = al.t("fa_kf", [128, 4, T], BF16)
    vf = al.t("fa_vf", [128, 16, 520], BF16)
    kb = al.t("fa_kb", [128, 4, TGW], BF16)
    vb = al.t("fa_vb", [128, 4, 520], BF16)
    msk = al.t("fa_msk", [128, 2, 8, TGW], BF16)
    identb = al.t("c_ident_bf", [128, 128], BF16)
    pbf = al.t("fa_pbf", [128, 2, 2, TGW], BF16)
    rden = al.t("fa_rden", [1, 2 * TGW], F32)
    of = al.t("fa_of", [128, 4, T], BF16)
    qkr = al.t("fa_qkr", [1, 128], F32)
    cbias = al.t("fa_cb", [128, 1], F32)
    rbcs = al.t("fa_rbc", [128, 2, TGW], F32)
    zitems = ([] if z_resident else
              [(zstage[:, dc * T:(dc + 1) * T], io("z32in")[dc], [])
               for dc in range(NDC)])
    dma_seq(pr, "sync", zitems +
                        [(qf[:, :, :], io("qf_d").rearrange("c p t -> p c t"), []),
                         (kf[:, :, :], io("kf_d").rearrange("c p t -> p c t"), []),
                         (vf[:, :, :], io("vf_d").rearrange("c p t -> p c t"), []),
                         (kb[:, :, :], io("kb_x"), []),
                         (vb[:, :, :], io("vb_x"), []),
                         (msk[:, :, :, :],
                          io("mskf").rearrange("w s p t -> p w s t"), []),
                         (identb[:, :], io("c_ident_bf"), []),
                         (qkr[0:1, :], io("qkr"), [])], "xin")
    xin_done = pr.cnt["xin"]
    for s in ("lq", "la", "lv"):
        pr.s(s)
    lq_b = pr.cnt["lq"]; la0 = pr.cnt["la"]; lv_b = pr.cnt["lv"]
    lq0 = lq_b + 1
    lv0 = lv_b + 2
    SA = [banks[0], banks[2]]
    SB = [banks[1], banks[3]]
    PA, PB = banks[4], banks[5]
    BC, BC2 = banks[6], banks[7]
    jobs = [(hp, w) for hp in range(4) for w in range(4)]

    def ksrc(hp, half, w, jc):
        r0 = half * 64
        if jc < 4:
            if w == 0:
                return kb[r0:r0 + 64, hp, jc * 128:(jc + 1) * 128]
            return kf[r0:r0 + 64, hp, (w - 1) * TGW + jc * 128:
                      (w - 1) * TGW + (jc + 1) * 128]
        return kf[r0:r0 + 64, hp, w * TGW + (jc - 4) * 128:
                  w * TGW + (jc - 3) * 128]

    def vsl(hp, half, w, jc, c0, cn):
        h = 2 * hp + half
        if jc < 4:
            if w == 0:
                return vb[:, jc, h * 65 + c0:h * 65 + c0 + cn]
            return vf[:, 4 * (w - 1) + jc, h * 65 + c0:h * 65 + c0 + cn]
        return vf[:, 4 * w + (jc - 4), h * 65 + c0:h * 65 + c0 + cn]

    def tb(t):
        t.wait_ge(pr.s("xin"), xin_done)
        t.wait_ge(pr.s("lv"), lv_b + 1)
        inc(pr, t.matmul(BC[0:128, 0:1], al.ones_f32[0:1, :],
                         rden[0:1, 64:65], start=True, stop=True), "lq")
        def emit_pv(J, i):
            hp, w = jobs[J]
            t.wait_ge(pr.s("la"), la0 + 16 * J + 2 * i + 2)
            t.matmul(PA[0:65, :], vsl(hp, 0, w, i, 0, 65), pbf[:, i % 2, 0, :],
                     start=(i == 0), stop=(i == 7), skip_group_check=True)
            t.matmul(PB[64:128, :], vsl(hp, 1, w, i, 0, 64), pbf[:, i % 2, 1, :],
                     start=(i == 0), stop=(i == 7), skip_group_check=True)
            last = t.matmul(PB[0:1, :], vsl(hp, 1, w, i, 64, 1),
                            pbf[:, i % 2, 1, :], start=(i == 0), stop=(i == 7),
                            skip_group_check=True)
            if i == 7:
                inc(pr, last, "lq")
        for J, (hp, w) in enumerate(jobs):
            wcl = 0 if w == 0 else 1
            for i in range(8):
                sw = la0 + 16 * J + 2 * (i - 1)
                if sw > la0 or J > 0:
                    t.wait_ge(pr.s("la"), max(sw, 0))
                if i == 0 and J >= 1:
                    t.wait_ge(pr.s("lv"), lv0 + 6 * J)
                for half in range(2):
                    bank = (SA if half == 0 else SB)[i % 2]
                    t.matmul(bank[:, :], ksrc(hp, half, w, i),
                             qf[half * 64:half * 64 + 64, hp,
                                w * TGW:(w + 1) * TGW],
                             start=True, stop=False, skip_group_check=True)
                    last = t.matmul(bank[:, :], identb[:, :],
                                    msk[:, wcl, i, :], start=False, stop=True,
                                    skip_group_check=True)
                    if half == 1:
                        inc(pr, last, "lq")
                if i >= 1:
                    emit_pv(J, i - 1)
            emit_pv(J, 7)
            t.wait_ge(pr.s("lv"), lv0 + 6 * J + 2)
            t.matmul(BC[0:128, 0:TGW], al.ones_f32[0:1, :], rden[0:1, 0:TGW],
                     start=True, stop=True)
            inc(pr, t.matmul(BC2[0:128, 0:TGW], al.ones_f32[0:1, :],
                             rden[0:1, TGW:2 * TGW], start=True, stop=True),
                "lq")
    pr.blk.tensor(tb)

    def ab(a):
        a.wait_ge(pr.s("lv"), lv_b + 2)
        for J, (hp, w) in enumerate(jobs):
            for i in range(8):
                a.wait_ge(pr.s("lq"), lq0 + 10 * J + i + 1)
                inc(pr, a.activation(pbf[:, i % 2, 0, :], SA[i % 2][:, :],
                                     AF.Exp, scale=8.0, bias=cbias[:, 0:1]),
                    "la")
                inc(pr, a.activation(pbf[:, i % 2, 1, :], SB[i % 2][:, :],
                                     AF.Exp, scale=8.0, bias=cbias[:, 0:1]),
                    "la")
    pr.blk.scalar(ab)

    def vbod(e):
        e.wait_ge(pr.s("xin"), xin_done)
        e.reduce_max(rden[0:1, 0:1], qkr[0:1, 0:64], mybir.AxisListType.X)
        ser(pr, e, e.reduce_max(rden[0:1, 1:2], qkr[0:1, 64:128],
                                mybir.AxisListType.X))
        inc(pr, e.tensor_tensor(rden[0:1, 64:65], rden[0:1, 0:1],
                                rden[0:1, 1:2], ALU.mult), "lv")
        e.wait_ge(pr.s("lq"), lq_b + 1)
        inc(pr, e.tensor_scalar_mul(cbias[:, 0:1], BC[0:128, 0:1], -8.5), "lv")
        for J, (hp, w) in enumerate(jobs):
            e.wait_ge(pr.s("lq"), lq0 + 10 * J + 9)
            inc(pr, e.reciprocal(rden[0:1, 0:TGW], PA[64:65, :]), "lv")
            inc(pr, e.reciprocal(rden[0:1, TGW:2 * TGW], PB[0:1, :]), "lv")
            e.wait_ge(pr.s("lq"), lq0 + 10 * J + 10)
            inc(pr, e.tensor_copy(rbcs[0:64, 0, :], BC[0:64, :]), "lv")
            inc(pr, e.tensor_copy(rbcs[64:128, 1, :], BC2[64:128, :]), "lv")
            inc(pr, e.tensor_tensor(of[0:64, hp, w * TGW:(w + 1) * TGW],
                                    PA[0:64, :], rbcs[0:64, 0, :], ALU.mult),
                "lv")
            inc(pr, e.tensor_tensor(of[64:128, hp, w * TGW:(w + 1) * TGW],
                                    PB[64:128, :], rbcs[64:128, 1, :],
                                    ALU.mult), "lv")
    pr.blk.vector(vbod)
    of_done_lv = pr.cnt["lv"]

    pr.blk.tensor(lambda t: t.wait_ge(pr.s("lv"), of_done_lv))
    pr.s("ev")
    def ev_fn(e, rt, rw, tg, ps):
        xs = zstage[:, rt * T + tg * TGW: rt * T + (tg + 1) * TGW]
        return e.tensor_tensor(xs, xs, ps[0:128, :], ALU.add)
    emit_proj_fm(pr, al, banks[:4],
                 rhs_tile=lambda dc, tg: of[:, dc, tg * TGW:(tg + 1) * TGW],
                 wT=io("wfoT"), n_in=512, n_out=D, ev_fn=ev_fn, tag="wo")
    z2_done = pr.cnt["ev"]
    extra = []
    if ZL_WRITES:
        zs0 = zstage[:, T - 1:T]
        zlap = bass.AP(zs0.tensor, zs0.offset, [list(zs0.ap[0]), [T, NDC]])
        zlc2 = al.t("zlcol2", [128, NDC], F32)
        pr.s("zlc")

        def evzl(e):
            e.wait_ge(pr.s("ev"), z2_done)
            inc(pr, e.tensor_copy(zlc2[:, :], zlap), "zlc")
        pr.blk.vector(evzl)
        extra = [(io("z2last"), zlc2[:, :], [("zlc", pr.cnt["zlc"])])]
    if extra:
        dma_seq(pr, "sync", extra, "dout")


def stage_ffn_ffn(pr, al, banks, io, final=False):
    cstv = load_consts(pr, al, io)
    zstage = al.t("xstage", [128, NDC * T], F32)
    xw = al.t("xw", [128, NDC * T], BF16)
    h = al.t("moe_h", [128, NK, TGW], BF16)
    arena1 = al.t("arena1", [128, 24576], BF16)
    arena2 = al.t("arena2", [128, 16384], BF16)
    zsh32 = al.overlay("zsh32", [128, 4, T], F32, "arena1", 0)
    al.overlay("lnsq_lnf", [128, 2, T], BF16, "arena1", 32768)
    al.overlay("lnxb_lnf", [128, 2, T], BF16, "arena1", 40960)
    wgu = al.overlay("ff_wgu", [128, NDC, 2816], BF16, "arena1", 0)
    al.overlay("lnbc_lnf", [128, 2 * T], F32, "arena2", 0)
    al.overlay("lnnrm_lnf", [128, 4, TGW], F32, "arena2", 16384)
    al.overlay("lnrows_lnf", [1, 4 * TGW], F32, "arena2", 24576)
    al.overlay("lnnrm2_lnf", [128, 4, TGW], F32, "moe_h", 0)
    wgd = al.overlay("ff_wgd", [128, NK, 1024], BF16, "arena2", 0)
    zg32 = al.t("zg32", [128, NDC], F32)
    lncols = al.t("lncols", [128, 2 * NDC], F32)
    b1c = al.t("fb1c", [128, 2, 22], F32)
    b2c = al.t("fb2c", [128, NDC], F32)
    h = al.t("moe_h", [128, NK, TGW], BF16)
    gbf = al.t("moe_g", [128, 2, TGW], BF16)
    dma_seq(pr, "sync", [(zg32[:, :], io("zg32"), []),
                         (lncols[:, :], io("lncols"), []),
                         (b1c[:, :, :], io("fb1cols"), []),
                         (b2c[:, :], io("fb2cols"), [])], "xin")
    xin_done = pr.cnt["xin"]
    pr.s("ev"); pr.s("ac"); pr.s("gp"); pr.s("mm")
    def evsh(e):
        e.wait_ge(pr.s("xin"), xin_done)
        for dch in range(4):
            e.tensor_copy(zsh32[:, dch, 1:T],
                          zstage[:, (4 + dch) * T:(5 + dch) * T - 1])
            inc(pr, e.tensor_copy(zsh32[:, dch, 0:1], zg32[:, 4 + dch:5 + dch]),
                "ev")
    pr.blk.vector(evsh)
    sh_done = pr.cnt["ev"]
    pr.blk.scalar(lambda a: a.wait_ge(pr.s("ev"), sh_done))
    pr.blk.vector(lambda e: e.wait_ge(pr.s("ev"), sh_done))
    def x_dc(dc):
        if dc < 4:
            return zstage[:, dc * T:(dc + 1) * T]
        return zsh32[:, dc - 4, 0:T]
    emit_ln(pr, al, banks, x_dc=x_dc,
            out_tile=lambda dc, tg: xw[:, dc * T + tg * TGW:
                                       dc * T + (tg + 1) * TGW],
            tag="lnf", cst_wait=cstv,
            affine_cols=(lncols[:, 0:NDC], lncols[:, NDC:2 * NDC]))
    ln_done = pr.cnt["ev"]
    if DBG_FF and not final:
        bc_t = al._cache["lnbc_lnf"]
        d1 = pr.nc.dram_tensor("dbgff_bc", [128, 2 * T], F32,
                               kind="ExternalOutput").ap()
        d2 = pr.nc.dram_tensor("dbgff_xw", [128, NDC * T], BF16,
                               kind="ExternalOutput").ap()
        d3 = pr.nc.dram_tensor("dbgff_zsh", [128, 4 * T], F32,
                               kind="ExternalOutput").ap()
        dma_seq(pr, "sync", [
            (d1, bc_t[:, :], [("ev", ln_done)]),
            (d2, xw[:, :], [("ev", ln_done)]),
            (d3, zsh32[:, :, :].rearrange("p a t -> p (a t)"),
             [("ev", ln_done)]),
        ], "dout")

    for s in ("mg", "mu", "md", "ag", "eu", "ed"):
        pr.s(s)
    mg0 = pr.cnt["mg"]; mu0 = pr.cnt["mu"]; md0 = pr.cnt["md"]
    ag0 = pr.cnt["ag"]; eu0 = pr.cnt["eu"]; ed0 = pr.cnt["ed"]
    guT = io("fw1T")
    gdT = io("fw2T")
    pr.s("mgu"); pr.s("mgd")
    mgu0 = pr.cnt["mgu"]; mgd0 = pr.cnt["mgd"]
    itemsA, itemsB = [], []
    for e in range(2):
        wu = [("ev", ln_done)] if e == 0 else [("mu", mu0 + NTG * NK * e)]
        wd = [("ev", ln_done)] if e == 0 else [("md", md0 + 32 * e)]
        gre = guT[e].rearrange("(dc p) o -> p dc o", p=128)
        dre = gdT[e].rearrange("(kc p) o -> p kc o", p=128)
        itemsA.append((wgu[:, :, 0:1408], gre[:, :, 0:1408], list(wu), "mgu"))
        itemsB.append((wgu[:, :, 1408:2816], gre[:, :, 1408:2816],
                       list(wu), "mgu"))
        itemsA.append((wgd[:, :, 0:512], dre[:, :, 0:512], list(wd), "mgd"))
        itemsB.append((wgd[:, :, 512:1024], dre[:, :, 512:1024],
                       list(wd), "mgd"))
    if MOE_OLD:
        inter = []
        for i in range(len(itemsA)):
            inter.append(itemsA[i]); inter.append(itemsB[i])
        dma_seq(pr, "gpsimd", inter, "mgu_unusedA")
    else:
        dma_seq(pr, "gpsimd", itemsA, "mgu_unusedA")
        dma_seq(pr, "sync", itemsB, "mgu_unusedB")
    tmpu = al.t("moe_tmpu", [128, TGW], BF16)

    def xlt(dc, tg):
        return xw[:, dc * T + tg * TGW: dc * T + (tg + 1) * TGW]

    def tmoe(t):
        t.wait_ge(pr.s("ev"), ln_done)
        for e in range(2):
            t.wait_ge(pr.s("mgu"), mgu0 + 32 * (e + 1))
            t.wait_ge(pr.s("mgd"), mgd0 + 32 * (e + 1))
            for tg in range(NTG):
                for k in range(NK):
                    for half in range(2):
                        jidx = ((e * NTG + tg) * NK + k) * 2 + half
                        bank = banks[jidx % 4]
                        pj = jidx - 4
                        if pj >= 0:
                            if pj % 2 == 0:
                                t.wait_ge(pr.s("eu"), eu0 + pj // 2 + 1)
                            else:
                                t.wait_ge(pr.s("ag"), ag0 + pj // 2 + 1)
                        last = None
                        for dc in range(NDC):
                            last = t.matmul(
                                bank[0:128, 0:TGW],
                                wgu[:, dc, (2 * k + half) * 128:
                                    (2 * k + half + 1) * 128],
                                xlt(dc, tg),
                                start=(dc == 0), stop=(dc == NDC - 1))
                        inc(pr, last, "mg" if half == 0 else "mu")
                for rt in range(NDC):
                    didx = (e * NTG + tg) * NDC + rt
                    bank = banks[4 + didx % 2]
                    if didx >= 2:
                        t.wait_ge(pr.s("ed"), ed0 + didx - 1)
                    t.wait_ge(pr.s("eu"), eu0 + (e * NTG + tg + 1) * NK)
                    last = None
                    for kc in range(NK):
                        last = t.matmul(bank[0:128, 0:TGW],
                                        wgd[:, kc, rt * 128:(rt + 1) * 128],
                                        h[:, kc, :],
                                        start=(kc == 0), stop=(kc == NK - 1))
                    inc(pr, last, "md")
    pr.blk.tensor(tmoe)

    def amoe(a):
        for e in range(2):
            for tg in range(NTG):
                for k in range(NK):
                    gidx = (e * NTG + tg) * NK + k
                    jidx = gidx * 2 + 1
                    a.wait_ge(pr.s("mu"), mu0 + gidx + 1)
                    if gidx >= 2:
                        a.wait_ge(pr.s("eu"), eu0 + gidx - 1)
                    inc(pr, a.activation(gbf[:, k % 2, :],
                                         banks[jidx % 4][0:128, 0:TGW],
                                         AF.Gelu, bias=b1c[:, e, 2 * k + 1:
                                                          2 * k + 2]), "ag")
    pr.blk.scalar(amoe)

    def vmoe(e_):
        for e in range(2):
            for tg in range(NTG):
                for k in range(NK):
                    gidx = (e * NTG + tg) * NK + k
                    jidx = gidx * 2
                    if k == 0 and (e, tg) != (0, 0):
                        e_.wait_ge(pr.s("md"), md0 + (e * NTG + tg) * NDC)
                    e_.wait_ge(pr.s("mg"), mg0 + gidx + 1)
                    e_.tensor_scalar_add(tmpu[:, :], banks[jidx % 4][0:128, 0:TGW],
                                         b1c[:, e, 2 * k:2 * k + 1])
                    e_.wait_ge(pr.s("ag"), ag0 + gidx + 1)
                    inc(pr, e_.tensor_tensor(h[:, k, :], tmpu[:, :],
                                             gbf[:, k % 2, :], ALU.mult), "eu")
                for rt in range(NDC):
                    didx = (e * NTG + tg) * NDC + rt
                    e_.wait_ge(pr.s("md"), md0 + didx + 1)
                    xs = zstage[:, rt * T + tg * TGW: rt * T + (tg + 1) * TGW]
                    inc(pr, e_.scalar_tensor_tensor(
                        xs, banks[4 + didx % 2][0:128, 0:TGW],
                        b2c[:, rt:rt + 1] if e == 1 else 0.0,
                        xs, ALU.add, ALU.add), "ed")
    pr.blk.vector(vmoe)
    z3_done = pr.cnt["ed"]

    if final:
        x3t = al.t("x3t", [128, 2, TGW], F32)
        pr.s("fx0"); pr.s("fx1")
        base_fx = [pr.cnt["fx0"], pr.cnt["fx1"]]
        ev_f0 = pr.cnt["ev"]
        items = []
        for j in range(NDC * NTG):
            dc, tg = j // NTG, j % NTG
            w = [("ed", z3_done)]
            if j >= 2:
                w.append(("ev", ev_f0 + j - 1))
            items.append((x3t[:, j % 2, :],
                          io("x3_d")[dc, :, tg * TGW:(tg + 1) * TGW], w,
                          f"fx{j % 2}"))
        dma_seq(pr, "sync", items, "fx_unused")
        def evf(e):
            for j in range(NDC * NTG):
                dc, tg = j // NTG, j % NTG
                e.wait_ge(pr.s(f"fx{j % 2}"), base_fx[j % 2] + 16 * (j // 2 + 1))
                xs = zstage[:, dc * T + tg * TGW: dc * T + (tg + 1) * TGW]
                inc(pr, e.tensor_tensor(xs, xs, x3t[:, j % 2, :], ALU.add), "ev")
        pr.blk.vector(evf)
        out_done = pr.cnt["ev"]
        dma_seq(pr, "sync", [(io("zout")[dc], zstage[:, dc * T:(dc + 1) * T],
                              [("ev", out_done)]) for dc in range(NDC)], "dout")
    else:
        extra = []
        if ZL_WRITES:
            zs0 = zstage[:, T - 1:T]
            zlap = bass.AP(zs0.tensor, zs0.offset, [list(zs0.ap[0]), [T, NDC]])
            zlc2 = al.t("zlcol2", [128, NDC], F32)
            pr.s("zlc")

            def evzl(e):
                e.wait_ge(pr.s("ed"), z3_done)
                inc(pr, e.tensor_copy(zlc2[:, :], zlap), "zlc")
            pr.blk.vector(evzl)
            extra = [(io("z3last"), zlc2[:, :], [("zlc", pr.cnt["zlc"])])]
        if extra:
            dma_seq(pr, "sync", extra, "dout")


# version 18
# speedup vs baseline: 1.0630x; 1.0630x over previous
import numpy as np
import ml_dtypes
import concourse.bass as bass
import concourse.mybir as mybir
from concourse.bass_utils import run_bass_kernel_spmd

F32 = mybir.dt.float32
BF16 = mybir.dt.bfloat16
U32 = mybir.dt.uint32
AF = mybir.ActivationFunctionType
ALU = mybir.AluOpType
NPBF = ml_dtypes.bfloat16

T = 2048
D = 1024
NDC = 8
TGW = 512
NTG = T // TGW
S_FULL = 4096
NCORE = 8


class Prog:
    def __init__(self, nc):
        self.nc = nc
        self.blk = None
        self.sems = {}
        self.cnt = {}

    def s(self, name):
        if name not in self.sems:
            self.sems[name] = self.nc.alloc_semaphore(name=f"sem_{name}")
            self.cnt[name] = 0
        return self.sems[name]


def inc(pr, inst, name, n=1):
    inst.then_inc(pr.s(name), n)
    pr.cnt[name] += n
    return pr.cnt[name]


def ser(pr, eng, inst, name="chx"):
    inc(pr, inst, name)
    eng.wait_ge(pr.s(name), pr.cnt[name])
    return inst


def pipe(pr, jobs, banks, ev_eng="vector", mm="mm", ev="ev"):
    nb = len(banks)
    base_mm = pr.cnt[mm]
    base_ev = pr.cnt[ev]
    n = len(jobs)
    pr.s(mm); pr.s(ev)

    def tbody(t):
        for i, job in enumerate(jobs):
            if i >= nb:
                t.wait_ge(pr.s(ev), base_ev + i - nb + 1)
            if 'pre_mm' in job:
                job['pre_mm'](t)
            job['mm'](t, banks[i % nb]).then_inc(pr.s(mm), 1)

    def ebody(e):
        for i, job in enumerate(jobs):
            e.wait_ge(pr.s(mm), base_mm + i + 1)
            if 'pre_ev' in job:
                job['pre_ev'](e)
            job['ev'](e, banks[i % nb]).then_inc(pr.s(ev), 1)

    pr.blk.tensor(tbody)
    pr.cnt[mm] += n
    getattr(pr.blk, ev_eng)(ebody)
    pr.cnt[ev] += n
    return base_mm, base_ev


def dma_seq(pr, eng, items, sem, nsem=1):
    names = [sem] if nsem == 1 else [f"{sem}{i}" for i in range(nsem)]
    for n in names:
        pr.s(n)
    def semof(j, it):
        return it[3] if len(it) > 3 else names[j % nsem]
    for j, it in enumerate(items):
        pr.s(semof(j, it))
    def body(e):
        for j, it in enumerate(items):
            dst, src, waits = it[0], it[1], it[2]
            for (sn, v) in waits:
                e.wait_ge(pr.s(sn), v)
            e.dma_start(out=dst, in_=src).then_inc(pr.s(semof(j, it)), 16)
    getattr(pr.blk, eng)(body)
    for j, it in enumerate(items):
        pr.cnt[semof(j, it)] += 16
    return pr.cnt[names[0]]


def ceil_div(a, b):
    return (a + b - 1) // b


class SbAl:
    def __init__(self, nc):
        self.nc = nc
        self._cache = {}

    def t(self, name, shape, dtype):
        if name in self._cache:
            return self._cache[name]
        h = self.nc.alloc_sbuf_tensor("sb_" + name, shape, dtype)
        self._cache[name] = h
        return h

    def overlay(self, name, shape, dtype, base_name, byte_off=0):
        if name in self._cache:
            return self._cache[name]
        base = self.nc.lookup_mloc(self._cache[base_name]).addr + byte_off
        h = self.nc.alloc_sbuf_tensor_at("sb_" + name, shape, dtype, offset=base)
        self._cache[name] = h
        return h



def load_consts(pr, al, io):
    ones_bf = al.t("c_ones_bf", [128, 1], BF16)
    ones_f32 = al.t("c_ones_f32", [1, 128], F32)
    al.ones_bf = ones_bf
    al.ones_f32 = ones_f32
    dma_seq(pr, "sync", [
        (ones_bf[:, :], io("c_ones_bf"), []),
        (ones_f32[0:1, :], io("c_ones_f32"), []),
    ], "cst")
    return pr.cnt["cst"]


def emit_ln(pr, al, banks, *, x_sb=None, x_dc=None, out_tile, tag,
            affine_cols=None, cst_wait=0, out_f32=False, out_pre=None,
            gp_post=None, ev_post=None, pass1_eng="gpsimd"):
    if x_dc is None:
        x_dc = lambda dc: x_sb[:, dc * T:(dc + 1) * T]
    sq = al.t(f"lnsq_{tag}", [128, 2, T], BF16)
    xb = al.t(f"lnxb_{tag}", [128, 2, T], BF16)
    rows = al.t(f"lnrows_{tag}", [1, 4 * TGW], F32)
    bc = al.t(f"lnbc_{tag}", [128, 2 * T], F32)
    nrm = al.t(f"lnnrm_{tag}", [128, 4, TGW], F32)
    stats_ps = banks[:NTG]
    bbA, bbB = banks[NTG], banks[NTG + 1]
    for s in ("mm", "ac", "ev", "gp"):
        pr.s(s)
    mm0 = pr.cnt["mm"]; ac0 = pr.cnt["ac"]; ev0 = pr.cnt["ev"]

    def acb(a):
        a.wait_ge(pr.s("cst"), cst_wait)
        for dc in range(NDC):
            if dc >= 2:
                a.wait_ge(pr.s("mm"), mm0 + (dc - 1))
            inc(pr, a.activation(sq[:, dc % 2, :], x_dc(dc),
                                 AF.Square), "ac")
    pr.blk.scalar(acb)

    def evb(e):
        for dc in range(NDC):
            if dc >= 2:
                e.wait_ge(pr.s("mm"), mm0 + (dc - 1))
            inc(pr, e.tensor_copy(xb[:, dc % 2, :], x_dc(dc)), "ev")
    pr.blk.vector(evb)

    def tb(t):
        for dc in range(NDC):
            t.wait_ge(pr.s("ac"), ac0 + dc + 1)
            t.wait_ge(pr.s("ev"), ev0 + dc + 1)
            last = None
            for tg in range(NTG):
                t.matmul(stats_ps[tg][0:1, :], al.ones_bf[:, 0:1],
                         xb[:, dc % 2, tg * TGW:(tg + 1) * TGW],
                         start=(dc == 0), stop=(dc == NDC - 1),
                         skip_group_check=True)
                last = t.matmul(stats_ps[tg][64:65, :], al.ones_bf[:, 0:1],
                                sq[:, dc % 2, tg * TGW:(tg + 1) * TGW],
                                start=(dc == 0), stop=(dc == NDC - 1),
                                skip_group_check=True)
            last.then_inc(pr.s("mm"), 1)
            pr.cnt["mm"] += 1
    pr.blk.tensor(tb)
    mm_stats = pr.cnt["mm"]

    R = lambda i: rows[0:1, i * TGW:(i + 1) * TGW]
    for tg in range(NTG):
        def evfin(e, tg=tg):
            e.wait_ge(pr.s("mm"), mm_stats)
            if tg > 0:
                e.wait_ge(pr.s("ev"), pr._ln_bcast_done)
            e.tensor_scalar_mul(R(0), stats_ps[tg][0:1, :], 1.0 / D)
            e.tensor_scalar_mul(R(1), stats_ps[tg][64:65, :], 1.0 / D)
            e.tensor_tensor(R(2), R(0), R(0), ALU.mult)
            e.tensor_tensor(R(1), R(1), R(2), ALU.subtract)
            inc(pr, e.tensor_scalar_add(R(1), R(1), 1e-5), "ev")
        pr.blk.vector(evfin)
        evv = pr.cnt["ev"]
        def acfin(a, tg=tg):
            a.wait_ge(pr.s("ev"), evv)
            a.activation(R(2), R(1), AF.Ln)
            inc(pr, a.activation(R(2), R(2), AF.Exp, scale=-0.5), "ac")
        pr.blk.scalar(acfin)
        acv = pr.cnt["ac"]
        def evfin2(e, tg=tg):
            e.wait_ge(pr.s("ac"), acv)
            inc(pr, e.scalar_tensor_tensor(R(3), R(0), -1.0, R(2),
                                           ALU.mult, ALU.mult), "ev")
        pr.blk.vector(evfin2)
        evv2 = pr.cnt["ev"]
        def tbb(t, tg=tg):
            t.wait_ge(pr.s("ev"), evv2)
            t.matmul(bbA[0:128, 0:TGW], al.ones_f32[0:1, :], R(2),
                     start=True, stop=True)
            inc(pr, t.matmul(bbB[0:128, 0:TGW], al.ones_f32[0:1, :], R(3),
                             start=True, stop=True), "mm")
        pr.blk.tensor(tbb)
        mmb = pr.cnt["mm"]
        def evbc(e, tg=tg):
            e.wait_ge(pr.s("mm"), mmb)
            e.tensor_copy(bc[:, tg * TGW:(tg + 1) * TGW], bbA[:, 0:TGW])
            inc(pr, e.tensor_copy(bc[:, T + tg * TGW:T + (tg + 1) * TGW],
                                  bbB[:, 0:TGW]), "ev")
        pr.blk.vector(evbc)
        pr._ln_bcast_done = pr.cnt["ev"]
    bc_done = pr.cnt["ev"]

    if pass1_eng == "vector":
        assert gp_post is None
        ev_out_base = pr.cnt["ev"]
        def evnf(e):
            e.wait_ge(pr.s("ev"), bc_done)
            j = 0
            for dc in range(NDC):
                for tg in range(NTG):
                    e.tensor_tensor(nrm[:, j % 4, :],
                                    x_dc(dc)[:, tg * TGW:(tg + 1) * TGW],
                                    bc[:, tg * TGW:(tg + 1) * TGW], ALU.mult)
                    if out_pre is not None:
                        out_pre(e, j)
                    if affine_cols is None:
                        inc(pr, e.tensor_tensor(
                            out_tile(dc, tg), nrm[:, j % 4, :],
                            bc[:, T + tg * TGW:T + (tg + 1) * TGW],
                            ALU.add), "ev")
                    else:
                        e.tensor_tensor(nrm[:, j % 4, :], nrm[:, j % 4, :],
                                        bc[:, T + tg * TGW:T + (tg + 1) * TGW],
                                        ALU.add)
                        inc(pr, e.tensor_scalar(
                            out_tile(dc, tg), nrm[:, j % 4, :],
                            affine_cols[0][:, dc:dc + 1],
                            affine_cols[1][:, dc:dc + 1],
                            ALU.mult, ALU.add), "ev")
                    if ev_post is not None:
                        ev_post(e, j)
                    j += 1
        pr.blk.vector(evnf)
        return ev_out_base

    gp0 = pr.cnt["gp"]
    def gpb(g):
        g.wait_ge(pr.s("ev"), bc_done)
        j = 0
        for dc in range(NDC):
            for tg in range(NTG):
                if j >= 4:
                    g.wait_ge(pr.s("ev"), ev_nrm_base + (j - 3))
                inc(pr, g.tensor_tensor(nrm[:, j % 4, :],
                                        x_dc(dc)[:, tg * TGW:(tg + 1) * TGW],
                                        bc[:, tg * TGW:(tg + 1) * TGW], ALU.mult), "gp")
                j += 1
                if gp_post is not None:
                    gp_post(g, j - 1, ev_nrm_base)
        if gp_post is not None:
            gp_post(g, NDC * NTG, ev_nrm_base)
    ev_nrm_base = pr.cnt["ev"]
    pr.blk.gpsimd(gpb)

    if affine_cols is None:
        ev_out_base = pr.cnt["ev"]
        def evn(e):
            j = 0
            for dc in range(NDC):
                for tg in range(NTG):
                    e.wait_ge(pr.s("gp"), gp0 + j + 1)
                    if out_pre is not None:
                        out_pre(e, j)
                    inc(pr, e.tensor_tensor(out_tile(dc, tg), nrm[:, j % 4, :],
                                            bc[:, T + tg * TGW:T + (tg + 1) * TGW],
                                            ALU.add), "ev")
                    if ev_post is not None:
                        ev_post(e, j)
                    j += 1
        pr.blk.vector(evn)
        return ev_out_base
    else:
        gcol, bcol = affine_cols
        nrm2 = al.t(f"lnnrm2_{tag}", [128, 4, TGW], F32)
        ev_out_base = pr.cnt["ev"]
        def evn(e):
            j = 0
            for dc in range(NDC):
                for tg in range(NTG):
                    e.wait_ge(pr.s("gp"), gp0 + j + 1)
                    e.tensor_tensor(nrm2[:, j % 4, :], nrm[:, j % 4, :],
                                    bc[:, T + tg * TGW:T + (tg + 1) * TGW],
                                    ALU.add)
                    inc(pr, e.tensor_scalar(out_tile(dc, tg),
                                            nrm2[:, j % 4, :],
                                            gcol[:, dc:dc + 1],
                                            bcol[:, dc:dc + 1],
                                            ALU.mult, ALU.add), "ev")
                    j += 1
        pr.blk.vector(evn)
        return ev_out_base


def emit_proj_fm(pr, al, banks, *, rhs_tile, wT, n_in, n_out, ev_fn,
                 ev_eng="vector", slab_cols=512, tag="", tgs=None,
                 ev_sem="ev"):
    if tgs is None:
        tgs = list(range(NTG))
    pr.s("mm"); pr.s(ev_sem)
    ndc = n_in // 128
    wre = wT.rearrange("(dc p) o -> p dc o", p=128)
    nslab = ceil_div(n_out, slab_cols)
    wslab = al.t(f"wsl_{ndc}x{slab_cols}", [128, 2, ndc, slab_cols], BF16)
    pr.s("wdma0"); pr.s("wdma1")
    base_w = [pr.cnt["wdma0"], pr.cnt["wdma1"]]
    base_mm = pr.cnt["mm"]
    slabs = []
    for s in range(nslab):
        c0 = s * slab_cols
        cw = min(slab_cols, n_out - c0)
        slabs.append((s, c0, cw, ceil_div(cw, 128)))
    jobs_before = [0]
    for (s, c0, cw, nrt) in slabs:
        jobs_before.append(jobs_before[-1] + nrt * len(tgs))
    items = []
    for (s, c0, cw, nrt) in slabs:
        if s >= 2:
            waits = [("mm", base_mm + jobs_before[s - 1])]
        else:
            waits = [("mm", base_mm)]
        items.append((wslab[:, s % 2, :, 0:cw], wre[:, :, c0:c0 + cw], waits))
    dma_seq(pr, "gpsimd", items, "wdma", nsem=2)
    jobs = []
    for (s, c0, cw, nrt) in slabs:
        for rti in range(nrt):
            r0 = c0 + rti * 128
            rw = min(128, n_out - r0)
            rt = r0 // 128
            for tg in tgs:
                def mk_mm(s=s, rti=rti, rw=rw, tg=tg):
                    def mmf(t, ps):
                        t.wait_ge(pr.s(f"wdma{s % 2}"),
                                  base_w[s % 2] + 16 * (s // 2 + 1))
                        last = None
                        for dc in range(ndc):
                            last = t.matmul(
                                ps[0:rw, 0:TGW],
                                wslab[:, s % 2, dc, rti * 128:rti * 128 + rw],
                                rhs_tile(dc, tg),
                                start=(dc == 0), stop=(dc == ndc - 1))
                        return last
                    return mmf
                def mk_ev(rt=rt, rw=rw, tg=tg):
                    return lambda e, ps: ev_fn(e, rt, rw, tg, ps)
                jobs.append({'mm': mk_mm(), 'ev': mk_ev()})
    return pipe(pr, jobs, banks, ev_eng=ev_eng, ev=ev_sem)


def ring_out(pr, al, *, n_jobs, tile_shape, dtype, dst_fn, ev_base,
             dsem, nring=4, ring_name=None, ev_sem="ev"):
    if ring_name is None:
        ring_name = f"ring_{'x'.join(map(str, tile_shape))}_{dtype}"
    ring = al.t(ring_name, [128, nring] + tile_shape, dtype)
    items = []
    for j in range(n_jobs):
        items.append((dst_fn(j), ring[:, j % nring],
                      [(ev_sem, ev_base + j + 1)]))
    names = [f"{dsem}{i}" for i in range(nring)]
    for n in names:
        pr.s(n)
    base_d = [pr.cnt[n] for n in names]
    dma_seq(pr, "sync", items, dsem, nsem=nring)

    def pre_ev(j):
        if j < nring:
            return [(names[i], base_d[i]) for i in range(nring)
                    if base_d[i] > 0]
        return [(names[j % nring], base_d[j % nring] + 16 * (j // nring))]
    return ring, pre_ev



def stage1(pr, al, banks, io):
    cstv = load_consts(pr, al, io)
    xstage = al.t("xstage", [128, NDC * T], F32)
    xln = al.t("xln", [128, NDC * T], BF16)
    x0 = io("x0")
    dma_seq(pr, "sync", [(xstage[:, dc * T:(dc + 1) * T], x0[dc], [])
                         for dc in range(NDC)], "xin")
    xin_done = pr.cnt["xin"]
    def w0(a):
        a.wait_ge(pr.s("xin"), xin_done)
    pr.blk.scalar(w0)
    pr.blk.vector(lambda e: e.wait_ge(pr.s("xin"), xin_done))
    emit_ln(pr, al, banks, x_sb=xstage,
            out_tile=lambda dc, tg: xln[:, dc * T + tg * TGW: dc * T + (tg + 1) * TGW],
            tag="ln1", cst_wait=cstv, pass1_eng="vector")

    def xt(dc, tg):
        return xln[:, dc * T + tg * TGW: dc * T + (tg + 1) * TGW]

    bq = al.t("bq_cols", [128, NDC], F32)
    bk = al.t("bk_cols", [128, NDC], F32)
    bvr = al.t("bv_row", [1, D], F32)
    dma_seq(pr, "sync", [(bq[:, :], io("bq_cols"), []),
                         (bk[:, :], io("bk_cols"), []),
                         (bvr[0:1, :], io("bv_row"), [])], "cstb")
    cst2 = pr.cnt["cstb"]

    ln_done = pr.cnt["ev"]
    pr.blk.tensor(lambda t: t.wait_ge(pr.s("ev"), ln_done))

    def do_proj(nm, bias, wname):
        dst = io(nm + "_d")
        njobs = NDC * NTG
        ev_base = pr.cnt["ev"]
        ring, pre_ev = ring_out(pr, al, n_jobs=njobs,
                                tile_shape=[TGW], dtype=BF16,
                                dst_fn=lambda j, dst=dst: dst[j // NTG, :,
                                                     (j % NTG) * TGW:(j % NTG + 1) * TGW],
                                ev_base=ev_base, dsem="dout")
        jcount = [0]
        def ev_fn(e, rt, rw, tg, ps, ring=ring, pre_ev=pre_ev, bias=bias, jc=jcount):
            j = jc[0]; jc[0] += 1
            for (sn, v) in pre_ev(j):
                e.wait_ge(pr.s(sn), v)
            return e.tensor_scalar_add(ring[:, j % 4], ps[0:128, :],
                                       bias[:, rt:rt + 1])
        def mkpre(a, cst2=cst2):
            a.wait_ge(pr.s("cstb"), cst2)
        pr.blk.vector(mkpre)
        emit_proj_fm(pr, al, banks[:4], rhs_tile=xt, wT=io(wname),
                     n_in=D, n_out=D, ev_fn=ev_fn, ev_eng="vector", tag=nm)

    do_proj("k0", bk, "wk0T")
    kv_snap = {"k": [pr.cnt.get(f"dout{i}", 0) for i in range(4)]}

    bvbc = al.t("bvbc", [128, D], F32)
    def tbv(t):
        t.wait_ge(pr.s("cstb"), cst2)
        t.matmul(banks[4][0:128, 0:TGW], al.ones_f32[0:1, :], bvr[0:1, 0:TGW],
                 start=True, stop=True)
        inc(pr, t.matmul(banks[5][0:128, 0:TGW], al.ones_f32[0:1, :],
                         bvr[0:1, TGW:2 * TGW], start=True, stop=True), "mm")
    pr.blk.tensor(tbv)
    mmv = pr.cnt["mm"]
    def evbv(e):
        e.wait_ge(pr.s("mm"), mmv)
        e.tensor_copy(bvbc[:, 0:TGW], banks[4][:, 0:TGW])
        inc(pr, e.tensor_copy(bvbc[:, TGW:2 * TGW], banks[5][:, 0:TGW]), "ev")
    pr.blk.vector(evbv)

    vdst = io("v0_d")
    wvre = io("wv0T").rearrange("(dc p) o -> p dc o", p=128)
    wslab = al.t(f"wsl_{NDC}x{TGW}", [128, 2, NDC, TGW], BF16)
    pr.s("wdma0"); pr.s("wdma1")
    base_w = [pr.cnt["wdma0"], pr.cnt["wdma1"]]
    base_mm = pr.cnt["mm"]
    items = [(wslab[:, s % 2, :, :], wvre[:, :, s * TGW:(s + 1) * TGW],
              [("mm", base_mm)]) for s in range(2)]
    dma_seq(pr, "gpsimd", items, "wdma", nsem=2)
    njobs = 2 * 16
    ev_base = pr.cnt["ev"]
    ring, pre_ev = ring_out(pr, al, n_jobs=njobs,
                            tile_shape=[TGW], dtype=BF16,
                            dst_fn=lambda j: vdst[j % 16, :,
                                                  (j // 16) * TGW:(j // 16 + 1) * TGW],
                            ev_base=ev_base, dsem="dout")
    jobs = []
    for s in range(2):
        for tt in range(16):
            j = s * 16 + tt
            def mk_mm(s=s, tt=tt):
                def mmf(t, ps):
                    t.wait_ge(pr.s(f"wdma{s % 2}"), base_w[s % 2] + 16)
                    last = None
                    for dc in range(NDC):
                        last = t.matmul(
                            ps[0:128, 0:TGW],
                            xln[:, dc * T + tt * 128: dc * T + (tt + 1) * 128],
                            wslab[:, s % 2, dc, :],
                            start=(dc == 0), stop=(dc == NDC - 1))
                    return last
                return mmf
            def mk_ev(s=s, tt=tt, j=j):
                def evf(e, ps):
                    for (sn, v) in pre_ev(j):
                        e.wait_ge(pr.s(sn), v)
                    return e.tensor_tensor(ring[:, j % 4], ps[0:128, :],
                                           bvbc[:, s * TGW:(s + 1) * TGW], ALU.add)
                return evf
            jobs.append({'mm': mk_mm(), 'ev': mk_ev()})
    pipe(pr, jobs, banks[:4])
    kv_snap["v"] = [pr.cnt.get(f"dout{i}", 0) for i in range(4)]
    do_proj("q0", bq, "wq0T")
    pr.s("ccs")
    def ccb(g):
        for i in range(4):
            if kv_snap["k"][i] > 0:
                g.wait_ge(pr.s(f"dout{i}"), kv_snap["k"][i])
        inc(pr, g.collective_compute(
            "AllGather", mybir.AluOpType.bypass, replica_groups=CC_GROUPS,
            ins=[io("k0_d__2d").opt()], outs=[io("k0cc").opt()]), "ccs")
        for i in range(4):
            if kv_snap["v"][i] > kv_snap["k"][i]:
                g.wait_ge(pr.s(f"dout{i}"), kv_snap["v"][i])
        inc(pr, g.collective_compute(
            "AllGather", mybir.AluOpType.bypass, replica_groups=CC_GROUPS,
            ins=[io("v0_d__2d").opt()], outs=[io("v0cc").opt()]), "ccs")
    pr.blk.gpsimd(ccb)



def _fmT(x):
    return np.ascontiguousarray(x.T.reshape(NDC, 128, x.shape[0]))


def _cols(v):
    return np.ascontiguousarray(v.reshape(NDC, 128).T)



def _l0_attn_consts(h):
    lg = np.float32(np.log(np.float32(0.95)))
    jls, slots = l0_jlists()
    gcol = np.zeros((128, NTG * 32), np.float32)
    tri = np.ones((NTG, 8, 128, 512), NPBF)
    for ig in range(NTG):
        i0 = h * T + TGW * ig
        for jc in range(32):
            delta = i0 - 128 * jc
            if delta <= -TGW:
                continue
            expo = (np.float32(i0) - (128 * jc + np.arange(128, dtype=np.float32)))
            gcol[:, ig * 32 + jc] = np.exp(expo * lg).astype(np.float32)
        for s, jc in enumerate(slots[ig]):
            delta = i0 - 128 * jc
            if -TGW < delta < 128:
                jj = np.arange(128)[:, None]
                ii = np.arange(512)[None, :]
                tri[ig, s] = (ii + delta >= jj).astype(NPBF)
    gii = np.broadcast_to(np.exp(np.arange(512, dtype=np.float32) * lg)[None, :],
                          (128, 512)).astype(np.float32)
    return gcol, np.ascontiguousarray(gii), tri


def _interleave_gu(g, u):
    out = np.empty((1024, 2816), np.float32)
    for k in range(11):
        out[:, 256 * k:256 * k + 128] = g[128 * k:128 * (k + 1)].T
        out[:, 256 * k + 128:256 * (k + 1)] = u[128 * k:128 * (k + 1)].T
    return out


def host_prepare3(ip, shared):
    gu = ip["L0_ffn_experts_gate_up"]
    dn = ip["L0_ffn_experts_down"]
    sg = ip["L0_ffn_shared_gate_w"]
    su = ip["L0_ffn_shared_up_w"]
    sd = ip["L0_ffn_shared_down_w"]
    guT = np.empty((10, 1024, 2816), np.float32)
    gdT = np.empty((10, 1408, 1024), np.float32)
    for e in range(8):
        guT[e] = _interleave_gu(gu[e, :1408], gu[e, 1408:])
        gdT[e] = np.ascontiguousarray(dn[e].T)
    guT[8] = _interleave_gu(sg[:1408], su[:1408])
    guT[9] = _interleave_gu(sg[1408:], su[1408:])
    gdT[8] = np.ascontiguousarray(sd[:, :1408].T)
    gdT[9] = np.ascontiguousarray(sd[:, 1408:].T)
    shared["guT"] = guT
    shared["gdT"] = gdT
    shared["gwsT"] = np.ascontiguousarray(
        np.concatenate([ip["L0_ffn_gate_w"].T,
                        ip["L0_ffn_shared_expert_gate_w"].T], axis=1))
    ew = [0, 1, 2, 3, 4, 5, 6, 7, 8, 8]
    sel = np.zeros((16, 10, 128), np.float32)
    for e in range(10):
        sel[ew[e], e, :] = 1.0
    shared["c_sel"] = sel
    shared["c_ident"] = np.eye(128, dtype=np.float32)


def _eo_perm():
    perm = np.empty(1024, np.int64)
    for h in range(16):
        for m in range(32):
            perm[h * 32 + m] = h * 64 + 2 * m
            perm[512 + h * 32 + m] = h * 64 + 2 * m + 1
    return perm


def _rope1_tables(h):
    m = (np.arange(128) % 32).astype(np.float32)
    inv_freq = (1.0 / (10000.0 ** (2.0 * m / 64.0))).astype(np.float32)
    t = (h * T + np.arange(T)).astype(np.float32)
    freqs = (t[None, :] * inv_freq[:, None]).astype(np.float32)
    sv = ((2.0 * m + 0.4 * 64.0) / (1.4 * 64.0)).astype(np.float32)
    power = ((t - S_FULL // 2) / 512.0).astype(np.float32)
    scale = (sv[:, None] ** power[None, :]).astype(np.float32)
    cos = np.cos(freqs.astype(np.float64)).astype(np.float32)
    sin = np.sin(freqs.astype(np.float64)).astype(np.float32)
    out = np.empty((128, 4, T), NPBF)
    out[:, 0] = (cos * scale).astype(NPBF)
    out[:, 1] = (sin * scale).astype(NPBF)
    out[:, 2] = (cos / scale).astype(NPBF)
    out[:, 3] = (sin / scale).astype(NPBF)
    return out


def host_prepare5(ip, shared, per_core):
    wqkv = ip["L1_attn_w_qkv"]
    perm = _eo_perm()
    shared["wq1T"] = np.ascontiguousarray(wqkv[0:1024][perm].T)
    shared["wk1T"] = np.ascontiguousarray(wqkv[1024:2048][perm].T)
    shared["wv1T"] = np.ascontiguousarray(wqkv[2048:3072].T)
    shared["wg1T"] = np.ascontiguousarray(ip["L1_attn_w_gated"].T)
    for c in range(NCORE):
        per_core[c]["rt1"] = _rope1_tables(c % 2)


def _l1_attn_consts(h):
    gammas = (1.0 - 2.0 ** (-5.0 - np.arange(16, dtype=np.float32))).astype(np.float32)
    jls, slots = l1_jlists()
    gcol = np.zeros((128, 16 * NTG * 32), np.float32)
    tri = np.ones((NTG, 8, 128, 512), NPBF)
    for hd in range(16):
        lg = np.float32(np.log(gammas[hd]))
        for ig in range(NTG):
            i0 = h * T + TGW * ig
            for jc in range(32):
                delta = i0 - 128 * jc
                if delta <= -TGW:
                    continue
                expo = (np.float32(i0) - (128 * jc + np.arange(128, dtype=np.float32)))
                with np.errstate(under="ignore"):
                    gcol[:, (hd * NTG + ig) * 32 + jc] =                         np.exp(expo * lg).astype(np.float32)
    for ig in range(NTG):
        i0 = h * T + TGW * ig
        for s, jc in enumerate(slots[ig]):
            delta = i0 - 128 * jc
            if -TGW < delta < 128:
                jj = np.arange(128)[:, None]
                ii = np.arange(512)[None, :]
                tri[ig, s] = (ii + delta >= jj).astype(NPBF)
    gii = np.zeros((128, 8 * TGW), np.float32)
    for hp in range(8):
        for half in range(2):
            g = gammas[2 * hp + half]
            vals = (np.exp(np.arange(512, dtype=np.float32) * np.float32(np.log(g)))
                    * np.float32(64 ** -0.5))
            gii[half * 64:(half + 1) * 64, hp * TGW:(hp + 1) * TGW] = vals[None, :]
    return gcol, gii, tri


def host_prepare67(ip, shared, per_core):
    shared["g1_cols"] = _cols(ip["L1_attn_gn_weight"])
    shared["b1_cols"] = _cols(ip["L1_attn_gn_bias"])
    shared["wp1T"] = np.ascontiguousarray(ip["L1_attn_w_proj"].T)
    for c in range(NCORE):
        gcol, gii, tri = _l1_attn_consts(c % 2)
        per_core[c]["gcol1"] = gcol
        per_core[c]["gii1"] = gii
        per_core[c]["tri1"] = tri


def host_prepare2(ip, shared, per_core):
    shared["wo0T"] = np.ascontiguousarray(ip["L0_attn_Wo"])
    shared["bo_cols"] = _cols(ip["L0_attn_bo"])
    hpo = np.zeros((128, 2), NPBF)
    hpo[0:64, 0] = 1
    hpo[64:128, 1] = 1
    shared["c_hp_ones"] = hpo
    hpoT = np.zeros((2, 128), np.float32)
    hpoT[0, 0:64] = 1
    hpoT[1, 64:128] = 1
    shared["c_hp_onesT"] = hpoT
    shared["g0_cols"] = _cols(ip["L0_attn_gamma"])
    shared["b0_cols"] = _cols(ip["L0_attn_beta"])
    for c in range(NCORE):
        h = c % 2
        gcol, gii, tri = _l0_attn_consts(h)
        per_core[c]["gcol0"] = gcol
        per_core[c]["gii0"] = gii
        per_core[c]["tri0"] = tri
        per_core[c].update(shared)


def exchange_kv(env, kname, vname, kfull, vfull):
    for c in range(NCORE):
        p = c ^ 1
        a, b = (c, p) if c % 2 == 0 else (p, c)
        env[c][kfull] = np.ascontiguousarray(
            np.stack([env[a][kname], env[b][kname]], axis=0))
        vtok = np.concatenate([env[a][vname], env[b][vname]], axis=0)
        env[c][vfull] = np.ascontiguousarray(vtok.transpose(1, 0, 2))


def _fperm():
    perm = np.empty(512, np.int64)
    for rt in range(4):
        half, pp = rt // 2, rt % 2
        for g in range(4):
            for m in range(32):
                perm[rt * 128 + 32 * g + m] = (4 * pp + g) * 64 + 32 * half + m
    return perm


def _ffn_rope_tables(h):
    m = (np.arange(128) % 32).astype(np.float32)
    inv_freq = (1.0 / (10000.0 ** (2.0 * m / 64.0))).astype(np.float32)
    t = (h * T + np.arange(T)).astype(np.float32)
    freqs = t[None, :] * inv_freq[:, None]
    base = ((2.0 * m + 0.4 * 64.0) / (1.4 * 64.0)).astype(np.float32)
    power = ((t - S_FULL // 2) / 256.0).astype(np.float32)
    with np.errstate(under="ignore"):
        scale = (base[:, None] ** power[None, :]).astype(np.float32)
    cos = np.cos(freqs.astype(np.float64)).astype(np.float32)
    sin = np.sin(freqs.astype(np.float64)).astype(np.float32)
    out = np.empty((128, 4, T), NPBF)
    out[:, 0] = (cos * scale).astype(NPBF)
    out[:, 1] = (sin * scale).astype(NPBF)
    out[:, 2] = (cos / scale).astype(NPBF)
    out[:, 3] = (sin / scale).astype(NPBF)
    return out


def _ffn_masks(h):
    msk = np.zeros((2, 8, 128, TGW), np.float32)
    jj = np.arange(128)[:, None]
    ii = np.arange(TGW)[None, :]
    for wcl in range(2):
        for jc in range(8):
            if jc < 4:
                vis = (128 * jc + jj) >= ii
                if wcl == 0 and h == 0:
                    vis = np.zeros_like(vis)
            else:
                vis = (128 * (jc - 4) + jj) <= ii
            msk[wcl, jc] = np.where(vis, 0.0, -1e30)
    return msk.astype(NPBF)


def _ffn_w1(w1, b1):
    A = np.zeros((1024, 2816), np.float32); A[:, :2730] = w1[:, :2730]
    G = np.zeros((1024, 2816), np.float32); G[:, :2730] = w1[:, 2730:]
    ba = np.zeros(2816, np.float32); ba[:2730] = b1[:2730]
    bg = np.zeros(2816, np.float32); bg[:2730] = b1[2730:]
    out = np.empty((2, 1024, 2816), np.float32)
    bc = np.empty((128, 2, 22), np.float32)
    for e in range(2):
        for k in range(11):
            col = e * 1408 + k * 128
            out[e][:, 256 * k:256 * k + 128] = A[:, col:col + 128]
            out[e][:, 256 * k + 128:256 * k + 256] = G[:, col:col + 128]
            bc[:, e, 2 * k] = ba[col:col + 128]
            bc[:, e, 2 * k + 1] = bg[col:col + 128]
    return out, bc


def host_prepare_ffn(ip, shared, per_core):
    perm = _fperm()
    for l in range(2):
        wqkv = ip["L1_ffn_wqkv"][l]
        shared[f"wfq{l}T"] = np.ascontiguousarray(wqkv[:, 0:512][:, perm])
        shared[f"wfk{l}T"] = np.ascontiguousarray(wqkv[:, 512:1024][:, perm])
        shared[f"wfv{l}T"] = np.ascontiguousarray(wqkv[:, 1024:1536])
        shared[f"wfo{l}T"] = np.ascontiguousarray(ip["L1_ffn_wo"][l])
        qs = ip["L1_ffn_q_scale"][l].astype(np.float32)
        ks = ip["L1_ffn_k_scale"][l].astype(np.float32)
        m = np.arange(128) % 32
        qc = np.zeros((128, 8), np.float32)
        qc[:, 0] = qs[m]; qc[:, 1] = qs[m]
        qc[:, 2] = qs[32 + m]; qc[:, 3] = qs[32 + m]
        qc[:, 4] = ks[m]; qc[:, 5] = ks[m]
        qc[:, 6] = ks[32 + m]; qc[:, 7] = ks[32 + m]
        shared[f"qksc{l}"] = qc
        qkr = np.zeros((1, 128), np.float32)
        qkr[0, 0:64] = qs; qkr[0, 64:128] = ks
        shared[f"qkr{l}"] = qkr
        w1T, b1c = _ffn_w1(ip["L1_ffn_w1"][l], ip["L1_ffn_b1"][l])
        shared[f"fw1T{l}"] = w1T
        shared[f"fb1c{l}"] = b1c
        w2p = np.zeros((2816, 1024), np.float32)
        w2p[:2730] = ip["L1_ffn_w2"][l]
        shared[f"fw2T{l}"] = np.ascontiguousarray(w2p.reshape(2, 1408, 1024))
        shared[f"fb2c{l}"] = _cols(ip["L1_ffn_b2"][l])
        shared[f"flnc{l}"] = np.concatenate(
            [_cols(ip["L1_ffn_ln_g"][l]), _cols(ip["L1_ffn_ln_b"][l])], 1)
    shared["c_ident_bf"] = np.eye(128, dtype=NPBF)
    hb4 = np.zeros((128, 4), NPBF)
    hb4T = np.zeros((4, 128), np.float32)
    for g in range(4):
        hb4[32 * g:32 * g + 32, g] = 1
        hb4T[g, 32 * g:32 * g + 32] = 1
    shared["c_hb4"] = hb4
    shared["c_hb4T"] = hb4T
    for c in range(NCORE):
        per_core[c]["rtf"] = _ffn_rope_tables(c % 2)
        per_core[c]["mskf"] = _ffn_masks(c % 2)


def host_prepare(inputs):
    ip = {k: np.asarray(v) for k, v in inputs.items()}
    shared = {}
    shared["c_ones_bf"] = np.ones((128, 1), NPBF)
    shared["c_ones_f32"] = np.ones((1, 128), np.float32)
    shared["wq0T"] = np.ascontiguousarray(ip["L0_attn_Wq"])
    shared["wk0T"] = np.ascontiguousarray(ip["L0_attn_Wk"])
    shared["wv0T"] = np.ascontiguousarray(ip["L0_attn_Wv"])
    shared["bq_cols"] = _cols(ip["L0_attn_bq"])
    shared["bk_cols"] = _cols(ip["L0_attn_bk"])
    shared["bv_row"] = ip["L0_attn_bv"][None, :].astype(np.float32)
    per_core = []
    x = ip["L0_attn_x"]
    for c in range(NCORE):
        b, h = c // 2, c % 2
        d = dict(shared)
        d["x0"] = _fmT(x[b, h * T:(h + 1) * T])
        d["c_isodd"] = np.full((128, 1), float(h), np.float32)
        per_core.append(d)
    host_prepare3(ip, shared)
    host_prepare2(ip, shared, per_core)
    host_prepare5(ip, shared, per_core)
    host_prepare67(ip, shared, per_core)
    host_prepare_ffn(ip, shared, per_core)
    for nm in ("wq0T", "wk0T", "wv0T", "wo0T", "wq1T", "wk1T", "wv1T",
               "wg1T", "wp1T", "guT", "gdT",
               "wfq0T", "wfk0T", "wfv0T", "wfo0T", "fw1T0", "fw2T0",
               "wfq1T", "wfk1T", "wfv1T", "wfo1T", "fw1T1", "fw2T1"):
        shared[nm] = np.asarray(shared[nm]).astype(NPBF)
    for c in range(NCORE):
        per_core[c].update(shared)
    return per_core



BUFS = {
    "x0": ([NDC, 128, T], np.float32),
    "c_ones_bf": ([128, 1], NPBF),
    "c_ones_f32": ([1, 128], np.float32),
    "wq0T": ([D, D], NPBF),
    "wk0T": ([D, D], NPBF),
    "wv0T": ([D, D], NPBF),
    "bq_cols": ([128, NDC], np.float32),
    "bk_cols": ([128, NDC], np.float32),
    "bv_row": ([1, D], np.float32),
    "q0_d": ([NDC, 128, T], NPBF),
    "k0_d": ([NDC, 128, T], NPBF),
    "v0_d": ([16, 128, D], NPBF),
    "k_full": ([2, NDC, 128, T], NPBF),
    "v_full": ([128, S_FULL // 128, D], NPBF),
    "gcol0": ([128, 4 * 32], np.float32),
    "gii0": ([128, 512], np.float32),
    "tri0": ([4, 8, 128, 512], NPBF),
    "c_hp_ones": ([128, 2], NPBF),
    "wo0T": ([D, D], NPBF),
    "bo_cols": ([128, NDC], np.float32),
    "o0_d": ([NDC, 128, T], np.float32),
    "st0_d": ([2, 16], np.float32),
    "st0_p": ([2, 16], np.float32),
    "c_hp_onesT": ([2, 128], np.float32),
    "g0_cols": ([128, NDC], np.float32),
    "b0_cols": ([128, NDC], np.float32),
    "x1_d": ([NDC, 128, T], np.float32),
    "xln2_d": ([NDC, 128, T], NPBF),
    "x2_d": ([NDC, 128, T], np.float32),
    "guT": ([10, D, 2816], NPBF),
    "gdT": ([10, 1408, D], NPBF),
    "gwsT": ([D, 9], np.float32),
    "c_sel": ([16, 10, 128], np.float32),
    "c_ident": ([128, 128], np.float32),
    "wq1T": ([D, D], NPBF),
    "wk1T": ([D, D], NPBF),
    "wv1T": ([D, D], NPBF),
    "wg1T": ([D, D], NPBF),
    "rt1": ([128, 4, T], NPBF),
    "q1_d": ([NDC, 128, T], NPBF),
    "k1_d": ([NDC, 128, T], NPBF),
    "v1_d": ([16, 128, D], NPBF),
    "gated_d": ([NDC, 128, T], NPBF),
    "k1_full": ([2, NDC, 128, T], NPBF),
    "v1_full": ([128, S_FULL // 128, D], NPBF),
    "gcol1": ([128, 16 * NTG * 32], np.float32),
    "gii1": ([128, 8 * TGW], np.float32),
    "tri1": ([4, 8, 128, 512], NPBF),
    "y1_d": ([NDC, 128, T], np.float32),
    "st1_d": ([2, 16], np.float32),
    "st1_p": ([2, 16], np.float32),
    "g1_cols": ([128, NDC], np.float32),
    "b1_cols": ([128, NDC], np.float32),
    "wp1T": ([D, D], NPBF),
    "x3_d": ([NDC, 128, T], np.float32),
    "z_d": ([NDC, 128, T], np.float32),
    "zbf_d": ([NDC, 128, T], NPBF),
    "c_ident_bf": ([128, 128], NPBF),
    "c_hb4": ([128, 4], NPBF),
    "c_hb4T": ([4, 128], np.float32),
    "rtf": ([128, 4, T], NPBF),
    "mskf": ([2, 8, 128, TGW], NPBF),
    "out_d": ([NDC, 128, T], np.float32),
    "z3_0d": ([NDC, 128, T], np.float32),
    "z3bf_0d": ([NDC, 128, T], NPBF),
    "c_isodd": ([128, 1], np.float32),
    "zlast": ([128, NDC], np.float32),
    "z2last": ([128, NDC], np.float32),
    "z3last": ([128, NDC], np.float32),
}
for _l in range(2):
    BUFS.update({
        f"wfq{_l}T": ([D, 512], NPBF),
        f"wfk{_l}T": ([D, 512], NPBF),
        f"wfv{_l}T": ([D, 512], NPBF),
        f"wfo{_l}T": ([512, D], NPBF),
        f"qksc{_l}": ([128, 8], np.float32),
        f"qkr{_l}": ([1, 128], np.float32),
        f"fw1T{_l}": ([2, D, 2816], NPBF),
        f"fw2T{_l}": ([2, 1408, D], NPBF),
        f"fb1c{_l}": ([128, 2, 22], np.float32),
        f"fb2c{_l}": ([128, NDC], np.float32),
        f"flnc{_l}": ([128, 2 * NDC], np.float32),
        f"qf{_l}_d": ([4, 128, T], NPBF),
        f"kf{_l}_d": ([4, 128, T], NPBF),
        f"vf{_l}_d": ([16, 128, 520], NPBF),
        f"kb{_l}_x": ([128, 4, TGW], NPBF),
        f"vb{_l}_x": ([128, 4, 520], NPBF),
        f"zg8_{_l}": ([128, NDC], NPBF),
        f"zg32_{_l}": ([128, NDC], np.float32),
        f"z2_{_l}d": ([NDC, 128, T], np.float32),
        f"z2bf_{_l}d": ([NDC, 128, T], NPBF),
    })

MLDT = {np.float32: F32, NPBF: BF16, np.uint32: U32}


def run_stage(emit_fn, reads, writes, env_list, nexec=NCORE, rename=None):
    rename = rename or {}
    nc = bass.Bass("TRN2", target_bir_lowering=False)
    pr = Prog(nc)
    al = SbAl(nc)
    handles = {}
    for nm in reads:
        shape, dt = BUFS[nm]
        handles[nm] = nc.dram_tensor(nm, list(shape), MLDT[dt],
                                     kind="ExternalInput").ap()
    for nm in writes:
        shape, dt = BUFS[nm]
        handles[nm] = nc.dram_tensor(nm, list(shape), MLDT[dt],
                                     kind="ExternalOutput").ap()
    banks = [nc.alloc_psum_tensor(f"bank{i}", [128, 512], F32) for i in range(8)]
    with nc.Block() as blk:
        pr.blk = blk
        emit_fn(pr, al, banks, lambda nm: handles[rename.get(nm, nm)])
        def closing(s):
            for nm in list(pr.cnt):
                if (nm.startswith("dout") or nm.startswith("xout")
                        or nm.startswith("zo") or nm.startswith("zc")
                        or nm.startswith("rdo") or nm.startswith("fd")
                        or nm.startswith("vdo") or nm.startswith("dbf")
                        ) and pr.cnt[nm] > 0:
                    s.wait_ge(pr.s(nm), pr.cnt[nm])
        pr.blk.sync(closing)
    in_maps = []
    for c in range(NCORE):
        m = {}
        for nm in reads:
            shape, dt = BUFS[nm]
            a = np.asarray(env_list[c][nm])
            assert tuple(a.shape) == tuple(shape), (nm, a.shape, shape)
            m[nm] = a.astype(dt) if a.dtype != dt else a
        in_maps.append(m)
    res = run_bass_kernel_spmd(nc, in_maps, core_ids=list(range(NCORE)))
    return [r for r in res.results]



def _ghost_cols(env, src, dst, dt):
    for c in range(NCORE):
        if c % 2 == 0:
            env[c][dst] = np.zeros((128, NDC), dt)
        else:
            env[c][dst] = np.ascontiguousarray(
                np.asarray(env[c ^ 1][src])[:, :, T - 1].T).astype(dt)


def _kvb_exchange(env, l):
    for c in range(NCORE):
        p = c ^ 1
        env[c][f"kb{l}_x"] = np.ascontiguousarray(
            np.asarray(env[p][f"kf{l}_d"])[:, :, T - TGW:].transpose(1, 0, 2))
        env[c][f"vb{l}_x"] = np.ascontiguousarray(
            np.asarray(env[p][f"vf{l}_d"])[12:16].transpose(1, 0, 2))


def kernel_multilaunch(**inputs):
    env = [dict(p) for p in host_prepare(inputs)]

    def rs(fn, reads, writes, rename=None):
        outs = run_stage(fn, reads, writes, env, rename=rename)
        for c in range(NCORE):
            env[c].update(outs[c])

    CST = ["c_ones_bf", "c_ones_f32"]
    rs(stage1, ["x0", "wq0T", "wk0T", "wv0T", "bq_cols", "bk_cols",
                "bv_row"] + CST, ["q0_d", "k0_d", "v0_d"])
    exchange_kv(env, "k0_d", "v0_d", "k_full", "v_full")
    rs(stage2, ["q0_d", "k_full", "v_full", "c_hp_ones", "gcol0", "gii0",
                "tri0", "wo0T", "bo_cols"] + CST, ["o0_d", "st0_d"])
    for c in range(NCORE):
        env[c]["st0_p"] = env[c ^ 1]["st0_d"]
    rs(stage3, ["x0", "o0_d", "st0_d", "st0_p", "c_hp_onesT", "g0_cols",
                "b0_cols"] + CST, ["x1_d", "xln2_d"])
    rs(stage4, ["x1_d", "xln2_d", "guT", "gdT", "gwsT", "c_sel",
                "c_ident"] + CST, ["x2_d"])
    rs(stage5, ["x2_d", "wq1T", "wk1T", "wv1T", "wg1T", "rt1"] + CST,
       ["q1_d", "k1_d", "v1_d", "gated_d"])
    exchange_kv(env, "k1_d", "v1_d", "k1_full", "v1_full")
    rs(stage6, ["q1_d", "k1_full", "v1_full", "gcol1", "gii1", "tri1",
                "c_hp_ones"] + CST, ["y1_d", "st1_d"])
    for c in range(NCORE):
        env[c]["st1_p"] = env[c ^ 1]["st1_d"]
    rs(stage7, ["x2_d", "y1_d", "st1_d", "st1_p", "gated_d", "g1_cols",
                "b1_cols", "wp1T", "c_hp_onesT"] + CST,
       ["x3_d", "z_d", "zbf_d"])

    zf32, zbf = "z_d", "zbf_d"
    for l in range(2):
        _ghost_cols(env, zbf, f"zg8_{l}", NPBF)
        rs(lambda pr, al, banks, io, l=l: stage_ffn_qkv(pr, al, banks, io, l),
           [zbf, f"zg8_{l}", "rtf", f"qksc{l}", f"wfq{l}T", f"wfk{l}T",
            f"wfv{l}T", "c_hb4", "c_hb4T"] + CST,
           [f"qf{l}_d", f"kf{l}_d", f"vf{l}_d"],
           rename={"zin": zbf, "zg": f"zg8_{l}", "qksc": f"qksc{l}",
                   "wfqT": f"wfq{l}T", "wfkT": f"wfk{l}T",
                   "wfvT": f"wfv{l}T", "qf_d": f"qf{l}_d",
                   "kf_d": f"kf{l}_d", "vf_d": f"vf{l}_d"})
        _kvb_exchange(env, l)
        rs(stage_ffn_attn,
           [zf32, f"qf{l}_d", f"kf{l}_d", f"vf{l}_d", f"kb{l}_x",
            f"vb{l}_x", "mskf", "c_ident_bf", f"qkr{l}", f"wfo{l}T"] + CST,
           [f"z2_{l}d", f"z2bf_{l}d"],
           rename={"z32in": zf32, "qf_d": f"qf{l}_d", "kf_d": f"kf{l}_d",
                   "vf_d": f"vf{l}_d", "kb_x": f"kb{l}_x",
                   "vb_x": f"vb{l}_x", "qkr": f"qkr{l}",
                   "wfoT": f"wfo{l}T", "z2out": f"z2_{l}d",
                   "z2bfout": f"z2bf_{l}d"})
        _ghost_cols(env, f"z2_{l}d", f"zg32_{l}", np.float32)
        final = (l == 1)
        zout = "out_d" if final else "z3_0d"
        reads = [f"z2_{l}d", f"zg32_{l}", f"flnc{l}", f"fb1c{l}",
                 f"fb2c{l}", f"fw1T{l}", f"fw2T{l}"] + CST
        rename = {"z32in": f"z2_{l}d", "zg32": f"zg32_{l}",
                  "lncols": f"flnc{l}", "fb1cols": f"fb1c{l}",
                  "fb2cols": f"fb2c{l}", "fw1T": f"fw1T{l}",
                  "fw2T": f"fw2T{l}", "zout": zout,
                  "zbfout": "z3bf_0d"}
        writes = [zout] if final else [zout, "z3bf_0d"]
        if final:
            reads = reads + ["x3_d"]
        rs(lambda pr, al, banks, io, f=final: stage_ffn_ffn(pr, al, banks,
                                                            io, final=f),
           reads, writes, rename=rename)
        zf32, zbf = "z3_0d", "z3bf_0d"

    x = np.asarray(inputs["L0_attn_x"])
    out = np.empty(x.shape, np.float32)
    for c in range(NCORE):
        b, h = c // 2, c % 2
        out[b, h * T:(h + 1) * T] = env[c]["out_d"].reshape(D, T).T
    return out



CC_GROUPS = [[0, 1], [2, 3], [4, 5], [6, 7]]

EXTERNAL_INPUTS = [
    "x0", "wq0T", "wk0T", "wv0T", "bq_cols", "bk_cols", "bv_row",
    "c_ones_bf", "c_ones_f32", "c_hp_ones", "c_hp_onesT",
    "gcol0", "gii0", "tri0", "wo0T", "bo_cols", "g0_cols", "b0_cols",
    "guT", "gdT", "gwsT", "c_sel", "c_ident",
    "wq1T", "wk1T", "wv1T", "wg1T", "rt1",
    "gcol1", "gii1", "tri1", "g1_cols", "b1_cols", "wp1T",
    "rtf", "mskf", "c_ident_bf", "c_hb4", "c_hb4T", "c_isodd",
] + [f"{nm}{l}{sfx}" for l in range(2) for (nm, sfx) in
     (("wfq", "T"), ("wfk", "T"), ("wfv", "T"), ("wfo", "T"),
      ("qksc", ""), ("qkr", ""), ("fw1T", ""), ("fw2T", ""),
      ("fb1c", ""), ("fb2c", ""), ("flnc", ""))]

FUSED_INTERNALS = [
    "q0_d", "k0_d", "v0_d", "o0_d", "st0_d", "x1_d", "xln2_d", "x2_d",
    "q1_d", "k1_d", "v1_d", "gated_d", "y1_d", "st1_d",
    "x3_d", "z_d", "zbf_d", "z3_0d", "z3bf_0d",
] + [f"{nm}{l}{sfx}" for l in range(2) for (nm, sfx) in
     (("qf", "_d"), ("kf", "_d"), ("vf", "_d"), ("z2_", "d"),
      ("z2bf_", "d"), ("zg8_", ""), ("zg32_", ""))]


class RecAl(SbAl):

    def __init__(self, nc, rec):
        super().__init__(nc)
        self.rec = rec
        self.step = "*"
        self.base_of = {}

    def _charge(self, name):
        name = self.base_of.get(name, name)
        self.rec[name][3].add(self.step)

    def t(self, name, shape, dtype):
        if name in self._cache:
            self._charge(name)
            return self._cache[name]
        bpp = int(np.prod(shape[1:])) * (2 if dtype == BF16 else 4)
        r = self.rec.setdefault(name, [list(shape), dtype, bpp, set()])
        assert r[0] == list(shape) and r[1] == dtype, (name, shape, r)
        r[3].add(self.step)
        h = self.nc.alloc_sbuf_tensor_at("sb_" + name, shape, dtype, offset=0)
        self._cache[name] = h
        return h

    def overlay(self, name, shape, dtype, base_name, byte_off=0):
        if name in self._cache:
            self._charge(name)
            return self._cache[name]
        base_name = self.base_of.get(base_name, base_name)
        self.base_of[name] = base_name
        self.rec[base_name][3].add(self.step)
        return super().overlay(name, shape, dtype, base_name, byte_off)


def _place(rec, align=256):
    occ = {}

    def check_steps(ss):
        if "*" in ss:
            return set(occ) | {"*"}
        return set(ss) | {"*"}

    def rec_steps(ss):
        return {"*"} if "*" in ss else set(ss)

    def fits(off, size, chk):
        for st in chk:
            for (lo, hi) in occ.get(st, []):
                if off < hi and lo < off + size:
                    return False
        return True

    pos = {}
    total = 0
    for nm in sorted(rec, key=lambda n: (-rec[n][2], n)):
        shape, dt, bpp, ss = rec[nm]
        size = (bpp + align - 1) // align * align
        chk = check_steps(ss)
        cands = {0}
        for st in chk:
            for (lo, hi) in occ.get(st, []):
                cands.add((hi + align - 1) // align * align)
        off = min(o for o in sorted(cands) if fits(o, size, chk))
        pos[nm] = off
        for st in rec_steps(ss):
            occ.setdefault(st, []).append((off, off + size))
        total = max(total, off + size)
    return pos, total


def _mk_handles(nc, dbg=()):
    h = {}
    for nm in EXTERNAL_INPUTS:
        shape, dt = BUFS[nm]
        h[nm] = nc.dram_tensor(nm, list(shape), MLDT[dt],
                               kind="ExternalInput").ap()
    for nm in FUSED_INTERNALS:
        shape, dt = BUFS[nm]
        kind = "ExternalOutput" if nm in dbg else "Internal"
        h[nm] = nc.dram_tensor(nm, list(shape), MLDT[dt], kind=kind).ap()
    shape, dt = BUFS["out_d"]
    h["out_d"] = nc.dram_tensor("out_d", list(shape), MLDT[dt],
                                kind="ExternalOutput").ap()

    def internal(nm, shape, dt):
        kind = "ExternalOutput" if nm in dbg else "Internal"
        t = nc.dram_tensor(nm, shape, dt, kind=kind)
        h[nm] = t.ap()
        return t

    for l, knm, vnm in ((0, "k0", "v0"), (1, "k1", "v1")):
        kcc = internal(f"{knm}cc", [2 * NDC * 128, T], BF16)
        vcc = internal(f"{vnm}cc", [2 * 16 * 128, D], BF16)
        h[f"{knm}ccr"] = kcc.ap().rearrange("(r d p) t -> r d p t",
                                            r=2, d=NDC, p=128)
        h[f"{vnm}ccp"] = vcc.ap().rearrange("(t p) d -> p t d", p=128)
    for snm in ("st0", "st1"):
        scc = internal(f"{snm}cc", [4, 16], F32)
        h[f"{snm}cc_a"] = scc.ap()[0:2, :]
        h[f"{snm}cc_b"] = scc.ap()[2:4, :]
    for g in range(4):
        internal(f"zl{g}", [128, NDC], F32)
        internal(f"zgt{g}", [2 * 128, NDC], F32)
        internal(f"zcb{g}", [NDC * 128, 1], F32)
        internal(f"zgc{g}", [2 * NDC * 128, 1], F32)
    for l in range(2):
        internal(f"kbb{l}", [4 * 128, TGW], BF16)
        kbcc = internal(f"kbcc{l}", [2 * 4 * 128, TGW], BF16)
        vbcc = internal(f"vbcc{l}", [2 * 4 * 128, 520], BF16)
        h[f"kb{l}_x"] = kbcc.ap()[0:4 * 128, :].rearrange(
            "(c p) t -> p c t", c=4, p=128)
        h[f"vb{l}_x"] = vbcc.ap()[0:4 * 128, :].rearrange(
            "(c p) f -> p c f", c=4, p=128)
    h["k0_d__2d"] = h["k0_d"].rearrange("a p t -> (a p) t")
    h["k1_d__2d"] = h["k1_d"].rearrange("a p t -> (a p) t")
    h["v0_d__2d"] = h["v0_d"].rearrange("a p t -> (a p) t")
    h["v1_d__2d"] = h["v1_d"].rearrange("a p t -> (a p) t")
    return h


def _boundary(pr, al, io):
    stamp = al.t("b_stamp", [1, 1], F32)
    pr.s("stg")
    waits = [(nm, pr.cnt[nm]) for nm in sorted(pr.cnt)
             if pr.cnt[nm] > 0 and nm != "stg"]

    def sb(s):
        for nm, v in waits:
            s.wait_ge(pr.s(nm), v)
        s.dma_start(out=stamp[0:1, 0:1],
                    in_=io("c_ones_f32")[0:1, 0:1]).then_inc(pr.s("stg"), 16)
    pr.blk.sync(sb)
    pr.cnt["stg"] += 16
    tgt = pr.cnt["stg"]
    for eng in ("tensor", "vector", "scalar", "gpsimd"):
        getattr(pr.blk, eng)(lambda e, tgt=tgt: e.wait_ge(pr.s("stg"), tgt))


def _final_wait(pr, al, io):
    waits = [(nm, pr.cnt[nm]) for nm in sorted(pr.cnt) if pr.cnt[nm] > 0]

    def sb(s):
        for nm, v in waits:
            s.wait_ge(pr.s(nm), v)
    pr.blk.sync(sb)


def cc_step(pairs):
    def emit(pr, al, banks, io):
        pr.s("ccs")

        def gb(g):
            for (src, dst) in pairs:
                inc(pr, g.collective_compute(
                    "AllGather", mybir.AluOpType.bypass,
                    replica_groups=CC_GROUPS,
                    ins=[io(src).opt()], outs=[io(dst).opt()]), "ccs")
        pr.blk.gpsimd(gb)
    return emit


GHOST_OLD = False
ZL_WRITES = True
MOE_OLD = False
ATTN_OLD = False


def cc_col_step_old(src_name, g):
    def emit(pr, al, banks, io):
        pr.s("ccb"); pr.s("ccs")

        def gb(g_):
            g_.dma_start(
                out=io(f"zcb{g}"),
                in_=io(src_name).rearrange("d p t -> (d p) t")[:, T - 1:T],
                allow_slow_non_contiguous=True,
            ).then_inc(pr.s("ccb"), 16)
            pr.cnt["ccb"] += 16
            g_.wait_ge(pr.s("ccb"), pr.cnt["ccb"])
            inc(pr, g_.collective_compute(
                "AllGather", mybir.AluOpType.bypass, replica_groups=CC_GROUPS,
                ins=[io(f"zcb{g}").opt()], outs=[io(f"zgc{g}").opt()]), "ccs")
        pr.blk.gpsimd(gb)
    return emit


def gh_step_old(g, dst_name, to_bf16):
    def emit(pr, al, banks, io):
        g32 = al.t("gh_g32", [128, NDC], F32)
        gbf = al.t("gh_gbf", [128, NDC], BF16)
        iso = al.t("gh_iso", [128, 1], F32)
        src = io(f"zgc{g}")
        dma_seq(pr, "sync",
                [(g32[:, dc:dc + 1], src[dc * 128:(dc + 1) * 128, :], [])
                 for dc in range(NDC)] +
                [(iso[:, :], io("c_isodd"), [])], "ghl")
        ghl = pr.cnt["ghl"]
        pr.s("ghe")

        def eb(e):
            e.wait_ge(pr.s("ghl"), ghl)
            ser(pr, e, e.tensor_scalar_mul(g32[:, :], g32[:, :], iso[:, 0:1]))
            if to_bf16:
                ser(pr, e, e.tensor_copy(gbf[:, :], g32[:, :]))
            inc(pr, e.tensor_copy(g32[:, 0:1], g32[:, 0:1]), "ghe")
        pr.blk.vector(eb)
        tile = gbf if to_bf16 else g32
        dma_seq(pr, "sync", [(io(dst_name), tile[:, :],
                              [("ghe", pr.cnt["ghe"])])], "gho")
    return emit


def cc_last_step(g):
    def emit(pr, al, banks, io):
        pr.s("ccs")

        def gb(g_):
            inc(pr, g_.collective_compute(
                "AllGather", mybir.AluOpType.bypass, replica_groups=CC_GROUPS,
                ins=[io(f"zl{g}").opt()], outs=[io(f"zgt{g}").opt()]), "ccs")
        pr.blk.gpsimd(gb)
    return emit


def cc_kb_step(l):
    def emit(pr, al, banks, io):
        pr.s("ccb"); pr.s("ccs")

        def gb(g_):
            g_.dma_start(
                out=io(f"kbb{l}"),
                in_=io(f"kf{l}_d").rearrange("c p t -> (c p) t")[:, T - TGW:T]
            ).then_inc(pr.s("ccb"), 16)
            pr.cnt["ccb"] += 16
            g_.wait_ge(pr.s("ccb"), pr.cnt["ccb"])
            inc(pr, g_.collective_compute(
                "AllGather", mybir.AluOpType.bypass, replica_groups=CC_GROUPS,
                ins=[io(f"kbb{l}").opt()], outs=[io(f"kbcc{l}").opt()]), "ccs")
            inc(pr, g_.collective_compute(
                "AllGather", mybir.AluOpType.bypass, replica_groups=CC_GROUPS,
                ins=[io(f"vf{l}_d")[12:16].rearrange("c p f -> (c p) f").opt()],
                outs=[io(f"vbcc{l}").opt()]), "ccs")
        pr.blk.gpsimd(gb)
    return emit


def gh_step(g, dst_name, to_bf16):
    def emit(pr, al, banks, io):
        al.t("xstage", [128, NDC * T], F32)
        g32 = al.t("gh_g32", [128, NDC], F32)
        gbf = al.t("gh_gbf", [128, NDC], BF16)
        iso = al.t("gh_iso", [128, 1], F32)
        dma_seq(pr, "sync",
                [(g32[:, :], io(f"zgt{g}")[0:128, :], []),
                 (iso[:, :], io("c_isodd"), [])], "ghl")
        ghl = pr.cnt["ghl"]
        pr.s("ghe")

        def eb(e):
            e.wait_ge(pr.s("ghl"), ghl)
            ser(pr, e, e.tensor_scalar_mul(g32[:, :], g32[:, :], iso[:, 0:1]))
            if to_bf16:
                ser(pr, e, e.tensor_copy(gbf[:, :], g32[:, :]))
            inc(pr, e.tensor_copy(g32[:, 0:1], g32[:, 0:1]), "ghe")
        pr.blk.vector(eb)
        tile = gbf if to_bf16 else g32
        dma_seq(pr, "sync", [(io(dst_name), tile[:, :],
                              [("ghe", pr.cnt["ghe"])])], "gho")
    return emit


def fused_steps():
    steps = [
        (stage1, {}),
        (stage2, {"k_full": "k0ccr", "v_full": "v0ccp"}),
        (cc_step([("st0_d", "st0cc")]), {}),
        (stage3, {"st0_d": "st0cc_a", "st0_p": "st0cc_b"}),
        (stage4, {}),
        (stage5, {}),
        (stage6, {"k1_full": "k1ccr", "v1_full": "v1ccp"}),
        (cc_step([("st1_d", "st1cc")]), {}),
        (stage7, {"st1_d": "st1cc_a", "st1_p": "st1cc_b", "zlast": "zl0"}),
    ]
    zf32, zbf = "z_d", "zbf_d"
    ghsrc8 = {0: "z_d", 1: "z3_0d"}
    for l in range(2):
        gq = 2 * l
        if GHOST_OLD:
            steps.append((cc_col_step_old(ghsrc8[l], [0, 2][l]), {}))
            steps.append((gh_step_old([0, 2][l], f"zg8_{l}", True), {}))
        else:
            steps.append((cc_last_step([0, 2][l]), {}))
            steps.append((gh_step([0, 2][l], f"zg8_{l}", True), {}))
        steps.append((lambda pr, al, banks, io, l=l:
                      stage_ffn_qkv(pr, al, banks, io, l),
                      {"zin": zbf, "zg": f"zg8_{l}", "qksc": f"qksc{l}",
                       "wfqT": f"wfq{l}T", "wfkT": f"wfk{l}T",
                       "wfvT": f"wfv{l}T", "qf_d": f"qf{l}_d",
                       "kf_d": f"kf{l}_d", "vf_d": f"vf{l}_d"}))
        steps.append((cc_kb_step(l), {}))
        steps.append((lambda pr, al, banks, io, zr=(l == 1):
                      stage_ffn_attn(pr, al, banks, io, z_resident=zr),
                      {"z32in": zf32, "qf_d": f"qf{l}_d", "kf_d": f"kf{l}_d",
                       "vf_d": f"vf{l}_d", "kb_x": f"kb{l}_x",
                       "vb_x": f"vb{l}_x", "qkr": f"qkr{l}",
                       "wfoT": f"wfo{l}T", "z2out": f"z2_{l}d",
                       "z2last": ["zl1", "zl3"][l],
                       "z2bfout": f"z2bf_{l}d"}))
        if GHOST_OLD:
            steps.append((cc_col_step_old(f"z2_{l}d", [1, 3][l]), {}))
            steps.append((gh_step_old([1, 3][l], f"zg32_{l}", False), {}))
        else:
            steps.append((cc_last_step([1, 3][l]), {}))
            steps.append((gh_step([1, 3][l], f"zg32_{l}", False), {}))
        final = (l == 1)
        zout = "out_d" if final else "z3_0d"
        steps.append((lambda pr, al, banks, io, f=final:
                      stage_ffn_ffn(pr, al, banks, io, final=f),
                      {"z32in": f"z2_{l}d", "zg32": f"zg32_{l}",
                       "lncols": f"flnc{l}", "fb1cols": f"fb1c{l}",
                       "fb2cols": f"fb2c{l}", "fw1T": f"fw1T{l}",
                       "fw2T": f"fw2T{l}", "zout": zout,
                       "z3last": "zl2",
                       "zbfout": "z3bf_0d"}))
        zf32, zbf = "z3_0d", "z3bf_0d"
    return steps


def build_fused(placement=None, rec=None, dbg=(), dbg_copy=None):
    nc = bass.Bass("TRN2", target_bir_lowering=False)
    pr = Prog(nc)
    handles = _mk_handles(nc, dbg=dbg)
    banks = [nc.alloc_psum_tensor(f"bank{i}", [128, 512], F32)
             for i in range(8)]
    if rec is not None:
        al = RecAl(nc, rec)
    else:
        al = SbAl(nc)
        pos, total, prec = placement
        al.t("mega", [128, (total + 3) // 4], F32)
        for nm, (shape, dt, bpp, ss) in prec.items():
            al.overlay(nm, shape, dt, "mega", byte_off=pos[nm])
    steps = fused_steps()
    with nc.Block() as blk:
        pr.blk = blk
        for i, (fn, rename) in enumerate(steps):
            io = lambda nm, rename=rename: handles[rename.get(nm, nm)]
            if i > 0:
                if rec is not None:
                    al.step = "*"
                _boundary(pr, al, io)
            if rec is not None:
                al.step = i
            fn(pr, al, banks, io)
        if rec is not None:
            al.step = "*"
        if dbg_copy:
            pr.s("dbgc")
            waits = [(nm, pr.cnt[nm]) for nm in sorted(pr.cnt)
                     if pr.cnt[nm] > 0]

            def cb(g, waits=waits):
                for nm, v in waits:
                    g.wait_ge(pr.s(nm), v)
                for nm, (shape, dt) in dbg_copy.items():
                    t = nc.dram_tensor("dbg_" + nm, list(shape), dt,
                                       kind="ExternalOutput")
                    g.dma_start(out=t.ap(), in_=handles[nm]).then_inc(
                        pr.s("dbgc"), 16)
                    pr.cnt["dbgc"] += 16
            pr.blk.gpsimd(cb)
        _final_wait(pr, al, lambda nm: handles[nm])
    return nc


def kernel(**inputs):
    env = [dict(p) for p in host_prepare(inputs)]
    rec = {}
    build_fused(rec=rec)
    pos, total = _place(rec)
    nc = build_fused(placement=(pos, total, rec))
    in_maps = []
    for c in range(NCORE):
        m = {}
        for nm in EXTERNAL_INPUTS:
            shape, dt = BUFS[nm]
            a = np.asarray(env[c][nm])
            assert tuple(a.shape) == tuple(shape), (nm, a.shape, shape)
            m[nm] = a.astype(dt) if a.dtype != dt else a
        in_maps.append(m)
    res = run_bass_kernel_spmd(nc, in_maps, core_ids=list(range(NCORE)))
    x = np.asarray(inputs["L0_attn_x"])
    out = np.empty(x.shape, np.float32)
    for c in range(NCORE):
        b, h = c // 2, c % 2
        out[b, h * T:(h + 1) * T] = res.results[c]["out_d"].reshape(D, T).T
    return out



def l0_jlists():
    lg = np.log(0.95)
    jls, slots = [], []
    for g in range(NTG):
        js = set()
        sl = set()
        for half in range(2):
            i0 = half * T + g * TGW
            for jc in range(S_FULL // 128):
                if jc * 128 > i0 + TGW - 1:
                    continue
                dmin = i0 - (jc * 128 + 127)
                if dmin > 0 and np.exp(dmin * lg) < 1e-40:
                    continue
                js.add(jc)
                delta = i0 - jc * 128
                if -TGW < delta < 128:
                    sl.add(jc)
        jls.append(sorted(js))
        slots.append(sorted(sl))
    return jls, slots


def l1_jlists():
    jls, slots = [], []
    for g in range(NTG):
        js = set()
        sl = set()
        for half in range(2):
            i0 = half * T + g * TGW
            for jc in range(S_FULL // 128):
                if jc * 128 > i0 + TGW - 1:
                    continue
                js.add(jc)
                delta = i0 - jc * 128
                if -TGW < delta < 128:
                    sl.add(jc)
        jls.append(sorted(js))
        slots.append(sorted(sl))
    return jls, slots


def emit_attn(pr, al, banks, io, *, q_name, k_name, v_name, y_sb, jls, slots,
              gcol_name, gcol_per_head, gii_name, tri_name, tag,
              stats_stc=None):
    for s in ("amS", "aac", "aev", "agp", "akv", "amY", "aevY", "amsk"):
        pr.s(s)
    q_sb = al.t(f"q_{tag}", [128, NDC, T], BF16)
    khp = al.t("a_khp", [128, 2, S_FULL], BF16)
    vhp = al.t("a_vhp", [128, 2, S_FULL // 128, 128], BF16)
    sbf = al.t("a_sbf", [128, 3, 2, TGW], BF16)
    trim = al.t("a_trim", [128, 8, TGW], BF16)
    gcw = 16 * NTG * 32 if gcol_per_head else NTG * 32
    gcol = al.t(f"a_gcol_{tag}", [128, gcw], F32)
    giw = 8 * TGW if gcol_per_head else TGW
    gii = al.t(f"a_gii_{tag}", [128, giw], F32)
    NJC = S_FULL // 128

    dma_seq(pr, "sync", [
        (q_sb[:, :, :], io(q_name).rearrange("h p t -> p h t"), []),
        (gcol[:, :], io(gcol_name), []),
        (gii[:, :], io(gii_name), []),
    ], "astat")
    stat_loads = pr.cnt["astat"]

    SA = [banks[0], banks[2], banks[6]]
    SB = [banks[1], banks[3], banks[7]]
    YB = [banks[4], banks[5]]

    jobs = [(ig, hp) for ig in range(NTG) for hp in range(8)]
    njobs = len(jobs)
    base_mY = pr.cnt["amY"]
    base_evY = pr.cnt["aevY"]

    pr.s("amsk"); pr.s("akv0"); pr.s("akv1")
    base_msk = pr.cnt["amsk"]
    base_kv = [pr.cnt["akv0"], pr.cnt["akv1"]]
    HT = S_FULL // 2
    items = []
    for ig in range(NTG):
        J0 = ig * 8
        w = [("amY", base_mY + J0)] if ig > 0 else []
        items.append((trim[:, :, :],
                      io(tri_name)[ig].rearrange("s p t -> p s t"), w, "amsk"))
        for hp in range(8):
            J = J0 + hp
            w = [("amY", base_mY + J - 1)] if J >= 2 else [("astat", stat_loads)]
            items.append((khp[:, J % 2, 0:HT], io(k_name)[0, hp], list(w),
                          f"akv{J % 2}"))
            items.append((khp[:, J % 2, HT:S_FULL], io(k_name)[1, hp], list(w),
                          f"akv{J % 2}"))
            items.append((vhp[:, J % 2],
                          io(v_name)[:, :, hp * 128:(hp + 1) * 128],
                          list(w), f"akv{J % 2}"))
    dma_seq(pr, "sync", items, "akv_unused")

    acb = [0] * njobs; evb = [0] * njobs; gpb = [0] * njobs; msb = [0] * njobs
    a_ac = pr.cnt["aac"]; a_ev = pr.cnt["aev"]; a_gp = pr.cnt["agp"]
    a_mS = pr.cnt["amS"]
    for J, (ig, hp) in enumerate(jobs):
        L = len(jls[ig])
        M = len([jc for jc in jls[ig] if jc in slots[ig]])
        msb[J] = a_mS; acb[J] = a_ac; evb[J] = a_ev; gpb[J] = a_gp
        a_mS += L; a_ac += L; a_ev += L; a_gp += 2 * M

    def tbody(t):
        t.wait_ge(pr.s("astat"), stat_loads)
        for J, (ig, hp) in enumerate(jobs):
            jl = jls[ig]
            L = len(jl)
            t.wait_ge(pr.s(f"akv{J % 2}"), base_kv[J % 2] + 48 * (J // 2 + 1))
            if J >= 2:
                t.wait_ge(pr.s("aevY"), base_evY + J - 1)

            def emit_pv(i):
                jc = jl[i]
                t.wait_ge(pr.s("aac"), acb[J] + i + 1)
                t.wait_ge(pr.s("aev"), evb[J] + i + 1)
                if jc in slots[ig]:
                    nmask = len([x for x in jl[:i + 1] if x in slots[ig]])
                    t.wait_ge(pr.s("agp"), gpb[J] + 2 * nmask)
                t.matmul(YB[J % 2][0:64, :],
                         vhp[:, J % 2, jc, 0:64],
                         sbf[:, i % 3, 0, :],
                         start=(i == 0), stop=False, skip_group_check=True)
                return t.matmul(YB[J % 2][64:128, :],
                                vhp[:, J % 2, jc, 64:128],
                                sbf[:, i % 3, 1, :],
                                start=(i == 0), stop=(i == L - 1),
                                skip_group_check=True)

            for i, jc in enumerate(jl):
                if i >= 3:
                    t.wait_ge(pr.s("aac"), acb[J] + i - 2)
                    t.wait_ge(pr.s("aev"), evb[J] + i - 2)
                t.matmul(SA[i % 3][:, :], khp[0:64, J % 2, jc * 128:(jc + 1) * 128],
                         q_sb[0:64, hp, ig * TGW:(ig + 1) * TGW],
                         start=True, stop=True)
                t.matmul(SB[i % 3][:, :],
                         khp[64:128, J % 2, jc * 128:(jc + 1) * 128],
                         q_sb[64:128, hp, ig * TGW:(ig + 1) * TGW],
                         start=True, stop=True).then_inc(pr.s("amS"), 1)
                if i >= 2:
                    emit_pv(i - 2)
            if L >= 2:
                emit_pv(L - 2)
            emit_pv(L - 1).then_inc(pr.s("amY"), 1)
    pr.blk.tensor(tbody)
    pr.cnt["amS"] = a_mS
    pr.cnt["amY"] = base_mY + njobs

    def acbody(a):
        for J, (ig, hp) in enumerate(jobs):
            jl = jls[ig]
            for i, jc in enumerate(jl):
                a.wait_ge(pr.s("amS"), msb[J] + i + 1)
                if gcol_per_head:
                    col = gcol[:, ((2 * hp) * NTG + ig) * 32 + jc:
                               ((2 * hp) * NTG + ig) * 32 + jc + 1]
                else:
                    col = gcol[:, ig * 32 + jc: ig * 32 + jc + 1]
                a.activation(sbf[:, i % 3, 0, :], SA[i % 3][:, :], AF.Copy,
                             scale=col).then_inc(pr.s("aac"), 1)
    pr.blk.scalar(acbody)
    pr.cnt["aac"] = a_ac

    def evbody(e):
        for J, (ig, hp) in enumerate(jobs):
            jl = jls[ig]
            if not ATTN_OLD:
                e.wait_ge(pr.s("amsk"), base_msk + 16 * (ig + 1))
            for i, jc in enumerate(jl):
                e.wait_ge(pr.s("amS"), msb[J] + i + 1)
                if gcol_per_head:
                    col = gcol[:, ((2 * hp + 1) * NTG + ig) * 32 + jc:
                               ((2 * hp + 1) * NTG + ig) * 32 + jc + 1]
                else:
                    col = gcol[:, ig * 32 + jc: ig * 32 + jc + 1]
                e.tensor_scalar(sbf[:, i % 3, 1, :], SB[i % 3][:, :], col, None,
                                ALU.mult).then_inc(pr.s("aev"), 1)
                if (not ATTN_OLD) and jc in slots[ig]:
                    s = slots[ig].index(jc)
                    e.wait_ge(pr.s("aac"), acb[J] + i + 1)
                    e.tensor_tensor(sbf[:, i % 3, 0, :], sbf[:, i % 3, 0, :],
                                    trim[:, s, :], ALU.mult).then_inc(
                                        pr.s("agp"), 1)
                    e.tensor_tensor(sbf[:, i % 3, 1, :], sbf[:, i % 3, 1, :],
                                    trim[:, s, :], ALU.mult).then_inc(
                                        pr.s("agp"), 1)
            if J >= 1:
                emit_yev(e, J - 1)
        emit_yev(e, njobs - 1)
    sqy = (al.t(f"sqy_{tag}", [128, 2, TGW], F32)
           if stats_stc is not None else None)

    def emit_yev(e, J):
        ig, hp = jobs[J]
        e.wait_ge(pr.s("amY"), base_mY + J + 1)
        if gcol_per_head:
            gt = gii[:, hp * TGW:(hp + 1) * TGW]
        else:
            gt = gii[:, 0:TGW]
        yt = y_sb[:, hp * T + ig * TGW: hp * T + (ig + 1) * TGW]
        if stats_stc is None:
            e.tensor_tensor(yt, YB[J % 2][:, :], gt,
                            ALU.mult).then_inc(pr.s("aevY"), 1)
        else:
            e.tensor_tensor(yt, YB[J % 2][:, :], gt, ALU.mult)
            e.reduce_sum(stats_stc[:, 0, hp, ig:ig + 1], yt,
                         mybir.AxisListType.X)
            e.tensor_tensor(sqy[:, J % 2, :], yt, yt, ALU.mult)
            e.reduce_sum(stats_stc[:, 1, hp, ig:ig + 1], sqy[:, J % 2, :],
                         mybir.AxisListType.X).then_inc(pr.s("aevY"), 1)
    pr.blk.vector(evbody)
    pr.cnt["aev"] = a_ev
    pr.cnt["aevY"] = base_evY + njobs

    if ATTN_OLD:
        def gpbody(g):
            for J, (ig, hp) in enumerate(jobs):
                jl = jls[ig]
                g.wait_ge(pr.s("amsk"), base_msk + 16 * (ig + 1))
                for i, jc in enumerate(jl):
                    if jc not in slots[ig]:
                        continue
                    s = slots[ig].index(jc)
                    g.wait_ge(pr.s("aac"), acb[J] + i + 1)
                    g.wait_ge(pr.s("aev"), evb[J] + i + 1)
                    g.tensor_tensor(sbf[:, i % 2, 0, :], sbf[:, i % 2, 0, :],
                                    trim[:, s, :], ALU.mult).then_inc(
                                        pr.s("agp"), 1)
                    g.tensor_tensor(sbf[:, i % 2, 1, :], sbf[:, i % 2, 1, :],
                                    trim[:, s, :], ALU.mult).then_inc(
                                        pr.s("agp"), 1)
        pr.blk.gpsimd(gpbody)
    pr.cnt["agp"] = a_gp



def emit_stats_pass(pr, al, banks, io, *, src_name, dst_name, cst_wait):
    for s in ("mm", "ac", "ev"):
        pr.s(s)
    src = io(src_name)
    ot = al.t("st_ot", [128, 2, TGW], F32)
    ob = al.t("st_ob", [128, 2, TGW], BF16)
    osq = al.t("st_osq", [128, 2, TGW], BF16)
    hp_ones = al.t("st_hpones", [128, 2], BF16)
    stats_sb = al.t("st_out", [2, 16], F32)
    dma_seq(pr, "sync", [(hp_ones[:, :], io("c_hp_ones"), [])], "cstc")
    cst2 = pr.cnt["cstc"]
    bank = banks[6]
    items = []
    for rt in range(NDC):
        for tg in range(NTG):
            j = rt * NTG + tg
            w = [("ac", None), ]
            items.append((ot[:, j % 2, :], src[rt, :, tg * TGW:(tg + 1) * TGW], j))
    ac0 = pr.cnt["ac"]; ev0 = pr.cnt["ev"]; mm0 = pr.cnt["mm"]
    pr.s("sst0"); pr.s("sst1")
    base_sst = [pr.cnt["sst0"], pr.cnt["sst1"]]
    def dmab(s):
        for (dst, srcap, j) in items:
            if j >= 2:
                s.wait_ge(pr.s("mm"), mm0 + (j - 1))
            s.dma_start(out=dst, in_=srcap).then_inc(pr.s(f"sst{j % 2}"), 16)
    pr.blk.sync(dmab)
    for (dst, srcap, j) in items:
        pr.cnt[f"sst{j % 2}"] += 16
    def acb(a):
        a.wait_ge(pr.s("cstc"), cst2)
        for j in range(len(items)):
            a.wait_ge(pr.s(f"sst{j % 2}"), base_sst[j % 2] + 16 * (j // 2 + 1))
            if j >= 2:
                a.wait_ge(pr.s("mm"), mm0 + (j - 1))
            inc(pr, a.activation(osq[:, j % 2, :], ot[:, j % 2, :], AF.Square), "ac")
    pr.blk.scalar(acb)
    def evb(e):
        for j in range(len(items)):
            e.wait_ge(pr.s(f"sst{j % 2}"), base_sst[j % 2] + 16 * (j // 2 + 1))
            if j >= 2:
                e.wait_ge(pr.s("mm"), mm0 + (j - 1))
            inc(pr, e.tensor_copy(ob[:, j % 2, :], ot[:, j % 2, :]), "ev")
            if j % NTG == NTG - 1:
                emit_red(e, j // NTG)
    def emit_red(e, rt):
        e.wait_ge(pr.s("mm"), mm0 + NTG * (rt + 1))
        inc(pr, e.reduce_sum(stats_sb[0:2, rt:rt + 1], bank[0:2, :],
                             mybir.AxisListType.X), "rsum")
        inc(pr, e.reduce_sum(stats_sb[0:2, 8 + rt:9 + rt], bank[64:66, :],
                             mybir.AxisListType.X), "rsum")
    pr.s("rsum")
    rsum0 = pr.cnt["rsum"]
    pr.blk.vector(evb)
    def tb(t):
        t.wait_ge(pr.s("cstc"), cst2)
        for j in range(len(items)):
            rt, tg = j // NTG, j % NTG
            t.wait_ge(pr.s("ac"), ac0 + j + 1)
            t.wait_ge(pr.s("ev"), ev0 + j + 1)
            if tg == 0 and rt > 0:
                t.wait_ge(pr.s("rsum"), rsum0 + 2 * rt)
            t.matmul(bank[0:2, :], hp_ones[:, :], ob[:, j % 2, :],
                     start=(tg == 0), stop=(tg == NTG - 1), skip_group_check=True)
            last = t.matmul(bank[64:66, :], hp_ones[:, :], osq[:, j % 2, :],
                            start=(tg == 0), stop=(tg == NTG - 1),
                            skip_group_check=True)
            last.then_inc(pr.s("mm"), 1)
            pr.cnt["mm"] += 1
    pr.blk.tensor(tb)
    rs_done = pr.cnt["rsum"]
    dma_seq(pr, "sync", [(io(dst_name), stats_sb[:, :], [("rsum", rs_done)])],
            "dout")


def emit_stats_final(pr, al, banks, io, *, stc, dst_name, tag, ev_wait):
    hp = al.t("st_hpones", [128, 2], BF16)
    stf = al.t(f"stf_{tag}", [128, 2, 8, 1], F32)
    stlo = al.t(f"stlo_{tag}", [128, 2, 8, 1], F32)
    stfb = al.t(f"stfb_{tag}", [128, 2, 2, 8, 1], BF16)
    sto = al.t(f"sto_{tag}", [2, 16], F32)
    dma_seq(pr, "sync", [(hp[:, :], io("c_hp_ones"), [])], "cstc")
    cst2 = pr.cnt["cstc"]
    pr.s("stv"); pr.s("stm")

    def evf(e):
        for (sn, v) in ev_wait:
            e.wait_ge(pr.s(sn), v)
        ser(pr, e, e.reduce_sum(stf[:, 0], stc[:, 0], mybir.AxisListType.X),
            name="chx_st")
        ser(pr, e, e.reduce_sum(stf[:, 1], stc[:, 1], mybir.AxisListType.X),
            name="chx_st")
        ser(pr, e, e.tensor_copy(stfb[:, 0], stf[:, :]), name="chx_st")
        ser(pr, e, e.tensor_tensor(stlo[:, :], stf[:, :], stfb[:, 0],
                                   ALU.subtract), name="chx_st")
        inc(pr, e.tensor_copy(stfb[:, 1], stlo[:, :]), "stv")
    pr.blk.vector(evf)
    stv = pr.cnt["stv"]

    def tbf(t):
        t.wait_ge(pr.s("stv"), stv)
        t.wait_ge(pr.s("cstc"), cst2)
        t.matmul(banks[7][0:2, 0:16], hp[:, :], stfb[:, 0],
                 start=True, stop=False, skip_group_check=True)
        inc(pr, t.matmul(banks[7][0:2, 0:16], hp[:, :], stfb[:, 1],
                         start=False, stop=True, skip_group_check=True), "stm")
    pr.blk.tensor(tbf)
    stm = pr.cnt["stm"]

    def evf2(e):
        e.wait_ge(pr.s("stm"), stm)
        inc(pr, e.tensor_copy(sto[:, :], banks[7][0:2, 0:16]), "stv")
    pr.blk.vector(evf2)
    dma_seq(pr, "sync", [(io(dst_name), sto[:, :],
                          [("stv", pr.cnt["stv"])])], "dout")


def stage2(pr, al, banks, io):
    cstv = load_consts(pr, al, io)
    al.t("xstage", [128, NDC * T], F32)
    y_sb = al.t("y_attn", [128, NDC * T], BF16)
    jls, slots = l0_jlists()
    emit_attn(pr, al, banks, io, q_name="q0_d", k_name="k_full", v_name="v_full",
              y_sb=y_sb, jls=jls, slots=slots, gcol_name="gcol0",
              gcol_per_head=False, gii_name="gii0", tri_name="tri0", tag="l0")
    attn_done_mm = pr.cnt["aevY"]
    pr.blk.tensor(lambda t: t.wait_ge(pr.s("aevY"), attn_done_mm))
    bo = al.t("bo_cols", [128, NDC], F32)
    dma_seq(pr, "sync", [(bo[:, :], io("bo_cols"), [])], "cstb")
    cst2 = pr.cnt["cstb"]
    pr.blk.vector(lambda e: e.wait_ge(pr.s("cstb"), cst2))
    dst = io("o0_d")
    for s in ("mm", "ev"):
        pr.s(s)
    ev_base = pr.cnt["ev"]
    ring, pre_ev = ring_out(pr, al, n_jobs=NDC * NTG, tile_shape=[TGW], dtype=F32,
                            dst_fn=lambda j: dst[j // NTG, :,
                                                 (j % NTG) * TGW:(j % NTG + 1) * TGW],
                            ev_base=ev_base, dsem="dout", ring_name="ring_f32")
    jc2 = [0]
    stc = al.t("stc_st0", [128, 2, 8, 4], F32)
    sqj = al.t("sqj_st0", [128, 2, TGW], F32)
    def ev_fn(e, rt, rw, tg, ps):
        j = jc2[0]; jc2[0] += 1
        for (sn, v) in pre_ev(j):
            e.wait_ge(pr.s(sn), v)
        r = ring[:, j % 4]
        e.tensor_scalar_add(r, ps[0:128, :], bo[:, rt:rt + 1])
        e.reduce_sum(stc[:, 0, rt, tg:tg + 1], r, mybir.AxisListType.X)
        e.tensor_tensor(sqj[:, j % 2, :], r, r, ALU.mult)
        return e.reduce_sum(stc[:, 1, rt, tg:tg + 1], sqj[:, j % 2, :],
                            mybir.AxisListType.X)
    emit_proj_fm(pr, al, banks[:4],
                 rhs_tile=lambda dc, tg: y_sb[:, dc * T + tg * TGW:
                                              dc * T + (tg + 1) * TGW],
                 wT=io("wo0T"), n_in=D, n_out=D, ev_fn=ev_fn, tag="wo0")
    emit_stats_final(pr, al, banks, io, stc=stc, dst_name="st0_d", tag="st0",
                     ev_wait=[("ev", pr.cnt["ev"])])



def emit_gn_apply(pr, al, banks, io, *, st_own, st_pair, g_cols, b_cols,
                  src_name, x_sb, nelem, tag, cst_wait):
    for s in ("mm", "ac", "ev", "gst"):
        pr.s(s)
    sto = al.t(f"gn_sto_{tag}", [2, 16], F32)
    stp = al.t(f"gn_stp_{tag}", [2, 16], F32)
    srow = al.t(f"gn_rows_{tag}", [2, 3 * 16], F32)
    cols = al.t(f"gn_cols_{tag}", [128, 4 * NDC], F32)
    gcols = al.t(f"gn_g_{tag}", [128, NDC], F32)
    bcols = al.t(f"gn_b_{tag}", [128, NDC], F32)
    hpT = al.t("c_hp_onesT", [2, 128], F32)
    ot = al.t(f"gn_ot_{tag}", [128, 2, TGW], F32)
    gt = al.t(f"gn_gt_{tag}", [128, 2, TGW], F32)
    dma_seq(pr, "sync", [
        (sto[:, :], io(st_own), []),
        (stp[:, :], io(st_pair), []),
        (gcols[:, :], io(g_cols), []),
        (bcols[:, :], io(b_cols), []),
        (hpT[:, :], io("c_hp_onesT"), []),
    ], "gst")
    gst0 = pr.cnt["gst"]
    SR = lambda i: srow[0:2, i * 16:(i + 1) * 16]
    def evst(e):
        e.wait_ge(pr.s("gst"), gst0)
        ser(pr, e, e.tensor_tensor(srow[0:2, 0:16], sto[:, :], stp[:, :],
                                   ALU.add))
        ser(pr, e, e.tensor_scalar_mul(SR(1), SR(0), 1.0 / nelem))
        ser(pr, e, e.tensor_tensor(SR(2)[:, 0:8], SR(1)[:, 0:8],
                                   SR(1)[:, 0:8], ALU.mult))
        ser(pr, e, e.tensor_tensor(SR(1)[:, 8:16], SR(1)[:, 8:16],
                                   SR(2)[:, 0:8], ALU.subtract))
        inc(pr, e.tensor_scalar_add(SR(1)[:, 8:16], SR(1)[:, 8:16], 1e-5), "ev")
    pr.blk.vector(evst)
    evv = pr.cnt["ev"]
    def acst(a):
        a.wait_ge(pr.s("ev"), evv)
        ser(pr, a, a.activation(SR(2)[:, 8:16], SR(1)[:, 8:16], AF.Ln),
            name="chxa")
        inc(pr, a.activation(SR(2)[:, 8:16], SR(2)[:, 8:16], AF.Exp, scale=-0.5),
            "ac")
    pr.blk.scalar(acst)
    acv = pr.cnt["ac"]
    def evst2(e):
        e.wait_ge(pr.s("ac"), acv)
        e.tensor_copy(SR(0)[:, 0:8], SR(2)[:, 8:16])
        inc(pr, e.tensor_tensor(SR(0)[:, 8:16], SR(1)[:, 0:8], SR(2)[:, 8:16],
                                ALU.mult), "ev")
    pr.blk.vector(evst2)
    evv2 = pr.cnt["ev"]
    def tbc(t):
        t.wait_ge(pr.s("ev"), evv2)
        inc(pr, t.matmul(banks[6][0:128, 0:16], hpT[:, :], SR(0),
                         start=True, stop=True), "mm")
    pr.blk.tensor(tbc)
    mmv = pr.cnt["mm"]
    def evst3(e):
        e.wait_ge(pr.s("mm"), mmv)
        ser(pr, e, e.tensor_copy(cols[:, 0:16], banks[6][0:128, 0:16]))
        e.tensor_tensor(cols[:, 16:24], cols[:, 0:8], gcols[:, :], ALU.mult)
        ser(pr, e, e.tensor_tensor(cols[:, 24:32], cols[:, 8:16], gcols[:, :],
                                   ALU.mult))
        inc(pr, e.tensor_tensor(cols[:, 24:32], bcols[:, :], cols[:, 24:32],
                                ALU.subtract), "ev")
    pr.blk.vector(evst3)
    cols_done = pr.cnt["ev"]
    scol = cols[:, 16:24]
    bcol = cols[:, 24:32]

    src = io(src_name)
    pr.s("gno0"); pr.s("gno1")
    base_o = [pr.cnt["gno0"], pr.cnt["gno1"]]
    ev0 = pr.cnt["ev"]
    items = []
    for j in range(NDC * NTG):
        rt, tg = j // NTG, j % NTG
        w = [("ev", ev0 + 2 * (j - 2) + 1)] if j >= 2 else []
        items.append((ot[:, j % 2, :], src[rt, :, tg * TGW:(tg + 1) * TGW], w,
                      f"gno{j % 2}"))
    dma_seq(pr, "sync", items, "gno_unused")
    def evg(e):
        e.wait_ge(pr.s("ev"), cols_done)
        for j in range(NDC * NTG):
            rt, tg = j // NTG, j % NTG
            e.wait_ge(pr.s(f"gno{j % 2}"), base_o[j % 2] + 16 * (j // 2 + 1))
            inc(pr, e.tensor_scalar(gt[:, 0, :], ot[:, j % 2, :],
                                    scol[:, rt:rt + 1], bcol[:, rt:rt + 1],
                                    ALU.mult, ALU.add), "ev")
            xs = x_sb[:, rt * T + tg * TGW: rt * T + (tg + 1) * TGW]
            inc(pr, e.tensor_tensor(xs, xs, gt[:, 0, :], ALU.add), "ev")
    pr.blk.vector(evg)
    return pr.cnt["ev"]


def stage3(pr, al, banks, io):
    cstv = load_consts(pr, al, io)
    xstage = al.t("xstage", [128, NDC * T], F32)
    res_done = emit_gn_apply(pr, al, banks, io, st_own="st0_d", st_pair="st0_p",
                             g_cols="g0_cols", b_cols="b0_cols", src_name="o0_d",
                             x_sb=xstage, nelem=64 * S_FULL, tag="gn0",
                             cst_wait=cstv)
    pr.blk.scalar(lambda a: a.wait_ge(pr.s("ev"), res_done))
    pr.blk.vector(lambda e: e.wait_ge(pr.s("ev"), res_done))
    xln = al.t("xw", [128, NDC * T], BF16)
    emit_ln(pr, al, banks, x_sb=xstage,
            out_tile=lambda dc, tg: xln[:, dc * T + tg * TGW: dc * T + (tg + 1) * TGW],
            tag="ln", cst_wait=cstv, pass1_eng="vector")



DBG_FF = False

NEXP = 10
NK = 11

def stage4(pr, al, banks, io):
    cstv = load_consts(pr, al, io)
    for s in ("mg", "mu", "md", "ag", "eu", "ed", "mrt", "ert", "mwb", "ewb"):
        pr.s(s)
    xacc = al.t("xstage", [128, NDC * T], F32)
    xln = al.t("xw", [128, NDC * T], BF16)
    wgu = al.t("wgu", [128, NDC, 2816], BF16)
    wgd = al.t("wgd", [128, NK, 1024], BF16)
    h = al.t("moe_h", [128, NK, TGW], BF16)
    gbf = al.t("moe_g", [128, 2, TGW], BF16)
    tmpu = al.t("moe_tmpu", [128, TGW], BF16)
    wbc = al.t("moe_wbc", [128, T], F32)
    gws = al.t("moe_gws", [128, NDC, 9], BF16)
    rout = al.t("moe_rout", [128, 16, 9], F32)
    w9 = al.t("moe_w9", [9, T], F32)
    sel = al.t("c_sel", [16, NEXP, 128], F32)
    ident = al.t("c_ident", [128, 128], F32)

    dma_seq(pr, "sync", [(sel[:, :, :], io("c_sel"), []),
                         (ident[:, :], io("c_ident"), [])], "xin")
    xin_done = pr.cnt["xin"]
    dma_seq(pr, "gpsimd", [(gws[:, :, :], io("gwsT").rearrange(
        "(dc p) o -> p dc o", p=128), [])], "mgw")
    gw_done = pr.cnt["mgw"]

    def trt(t):
        t.wait_ge(pr.s("xin"), xin_done)
        t.wait_ge(pr.s("mgw"), gw_done)
        for tt in range(16):
            bank = banks[6 + tt % 2]
            if tt >= 2:
                t.wait_ge(pr.s("ert"), tt - 1)
            last = None
            for dc in range(NDC):
                last = t.matmul(bank[0:128, 0:9],
                                xln[:, dc * T + tt * 128: dc * T + (tt + 1) * 128],
                                gws[:, dc, :], start=(dc == 0), stop=(dc == NDC - 1))
            inc(pr, last, "mrt")
    pr.blk.tensor(trt)
    def evrt(e):
        for tt in range(16):
            bank = banks[6 + tt % 2]
            e.wait_ge(pr.s("mrt"), tt + 1)
            inc(pr, e.tensor_copy(rout[:, tt, :], bank[0:128, 0:9]), "ert")
    pr.blk.vector(evrt)
    rt_done = pr.cnt["ert"]

    rows = al.t("moe_rows", [128, 4], F32)
    eq8 = al.t("moe_eq8", [128, 16], F32)
    for tt in range(16):
        sc = rout[:, tt, 0:8]
        def ev1(e, tt=tt, sc=sc):
            e.wait_ge(pr.s("ert"), rt_done)
            if tt > 0:
                e.wait_ge(pr.s("ag"), pr._sm_ac)
            inc(pr, e.reduce_max(rows[:, 0:1], sc, mybir.AxisListType.X,
                                 negate=True), "eu")
        pr.blk.vector(ev1)
        ev_a = pr.cnt["eu"]
        def ac1(a, tt=tt, sc=sc):
            a.wait_ge(pr.s("eu"), ev_a)
            a.activation(sc, sc, AF.Exp, bias=rows[:, 0:1])
            inc(pr, a.activation(rout[:, tt, 8:9], rout[:, tt, 8:9],
                                 AF.Sigmoid), "ag")
        pr.blk.scalar(ac1)
        pr._sm_ac = pr.cnt["ag"]
        def ev2(e, tt=tt, sc=sc):
            e.wait_ge(pr.s("ag"), pr._sm_ac)
            ser(pr, e, e.reduce_sum(rows[:, 0:1], sc, mybir.AxisListType.X))
            ser(pr, e, e.reciprocal(rows[:, 1:2], rows[:, 0:1]))
            ser(pr, e, e.tensor_scalar_mul(sc, sc, rows[:, 1:2]))
            ser(pr, e, e.reduce_max(rows[:, 2:3], sc,
                                    mybir.AxisListType.X))
            ser(pr, e, e.tensor_scalar(eq8[:, 0:8], sc, rows[:, 2:3], None,
                                       ALU.is_equal))
            ser(pr, e, e.scalar_tensor_tensor(eq8[:, 8:16], eq8[:, 0:8], -1e30,
                                              sc, ALU.mult, ALU.add))
            ser(pr, e, e.reduce_max(rows[:, 3:4], eq8[:, 8:16],
                                    mybir.AxisListType.X))
            ser(pr, e, e.tensor_scalar(eq8[:, 0:8], sc, rows[:, 3:4], None,
                                       ALU.is_ge))
            inc(pr, e.tensor_tensor(sc, sc, eq8[:, 0:8], ALU.mult), "eu")
        pr.blk.vector(ev2)
    sm_done = pr.cnt["eu"]

    def ttr(t):
        t.wait_ge(pr.s("eu"), sm_done)
        for tt in range(16):
            bank = banks[6 + tt % 2]
            if tt >= 2:
                t.wait_ge(pr.s("ewb"), w9b + tt - 1)
            inc(pr, t.transpose(bank[0:9, 0:128], rout[:, tt, :], ident[:, :]),
                "mwb")
    w9b = pr.cnt["ewb"]
    mwb0 = pr.cnt["mwb"]
    pr.blk.tensor(ttr)
    def evtr(e):
        for tt in range(16):
            bank = banks[6 + tt % 2]
            e.wait_ge(pr.s("mwb"), mwb0 + tt + 1)
            inc(pr, e.tensor_copy(w9[0:9, tt * 128:(tt + 1) * 128],
                                  bank[0:9, 0:128]), "ewb")
    pr.blk.vector(evtr)
    w9_done = pr.cnt["ewb"]

    guT = io("guT")
    gdT = io("gdT")
    ew = [0, 1, 2, 3, 4, 5, 6, 7, 8, 8]
    njg = 2 * NK * NTG
    mg0 = pr.cnt["mg"]; mu0 = pr.cnt["mu"]; md0 = pr.cnt["md"]
    ag0 = pr.cnt["ag"]; eu0 = pr.cnt["eu"]; ed0 = pr.cnt["ed"]
    mwb1 = pr.cnt["mwb"]; ewb1 = pr.cnt["ewb"]

    pr.s("mgu"); pr.s("mgd")
    mgu0 = pr.cnt["mgu"]; mgd0 = pr.cnt["mgd"]
    itemsA, itemsB = [], []
    for e in range(NEXP):
        wu = [("mu", mu0 + NTG * NK * e)] if e > 0 else []
        wd = [("md", md0 + 32 * e)] if e > 0 else []
        gre = guT[e].rearrange("(dc p) o -> p dc o", p=128)
        dre = gdT[e].rearrange("(kc p) o -> p kc o", p=128)
        itemsA.append((wgu[:, :, 0:1408], gre[:, :, 0:1408], list(wu), "mgu"))
        itemsB.append((wgu[:, :, 1408:2816], gre[:, :, 1408:2816],
                       list(wu), "mgu"))
        itemsA.append((wgd[:, :, 0:512], dre[:, :, 0:512], list(wd), "mgd"))
        itemsB.append((wgd[:, :, 512:1024], dre[:, :, 512:1024],
                       list(wd), "mgd"))
    if MOE_OLD:
        inter = []
        for i in range(len(itemsA)):
            inter.append(itemsA[i]); inter.append(itemsB[i])
        dma_seq(pr, "gpsimd", inter, "mgu_unusedA")
    else:
        dma_seq(pr, "gpsimd", itemsA, "mgu_unusedA")
        dma_seq(pr, "sync", itemsB, "mgu_unusedB")

    def tmoe(t):
        for e in range(NEXP):
            if e == 0:
                t.wait_ge(pr.s("ewb"), w9_done)
            for tg in range(NTG):
                gj = e * NTG + tg
                if gj >= 2:
                    t.wait_ge(pr.s("ewb"), ewb1 + gj - 1)
                inc(pr, t.matmul(banks[6 + gj % 2][0:128, 0:TGW],
                                 sel[0:9, ew[e], :],
                                 w9[0:9, tg * TGW:(tg + 1) * TGW],
                                 start=True, stop=True), "mwb")
            t.wait_ge(pr.s("mgu"), mgu0 + 32 * (e + 1))
            t.wait_ge(pr.s("mgd"), mgd0 + 32 * (e + 1))
            for tg in range(NTG):
                for k in range(NK):
                    for half in range(2):
                        jidx = ((e * NTG + tg) * NK + k) * 2 + half
                        bank = banks[jidx % 4]
                        pj = jidx - 4
                        if pj >= 0:
                            if pj % 2 == 0:
                                t.wait_ge(pr.s("ag"), ag0 + pj // 2 + 1)
                            else:
                                t.wait_ge(pr.s("eu"), eu0 + pj // 2 + 1)
                        last = None
                        for dc in range(NDC):
                            last = t.matmul(
                                bank[0:128, 0:TGW],
                                wgu[:, dc, (2 * k + half) * 128:
                                    (2 * k + half + 1) * 128],
                                xln[:, dc * T + tg * TGW: dc * T + (tg + 1) * TGW],
                                start=(dc == 0), stop=(dc == NDC - 1))
                        inc(pr, last, "mg" if half == 0 else "mu")
                for rt in range(NDC):
                    didx = (e * NTG + tg) * NDC + rt
                    bank = banks[4 + didx % 2]
                    if didx >= 2:
                        t.wait_ge(pr.s("ed"), ed0 + didx - 1)
                    t.wait_ge(pr.s("eu"), eu0 + (e * NTG + tg + 1) * NK)
                    last = None
                    for kc in range(NK):
                        last = t.matmul(bank[0:128, 0:TGW],
                                        wgd[:, kc, rt * 128:(rt + 1) * 128],
                                        h[:, kc, :],
                                        start=(kc == 0), stop=(kc == NK - 1))
                    inc(pr, last, "md")
    pr.blk.tensor(tmoe)

    def amoe(a):
        for e in range(NEXP):
            for tg in range(NTG):
                for k in range(NK):
                    gidx = (e * NTG + tg) * NK + k
                    jidx = gidx * 2
                    a.wait_ge(pr.s("mg"), mg0 + gidx + 1)
                    if gidx >= 2:
                        a.wait_ge(pr.s("eu"), eu0 + gidx - 1)
                    inc(pr, a.activation(gbf[:, k % 2, :], banks[jidx % 4][0:128, 0:TGW],
                                         AF.Silu), "ag")
    pr.blk.scalar(amoe)

    def vmoe(e):
        e.wait_ge(pr.s("xin"), xin_done)
        for ei in range(NEXP):
            for tg in range(NTG):
                gj = ei * NTG + tg
                e.wait_ge(pr.s("mwb"), mwb1 + gj + 1)
                inc(pr, e.tensor_copy(wbc[:, tg * TGW:(tg + 1) * TGW],
                                      banks[6 + gj % 2][0:128, 0:TGW]), "ewb")
            for tg in range(NTG):
                for k in range(NK):
                    gidx = (ei * NTG + tg) * NK + k
                    jidx = gidx * 2 + 1
                    if k == 0 and (ei, tg) != (0, 0):
                        e.wait_ge(pr.s("md"), md0 + (ei * NTG + tg) * NDC)
                    e.wait_ge(pr.s("mu"), mu0 + gidx + 1)
                    e.wait_ge(pr.s("ewb"), ewb1 + ei * NTG + NTG)
                    e.tensor_tensor(tmpu[:, :], banks[jidx % 4][0:128, 0:TGW],
                                    wbc[:, tg * TGW:(tg + 1) * TGW], ALU.mult)
                    e.wait_ge(pr.s("ag"), ag0 + gidx + 1)
                    inc(pr, e.tensor_tensor(h[:, k, :], tmpu[:, :], gbf[:, k % 2, :],
                                            ALU.mult), "eu")
                for rt in range(NDC):
                    didx = (ei * NTG + tg) * NDC + rt
                    e.wait_ge(pr.s("md"), md0 + didx + 1)
                    xs = xacc[:, rt * T + tg * TGW: rt * T + (tg + 1) * TGW]
                    inc(pr, e.tensor_tensor(xs, xs, banks[4 + didx % 2][0:128, 0:TGW],
                                            ALU.add), "ed")
    pr.blk.vector(vmoe)




def emit_rope_proj(pr, al, banks, io, *, w_name, out_name, tabA, tabB, sub_b,
                   tag, ln_done):
    xln = al.t("xw", [128, NDC * T], BF16)
    qes = al.t("rp_qes", [128, 4, T], BF16)
    qob = al.t("rp_qob", [128, 2, TGW], BF16)
    rs = al.t("rp_rs", [128, 4, TGW], BF16)
    ring = al.t("rp_ring", [128, 4, TGW], BF16)
    dst = io(out_name)
    for s in ("ro", "rg", "rv"):
        pr.s(s)
    rg0 = pr.cnt["rg"]; rv0 = pr.cnt["rv"]
    pr.s("rdo0"); pr.s("rdo1")
    base_rd = [pr.cnt["rdo0"], pr.cnt["rdo1"]]

    def combine(e, jo):
        e.wait_ge(pr.s("rg"), rg0 + jo + 1)
        if jo >= 2:
            e.wait_ge(pr.s(f"rdo{jo % 2}"), base_rd[jo % 2] + 128 * (jo // 2))
        elif base_rd[jo % 2] > 0:
            e.wait_ge(pr.s(f"rdo{jo % 2}"), base_rd[jo % 2])
        sl = (2 * jo) % 4
        e.tensor_tensor(ring[:, sl, :], rs[:, 0, :], rs[:, 1, :], ALU.add)
        inc(pr, e.tensor_tensor(ring[:, sl + 1, :], rs[:, 2, :], rs[:, 3, :],
                                ALU.subtract), "rv")

    jc = [0]
    def ev_fn(e, rt, rw, tg, ps):
        j = jc[0]; jc[0] += 1
        if j == 0 and rg0 > 0:
            e.wait_ge(pr.s("rg"), rg0)
        if j < 16:
            return e.tensor_copy(qes[:, rt, tg * TGW:(tg + 1) * TGW],
                                 ps[0:128, :])
        jo = j - 16
        if jo >= 1:
            combine(e, jo - 1)
        if jo >= 2:
            e.wait_ge(pr.s("rg"), rg0 + (jo - 1))
        return e.tensor_copy(qob[:, jo % 2, :], ps[0:128, :])

    def dispatch(dc, tg):
        return xln[:, dc * T + tg * TGW: dc * T + (tg + 1) * TGW]

    pr.blk.tensor(lambda t: t.wait_ge(pr.s("ev"), ln_done))
    base_mm, base_ev = emit_proj_fm(pr, al, banks[:4], rhs_tile=dispatch,
                                    wT=io(w_name), n_in=D, n_out=D,
                                    ev_fn=ev_fn, tag=tag)
    def vfin(e):
        combine(e, 15)
    pr.blk.vector(vfin)
    def gpb(g):
        for jo in range(16):
            rt, tg = jo // NTG, jo % NTG
            g.wait_ge(pr.s("ev"), base_ev + 16 + jo + 1)
            if jo >= 1:
                g.wait_ge(pr.s("rv"), rv0 + jo)
            A = tabA[:, tg * TGW:(tg + 1) * TGW]
            B = tabB[:, tg * TGW:(tg + 1) * TGW]
            qe = qes[:, rt, tg * TGW:(tg + 1) * TGW]
            qo = qob[:, jo % 2, :]
            g.tensor_tensor(rs[:, 0, :], qo, A, ALU.mult)
            g.tensor_tensor(rs[:, 1, :], qe, B, ALU.mult)
            g.tensor_tensor(rs[:, 2, :], qe, A, ALU.mult)
            inc(pr, g.tensor_tensor(rs[:, 3, :], qo, B, ALU.mult), "rg")
    pr.blk.gpsimd(gpb)
    items = []
    for jo in range(16):
        rt, tg = jo // NTG, jo % NTG
        sl = (2 * jo) % 4
        for g4 in range(4):
            h = 4 * rt + g4
            w = [("rv", rv0 + jo + 1)]
            items.append((dst[h // 2, (h % 2) * 64 + 32:(h % 2) * 64 + 64,
                              tg * TGW:(tg + 1) * TGW],
                          ring[32 * g4:32 * g4 + 32, sl, :], w, f"rdo{jo % 2}"))
            items.append((dst[h // 2, (h % 2) * 64:(h % 2) * 64 + 32,
                              tg * TGW:(tg + 1) * TGW],
                          ring[32 * g4:32 * g4 + 32, sl + 1, :], w,
                          f"rdo{jo % 2}"))
    dma_seq(pr, "sync", items, "rdo_unused")


def stage5(pr, al, banks, io):
    cstv = load_consts(pr, al, io)
    xstage = al.t("xstage", [128, NDC * T], F32)
    xln = al.t("xw", [128, NDC * T], BF16)
    emit_ln(pr, al, banks, x_sb=xstage,
            out_tile=lambda dc, tg: xln[:, dc * T + tg * TGW: dc * T + (tg + 1) * TGW],
            tag="ln", cst_wait=cstv, pass1_eng="vector")
    ln_done = pr.cnt["ev"]

    def xt(dc, tg):
        return xln[:, dc * T + tg * TGW: dc * T + (tg + 1) * TGW]

    pr.blk.tensor(lambda t: t.wait_ge(pr.s("ev"), ln_done))

    wvre = io("wv1T").rearrange("(dc p) o -> p dc o", p=128)
    wslab = al.t(f"wsl_{NDC}x{TGW}", [128, 2, NDC, TGW], BF16)
    pr.s("wdma0"); pr.s("wdma1")
    base_w = [pr.cnt["wdma0"], pr.cnt["wdma1"]]
    base_mm = pr.cnt["mm"]
    items = [(wslab[:, s % 2, :, :], wvre[:, :, s * TGW:(s + 1) * TGW],
              [("mm", base_mm)]) for s in range(2)]
    dma_seq(pr, "gpsimd", items, "wdma", nsem=2)
    vdst = io("v1_d")
    ev_base = pr.cnt["ev"]
    ring, pre_ev = ring_out(pr, al, n_jobs=32, tile_shape=[TGW], dtype=BF16,
                            dst_fn=lambda j: vdst[j % 16, :,
                                                  (j // 16) * TGW:(j // 16 + 1) * TGW],
                            ev_base=ev_base, dsem="dout")
    jobs = []
    for s in range(2):
        for tt in range(16):
            j = s * 16 + tt
            def mk_mm(s=s, tt=tt):
                def mmf(t, ps):
                    t.wait_ge(pr.s(f"wdma{s % 2}"), base_w[s % 2] + 16)
                    last = None
                    for dc in range(NDC):
                        last = t.matmul(
                            ps[0:128, 0:TGW],
                            xln[:, dc * T + tt * 128: dc * T + (tt + 1) * 128],
                            wslab[:, s % 2, dc, :],
                            start=(dc == 0), stop=(dc == NDC - 1))
                    return last
                return mmf
            def mk_ev(j=j):
                def evf(e, ps):
                    for (sn, v) in pre_ev(j):
                        e.wait_ge(pr.s(sn), v)
                    return e.tensor_copy(ring[:, j % 4], ps[0:128, :])
                return evf
            jobs.append({'mm': mk_mm(), 'ev': mk_ev()})
    pipe(pr, jobs, banks[:4])

    kv_snap = {"v": [pr.cnt.get(f"dout{i}", 0) for i in range(4)]}

    tabs = al.t("rp_tabs", [128, 4, T], BF16)
    dma_seq(pr, "sync", [(tabs[:, :, :], io("rt1"), [])], "cstb")
    tab_done = pr.cnt["cstb"]
    pr.blk.gpsimd(lambda g: g.wait_ge(pr.s("cstb"), tab_done))
    emit_rope_proj(pr, al, banks, io, w_name="wk1T", out_name="k1_d",
                   tabA=tabs[:, 2], tabB=tabs[:, 3], sub_b=True, tag="k1",
                   ln_done=ln_done)
    kv_snap["k"] = [pr.cnt.get(f"rdo{i}", 0) for i in range(2)]

    dstg = io("gated_d")
    ev_base_g = pr.cnt["ac"]
    ringg, pre_evg = ring_out(pr, al, n_jobs=NDC * NTG, tile_shape=[TGW],
                              dtype=BF16,
                              dst_fn=lambda j: dstg[j // NTG, :,
                                                    (j % NTG) * TGW:(j % NTG + 1) * TGW],
                              ev_base=ev_base_g, dsem="dout", ev_sem="ac")
    jcg = [0]
    def evg_fn(a, rt, rw, tg, ps):
        j = jcg[0]; jcg[0] += 1
        for (sn, v) in pre_evg(j):
            a.wait_ge(pr.s(sn), v)
        return a.activation(ringg[:, j % 4], ps[0:128, :], AF.Silu)
    emit_proj_fm(pr, al, banks[:4], rhs_tile=xt, wT=io("wg1T"), n_in=D, n_out=D,
                 ev_fn=evg_fn, ev_eng="scalar", tag="g1", ev_sem="ac")

    pr.s("ccs")
    def ccb(g):
        for i in range(2):
            if kv_snap["k"][i] > 0:
                g.wait_ge(pr.s(f"rdo{i}"), kv_snap["k"][i])
        inc(pr, g.collective_compute(
            "AllGather", mybir.AluOpType.bypass, replica_groups=CC_GROUPS,
            ins=[io("k1_d__2d").opt()], outs=[io("k1cc").opt()]), "ccs")
        for i in range(4):
            if kv_snap["v"][i] > 0:
                g.wait_ge(pr.s(f"dout{i}"), kv_snap["v"][i])
        inc(pr, g.collective_compute(
            "AllGather", mybir.AluOpType.bypass, replica_groups=CC_GROUPS,
            ins=[io("v1_d__2d").opt()], outs=[io("v1cc").opt()]), "ccs")
    pr.blk.gpsimd(ccb)

    emit_rope_proj(pr, al, banks, io, w_name="wq1T", out_name="q1_d",
                   tabA=tabs[:, 0], tabB=tabs[:, 1], sub_b=True, tag="q1",
                   ln_done=ln_done)



def stage6(pr, al, banks, io):
    cstv = load_consts(pr, al, io)
    y_sb = al.t("y_attn1", [128, NDC * T], BF16)
    al.t("xstage", [128, NDC * T], F32)
    stc = al.t("stc_st1", [128, 2, 8, 4], F32)
    jls, slots = l1_jlists()
    emit_attn(pr, al, banks, io, q_name="q1_d", k_name="k1_full",
              v_name="v1_full", y_sb=y_sb, jls=jls, slots=slots,
              gcol_name="gcol1", gcol_per_head=True, gii_name="gii1",
              tri_name="tri1", tag="l1", stats_stc=stc)
    y_done = pr.cnt["aevY"]
    emit_stats_final(pr, al, banks, io, stc=stc, dst_name="st1_d", tag="st1",
                     ev_wait=[("aevY", y_done)])


def emit_gn_cols(pr, al, banks, io, *, st_own, st_pair, g_cols, b_cols, nelem,
                 tag):
    for s in ("mm", "ac", "ev", "gst"):
        pr.s(s)
    sto = al.t(f"gn_sto_{tag}", [2, 16], F32)
    stp = al.t(f"gn_stp_{tag}", [2, 16], F32)
    srow = al.t(f"gn_rows_{tag}", [2, 3 * 16], F32)
    cols = al.t(f"gn_cols_{tag}", [128, 4 * NDC], F32)
    gcols = al.t(f"gn_g_{tag}", [128, NDC], F32)
    bcols = al.t(f"gn_b_{tag}", [128, NDC], F32)
    hpT = al.t("c_hp_onesT", [2, 128], F32)
    dma_seq(pr, "sync", [
        (sto[:, :], io(st_own), []),
        (stp[:, :], io(st_pair), []),
        (gcols[:, :], io(g_cols), []),
        (bcols[:, :], io(b_cols), []),
        (hpT[:, :], io("c_hp_onesT"), []),
    ], "gst")
    gst0 = pr.cnt["gst"]
    SR = lambda i: srow[0:2, i * 16:(i + 1) * 16]
    def evst(e):
        e.wait_ge(pr.s("gst"), gst0)
        ser(pr, e, e.tensor_tensor(srow[0:2, 0:16], sto[:, :], stp[:, :],
                                   ALU.add))
        ser(pr, e, e.tensor_scalar_mul(SR(1), SR(0), 1.0 / nelem))
        ser(pr, e, e.tensor_tensor(SR(2)[:, 0:8], SR(1)[:, 0:8],
                                   SR(1)[:, 0:8], ALU.mult))
        ser(pr, e, e.tensor_tensor(SR(1)[:, 8:16], SR(1)[:, 8:16],
                                   SR(2)[:, 0:8], ALU.subtract))
        inc(pr, e.tensor_scalar_add(SR(1)[:, 8:16], SR(1)[:, 8:16], 1e-5), "ev")
    pr.blk.vector(evst)
    evv = pr.cnt["ev"]
    def acst(a):
        a.wait_ge(pr.s("ev"), evv)
        ser(pr, a, a.activation(SR(2)[:, 8:16], SR(1)[:, 8:16], AF.Ln),
            name="chxa")
        inc(pr, a.activation(SR(2)[:, 8:16], SR(2)[:, 8:16], AF.Exp, scale=-0.5),
            "ac")
    pr.blk.scalar(acst)
    acv = pr.cnt["ac"]
    def evst2(e):
        e.wait_ge(pr.s("ac"), acv)
        e.tensor_copy(SR(0)[:, 0:8], SR(2)[:, 8:16])
        inc(pr, e.tensor_tensor(SR(0)[:, 8:16], SR(1)[:, 0:8], SR(2)[:, 8:16],
                                ALU.mult), "ev")
    pr.blk.vector(evst2)
    evv2 = pr.cnt["ev"]
    def tbc(t):
        t.wait_ge(pr.s("ev"), evv2)
        inc(pr, t.matmul(banks[6][0:128, 0:16], hpT[:, :], SR(0),
                         start=True, stop=True), "mm")
    pr.blk.tensor(tbc)
    mmv = pr.cnt["mm"]
    def evst3(e):
        e.wait_ge(pr.s("mm"), mmv)
        ser(pr, e, e.tensor_copy(cols[:, 0:16], banks[6][0:128, 0:16]))
        e.tensor_tensor(cols[:, 16:24], cols[:, 0:8], gcols[:, :], ALU.mult)
        ser(pr, e, e.tensor_tensor(cols[:, 24:32], cols[:, 8:16], gcols[:, :],
                                   ALU.mult))
        inc(pr, e.tensor_tensor(cols[:, 24:32], bcols[:, :], cols[:, 24:32],
                                ALU.subtract), "ev")
    pr.blk.vector(evst3)
    return cols[:, 16:24], cols[:, 24:32], pr.cnt["ev"]


def stage7(pr, al, banks, io):
    cstv = load_consts(pr, al, io)
    xstage = al.t("xstage", [128, NDC * T], F32)
    ybf = al.t("y_attn1", [128, NDC * T], BF16)
    gv = al.t("xw", [128, NDC * T], BF16)
    gat = al.t("gat7", [128, NDC * T], BF16)
    dma_seq(pr, "sync", [(gat[:, dc * T:(dc + 1) * T], io("gated_d")[dc], [])
                         for dc in range(NDC)], "xin")
    xin_done = pr.cnt["xin"]
    scol, bcol, cols_done = emit_gn_cols(pr, al, banks, io, st_own="st1_d",
                                         st_pair="st1_p", g_cols="g1_cols",
                                         b_cols="b1_cols", nelem=64 * S_FULL,
                                         tag="gn1")
    gtile = al.t("gn_gt_gn1", [128, 2, TGW], F32)
    def evg(e):
        e.wait_ge(pr.s("ev"), cols_done)
        e.wait_ge(pr.s("xin"), xin_done)
        for j in range(NDC * NTG):
            rt, tg = j // NTG, j % NTG
            ysl = ybf[:, rt * T + tg * TGW: rt * T + (tg + 1) * TGW]
            inc(pr, e.tensor_scalar(gtile[:, 0, :], ysl,
                                    scol[:, rt:rt + 1], bcol[:, rt:rt + 1],
                                    ALU.mult, ALU.add), "ev")
            sl = gv[:, rt * T + tg * TGW: rt * T + (tg + 1) * TGW]
            inc(pr, e.tensor_tensor(sl, gtile[:, 0, :],
                                    gat[:, rt * T + tg * TGW: rt * T + (tg + 1) * TGW],
                                    ALU.mult), "ev")
    pr.blk.vector(evg)
    gv_done = pr.cnt["ev"]
    pr.blk.tensor(lambda t: t.wait_ge(pr.s("ev"), gv_done))
    jc = [0]
    def ev_fn(e, rt, rw, tg, ps):
        xs = xstage[:, rt * T + tg * TGW: rt * T + (tg + 1) * TGW]
        return e.tensor_tensor(xs, xs, ps[0:128, :], ALU.add)
    emit_proj_fm(pr, al, banks[:4],
                 rhs_tile=lambda dc, tg: gv[:, dc * T + tg * TGW:
                                            dc * T + (tg + 1) * TGW],
                 wT=io("wp1T"), n_in=D, n_out=D, ev_fn=ev_fn, tag="wp1")
    x3_done = pr.cnt["ev"]
    x3 = io("x3_d")
    dma_seq(pr, "sync", [(x3[dc], xstage[:, dc * T:(dc + 1) * T],
                          [("ev", x3_done)]) for dc in range(NDC)], "dout")
    al.overlay("lnsq_ln7", [128, 2, T], BF16, "gat7", 0)
    al.overlay("lnxb_ln7", [128, 2, T], BF16, "gat7", 8192)
    al.overlay("lnbc_ln7", [128, 2 * T], F32, "gat7", 16384)
    pr.blk.scalar(lambda a: a.wait_ge(pr.s("ev"), x3_done))
    pr.blk.vector(lambda e: e.wait_ge(pr.s("ev"), x3_done))
    zr = al.t("zr32", [128, 4, TGW], F32)
    for i in range(4):
        pr.s(f"zo{i}"); pr.s(f"zc{i}")
    zo_base = [pr.cnt[f"zo{i}"] for i in range(4)]
    zc_base = [pr.cnt[f"zc{i}"] for i in range(4)]
    def out_pre(e, j):
        if j >= 4:
            e.wait_ge(pr.s(f"zo{j % 4}"), zo_base[j % 4] + 16 * (j // 4))
            e.wait_ge(pr.s(f"zc{j % 4}"), zc_base[j % 4] + 16 * (j // 4))
    def gp_post(g, j, evb):
        def cast(jt):
            dc, tg = jt // NTG, jt % NTG
            g.wait_ge(pr.s("ev"), evb + jt + 1)
            g.dma_start(out=io("zbf_d")[dc, :, tg * TGW:(tg + 1) * TGW],
                        in_=zr[:, jt % 4, :]).then_inc(pr.s(f"zc{jt % 4}"), 16)
            pr.cnt[f"zc{jt % 4}"] += 16
        if 1 <= j < NDC * NTG:
            cast(j - 1)
        elif j == NDC * NTG:
            cast(j - 1)
    zlcol = al.t("zlcol", [128, NDC], F32)
    pr.s("zlc")

    def ev_post(e, j):
        if j % NTG == NTG - 1:
            dc = j // NTG
            e.wait_ge(pr.s("ev"), pr.cnt["ev"])
            inc(pr, e.tensor_copy(zlcol[:, dc:dc + 1],
                                  zr[:, j % 4, TGW - 1:TGW]), "zlc")
    if not ZL_WRITES:
        ev_post = None
    zlc0 = pr.cnt["zlc"]
    evb_ln = emit_ln(pr, al, banks, x_sb=xstage,
                     out_tile=lambda dc, tg: zr[:, (dc * NTG + tg) % 4, :],
                     tag="ln7", cst_wait=cstv, out_pre=out_pre, gp_post=gp_post,
                     ev_post=ev_post)
    items_f = []
    for j in range(NDC * NTG):
        dc, tg = j // NTG, j % NTG
        w = [("ev", evb_ln + j + 1)]
        items_f.append((io("z_d")[dc, :, tg * TGW:(tg + 1) * TGW],
                        zr[:, j % 4, :], w, f"zo{j % 4}"))
    dma_seq(pr, "sync", items_f, "zo_unused")
    if ZL_WRITES:
        dma_seq(pr, "sync", [(io("zlast"), zlcol[:, :],
                              [("zlc", zlc0 + NDC)])], "dout")



def stage_ffn_qkv(pr, al, banks, io, l):
    cstv = load_consts(pr, al, io)
    zbf = al.t("xw", [128, NDC * T], BF16)
    zstage = al.t("xstage", [128, NDC * T], F32)
    zsh = al.t("zsh", [128, 4, T], BF16)
    zg = al.t("zg8", [128, NDC], BF16)
    zitems = ([(zbf[:, dc * T:(dc + 1) * T], io("zin")[dc], [])
               for dc in range(NDC)] if l == 0 else [])
    dma_seq(pr, "sync", zitems + [(zg[:, :], io("zg"), [])], "xin")
    xin_done = pr.cnt["xin"]
    def evsh(e):
        e.wait_ge(pr.s("xin"), xin_done)
        if l != 0:
            for dc in range(NDC):
                e.tensor_copy(zbf[:, dc * T:(dc + 1) * T],
                              zstage[:, dc * T:(dc + 1) * T])
        for dch in range(4):
            e.tensor_copy(zsh[:, dch, 1:T], zbf[:, (4 + dch) * T:(5 + dch) * T - 1])
            inc(pr, e.tensor_copy(zsh[:, dch, 0:1], zg[:, 4 + dch:5 + dch]), "ev")
    pr.blk.vector(evsh)
    sh_done = pr.cnt["ev"]

    def xt(dc, tg):
        if dc < 4:
            return zbf[:, dc * T + tg * TGW: dc * T + (tg + 1) * TGW]
        return zsh[:, dc - 4, tg * TGW:(tg + 1) * TGW]

    tabs = al.t("rp_tabs", [128, 4, T], BF16)
    qk_cols = al.t("qkcols", [128, 8], F32)
    hb4 = al.t("c_hb4", [128, 4], BF16)
    hb4T = al.t("c_hb4T", [4, 128], F32)
    dma_seq(pr, "sync", [(tabs[:, :, :], io("rtf"), []),
                         (qk_cols[:, :], io("qksc"), []),
                         (hb4[:, :], io("c_hb4"), []),
                         (hb4T[:, :], io("c_hb4T"), [])], "cstb")
    cst2 = pr.cnt["cstb"]
    pr.blk.tensor(lambda t: (t.wait_ge(pr.s("ev"), sh_done),
                             t.wait_ge(pr.s("cstb"), cst2)))
    pr.blk.scalar(lambda a: a.wait_ge(pr.s("cstb"), cst2))
    pr.blk.gpsimd(lambda g: g.wait_ge(pr.s("cstb"), cst2))

    wslab = al.t("wsl_qk", [128, NDC, TGW], BF16)
    qsq = al.t("qk_sq", [128, 2, TGW], BF16)
    rr = al.t("qk_rr", [4, 2, TGW], F32)
    rbc = al.t("qk_rbc", [128, 2, TGW], F32)
    pre = al.t("qk_pre", [128, 4, TGW], BF16)
    rs = al.t("rp_rs", [128, 4, TGW], BF16)
    ring = al.t("rp_ring", [128, 4, TGW], BF16)
    for s in ("fq", "fa", "fe", "fg", "fv", "fv2", "fw"):
        pr.s(s)

    for qk in range(2):
        nm = "q" if qk == 0 else "k"
        dst = io(f"{nm}f_d")
        wre = io(f"wf{nm}T").rearrange("(dc p) o -> p dc o", p=128)
        base_w = pr.cnt["fw"]
        dma_seq(pr, "gpsimd", [(wslab[:, :, :], wre[:, :, 0:TGW],
                                [("fq", pr.cnt["fq"])])], "fw")
        mm0 = pr.cnt["fq"]; fa0 = pr.cnt["fa"]; fe0 = pr.cnt["fe"]
        gp0 = pr.cnt["fg"]; fv0 = pr.cnt["fv"]; v20 = pr.cnt["fv2"]
        pr.s("fd0"); pr.s("fd1")
        base_fd = [pr.cnt["fd0"], pr.cnt["fd1"]]
        scol = qk_cols[:, 4 * qk:4 * qk + 4]
        jobs = [(pp, tg) for pp in range(2) for tg in range(NTG)]
        NJ = len(jobs)
        Aq = 0 if qk == 0 else 2
        def tb(t, mm0=mm0, fa0=fa0, fe0=fe0, fv0=fv0, base_w=base_w):
            t.wait_ge(pr.s("fw"), base_w + 16)
            for J, (pp, tg) in enumerate(jobs):
                if J >= 2:
                    t.wait_ge(pr.s("fv"), fv0 + 2 * J - 2)
                    t.wait_ge(pr.s("fa"), fa0 + 3 * J - 4)
                    t.wait_ge(pr.s("fe"), fe0 + 3 * J - 5)
                last = None
                for half in range(2):
                    rt = pp + 2 * half
                    for dc in range(NDC):
                        last = t.matmul(banks[2 * half + J % 2][0:128, 0:TGW],
                                        wslab[:, dc, rt * 128:(rt + 1) * 128],
                                        xt(dc, tg), start=(dc == 0),
                                        stop=(dc == NDC - 1))
                inc(pr, last, "fq")
                t.wait_ge(pr.s("fa"), fa0 + 3 * J + 2)
                t.matmul(banks[4 + J % 2][0:4, 0:TGW], hb4[:, :],
                         qsq[:, 0, :], start=True, stop=False,
                         skip_group_check=True)
                inc(pr, t.matmul(banks[4 + J % 2][0:4, 0:TGW], hb4[:, :],
                                 qsq[:, 1, :], start=False, stop=True,
                                 skip_group_check=True), "fq")
                t.wait_ge(pr.s("fe"), fe0 + 3 * J + 2)
                inc(pr, t.matmul(banks[6 + J % 2][0:128, 0:TGW], hb4T[:, :],
                                 rr[0:4, J % 2, :], start=True, stop=True), "fq")
        pr.blk.tensor(tb)
        def ab(a, mm0=mm0, fa0=fa0, fe0=fe0):
            for J in range(NJ):
                a.wait_ge(pr.s("fq"), mm0 + 3 * J + 1)
                if J >= 1:
                    a.wait_ge(pr.s("fq"), mm0 + 3 * J - 1)
                inc(pr, a.activation(qsq[:, 0, :], banks[J % 2][0:128, 0:TGW],
                                     AF.Square), "fa")
                inc(pr, a.activation(qsq[:, 1, :], banks[2 + J % 2][0:128, 0:TGW],
                                     AF.Square), "fa")
                a.wait_ge(pr.s("fe"), fe0 + 3 * J + 1)
                a.activation(rr[0:4, J % 2, :], rr[0:4, J % 2, :], AF.Ln)
                inc(pr, a.activation(rr[0:4, J % 2, :], rr[0:4, J % 2, :],
                                     AF.Exp, scale=0.5), "fa")
        pr.blk.scalar(ab)
        def emit_comb(e, J):
            pp, tg = jobs[J]
            e.wait_ge(pr.s("fg"), gp0 + J + 1)
            if J >= 2:
                e.wait_ge(pr.s(f"fd{J % 2}"), base_fd[J % 2] + 128 * (J // 2))
            elif base_fd[J % 2] > 0:
                e.wait_ge(pr.s(f"fd{J % 2}"), base_fd[J % 2])
            sl = (2 * J) % 4
            e.tensor_tensor(ring[:, sl, :], rs[:, 0, :], rs[:, 1, :], ALU.add)
            inc(pr, e.tensor_tensor(ring[:, sl + 1, :], rs[:, 2, :],
                                    rs[:, 3, :], ALU.subtract), "fv2")
        def vb(e, mm0=mm0, fa0=fa0, fe0=fe0, gp0=gp0):
            for J, (pp, tg) in enumerate(jobs):
                e.wait_ge(pr.s("fq"), mm0 + 3 * J + 2)
                inc(pr, e.tensor_copy(rr[0:4, J % 2, :],
                                      banks[4 + J % 2][0:4, 0:TGW]), "fe")
                e.wait_ge(pr.s("fa"), fa0 + 3 * J + 3)
                e.tensor_scalar_max(rr[0:4, J % 2, :], rr[0:4, J % 2, :], 1e-12)
                inc(pr, e.reciprocal(rr[0:4, J % 2, :], rr[0:4, J % 2, :]), "fe")
                e.wait_ge(pr.s("fq"), mm0 + 3 * J + 3)
                inc(pr, e.tensor_copy(rbc[:, J % 2, :],
                                      banks[6 + J % 2][0:128, 0:TGW]), "fe")
                for half in range(2):
                    if J >= 2:
                        e.wait_ge(pr.s("fg"), gp0 + J - 1)
                    inc(pr, e.scalar_tensor_tensor(
                        pre[:, 2 * half + J % 2, :],
                        banks[2 * half + J % 2][0:128, 0:TGW],
                        scol[:, pp + 2 * half: pp + 2 * half + 1],
                        rbc[:, J % 2, :], ALU.mult, ALU.mult), "fv")
                if J >= 1:
                    emit_comb(e, J - 1)
            emit_comb(e, NJ - 1)
        pr.blk.vector(vb)
        def gb(g, gp0=gp0, fv0=fv0, v20=v20):
            for J, (pp, tg) in enumerate(jobs):
                g.wait_ge(pr.s("fv"), fv0 + 2 * (J + 1))
                if J >= 1:
                    g.wait_ge(pr.s("fv2"), v20 + J)
                A = tabs[:, Aq, tg * TGW:(tg + 1) * TGW]
                B = tabs[:, Aq + 1, tg * TGW:(tg + 1) * TGW]
                lo = pre[:, 0 + J % 2, :]
                hi = pre[:, 2 + J % 2, :]
                g.tensor_tensor(rs[:, 0, :], hi, A, ALU.mult)
                g.tensor_tensor(rs[:, 1, :], lo, B, ALU.mult)
                g.tensor_tensor(rs[:, 2, :], lo, A, ALU.mult)
                inc(pr, g.tensor_tensor(rs[:, 3, :], hi, B, ALU.mult), "fg")
        pr.blk.gpsimd(gb)
        items = []
        for J, (pp, tg) in enumerate(jobs):
            sl = (2 * J) % 4
            for g4 in range(4):
                h = 4 * pp + g4
                w = [("fv2", v20 + J + 1)]
                items.append((dst[h // 2, (h % 2) * 64 + 32:(h % 2) * 64 + 64,
                                  tg * TGW:(tg + 1) * TGW],
                              ring[32 * g4:32 * g4 + 32, sl, :], w,
                              f"fd{J % 2}"))
                items.append((dst[h // 2, (h % 2) * 64:(h % 2) * 64 + 32,
                                  tg * TGW:(tg + 1) * TGW],
                              ring[32 * g4:32 * g4 + 32, sl + 1, :], w,
                              f"fd{J % 2}"))
        dma_seq(pr, "sync", items, "fd_unused")

    vringv = al.t("vring520", [128, 4, 520], BF16)
    pr.blk.vector(lambda e: [e.memset(vringv[:, i, 64::65], 1.0)
                             for i in range(4)])
    wre = io("wfvT").rearrange("(dc p) o -> p dc o", p=128)
    base_w = pr.cnt["fw"]
    dma_seq(pr, "gpsimd", [(wslab[:, :, :], wre[:, :, 0:TGW],
                            [("fq", pr.cnt["fq"])])], "fw")
    vdst = io("vf_d")
    mm0 = pr.cnt["fq"]
    fa_t = pr.cnt["fa"]; fv_t = pr.cnt["fv"]; fe_t = pr.cnt["fe"]
    for i in range(4):
        pr.s(f"vdo{i}")
    base_vd = [pr.cnt[f"vdo{i}"] for i in range(4)]
    ev0v = pr.cnt["fe"]
    def tv(t):
        t.wait_ge(pr.s("fw"), base_w + 16)
        t.wait_ge(pr.s("fa"), fa_t)
        t.wait_ge(pr.s("fv"), fv_t)
        for tt in range(16):
            if tt >= 2:
                t.wait_ge(pr.s("fe"), ev0v + tt - 1)
            last = None
            for dc in range(NDC):
                last = t.matmul(banks[tt % 2][0:128, 0:TGW],
                                zbf[:, dc * T + tt * 128: dc * T + (tt + 1) * 128]
                                if dc < 4 else
                                zsh[:, dc - 4, tt * 128:(tt + 1) * 128],
                                wslab[:, dc, :], start=(dc == 0),
                                stop=(dc == NDC - 1))
            inc(pr, last, "fq")
    pr.blk.tensor(tv)
    def vv(e):
        for tt in range(16):
            e.wait_ge(pr.s("fq"), mm0 + tt + 1)
            if tt >= 4:
                e.wait_ge(pr.s(f"vdo{tt % 4}"),
                          base_vd[tt % 4] + 16 * (tt // 4))
            dstv = vringv[:, tt % 4, :]
            inc(pr, e.tensor_copy(
                bass.AP(dstv.tensor, dstv.offset,
                        [list(dstv.ap[0]), [65, 8], [1, 64]]),
                banks[tt % 2][0:128, 0:TGW]), "fe")
    pr.blk.vector(vv)
    items = [(vdst[tt], vringv[:, tt % 4, :],
              [("fe", ev0v + tt + 1)], f"vdo{tt % 4}") for tt in range(16)]
    dma_seq(pr, "sync", items, "vdo_unused")


def stage_ffn_attn(pr, al, banks, io, z_resident=False):
    cstv = load_consts(pr, al, io)
    zstage = al.t("xstage", [128, NDC * T], F32)
    qf = al.t("fa_qf", [128, 4, T], BF16)
    kf = al.t("fa_kf", [128, 4, T], BF16)
    vf = al.t("fa_vf", [128, 16, 520], BF16)
    kb = al.t("fa_kb", [128, 4, TGW], BF16)
    vb = al.t("fa_vb", [128, 4, 520], BF16)
    msk = al.t("fa_msk", [128, 2, 8, TGW], BF16)
    identb = al.t("c_ident_bf", [128, 128], BF16)
    pbf = al.t("fa_pbf", [128, 2, 2, TGW], BF16)
    rden = al.t("fa_rden", [1, 2 * TGW], F32)
    of = al.t("fa_of", [128, 4, T], BF16)
    qkr = al.t("fa_qkr", [1, 128], F32)
    cbias = al.t("fa_cb", [128, 1], F32)
    rbcs = al.t("fa_rbc", [128, 2, TGW], F32)
    zitems = ([] if z_resident else
              [(zstage[:, dc * T:(dc + 1) * T], io("z32in")[dc], [])
               for dc in range(NDC)])
    dma_seq(pr, "sync", zitems +
                        [(qf[:, :, :], io("qf_d").rearrange("c p t -> p c t"), []),
                         (kf[:, :, :], io("kf_d").rearrange("c p t -> p c t"), []),
                         (vf[:, :, :], io("vf_d").rearrange("c p t -> p c t"), []),
                         (kb[:, :, :], io("kb_x"), []),
                         (vb[:, :, :], io("vb_x"), []),
                         (msk[:, :, :, :],
                          io("mskf").rearrange("w s p t -> p w s t"), []),
                         (identb[:, :], io("c_ident_bf"), []),
                         (qkr[0:1, :], io("qkr"), [])], "xin")
    xin_done = pr.cnt["xin"]
    for s in ("lq", "la", "lv"):
        pr.s(s)
    lq_b = pr.cnt["lq"]; la0 = pr.cnt["la"]; lv_b = pr.cnt["lv"]
    lq0 = lq_b + 1
    lv0 = lv_b + 2
    SA = [banks[0], banks[2]]
    SB = [banks[1], banks[3]]
    PA, PB = banks[4], banks[5]
    BC, BC2 = banks[6], banks[7]
    jobs = [(hp, w) for hp in range(4) for w in range(4)]

    def ksrc(hp, half, w, jc):
        r0 = half * 64
        if jc < 4:
            if w == 0:
                return kb[r0:r0 + 64, hp, jc * 128:(jc + 1) * 128]
            return kf[r0:r0 + 64, hp, (w - 1) * TGW + jc * 128:
                      (w - 1) * TGW + (jc + 1) * 128]
        return kf[r0:r0 + 64, hp, w * TGW + (jc - 4) * 128:
                  w * TGW + (jc - 3) * 128]

    def vsl(hp, half, w, jc, c0, cn):
        h = 2 * hp + half
        if jc < 4:
            if w == 0:
                return vb[:, jc, h * 65 + c0:h * 65 + c0 + cn]
            return vf[:, 4 * (w - 1) + jc, h * 65 + c0:h * 65 + c0 + cn]
        return vf[:, 4 * w + (jc - 4), h * 65 + c0:h * 65 + c0 + cn]

    def tb(t):
        t.wait_ge(pr.s("xin"), xin_done)
        t.wait_ge(pr.s("lv"), lv_b + 1)
        inc(pr, t.matmul(BC[0:128, 0:1], al.ones_f32[0:1, :],
                         rden[0:1, 64:65], start=True, stop=True), "lq")
        def emit_pv(J, i):
            hp, w = jobs[J]
            t.wait_ge(pr.s("la"), la0 + 16 * J + 2 * i + 2)
            t.matmul(PA[0:65, :], vsl(hp, 0, w, i, 0, 65), pbf[:, i % 2, 0, :],
                     start=(i == 0), stop=(i == 7), skip_group_check=True)
            t.matmul(PB[64:128, :], vsl(hp, 1, w, i, 0, 64), pbf[:, i % 2, 1, :],
                     start=(i == 0), stop=(i == 7), skip_group_check=True)
            last = t.matmul(PB[0:1, :], vsl(hp, 1, w, i, 64, 1),
                            pbf[:, i % 2, 1, :], start=(i == 0), stop=(i == 7),
                            skip_group_check=True)
            if i == 7:
                inc(pr, last, "lq")
        for J, (hp, w) in enumerate(jobs):
            wcl = 0 if w == 0 else 1
            for i in range(8):
                sw = la0 + 16 * J + 2 * (i - 1)
                if sw > la0 or J > 0:
                    t.wait_ge(pr.s("la"), max(sw, 0))
                if i == 0 and J >= 1:
                    t.wait_ge(pr.s("lv"), lv0 + 6 * J)
                for half in range(2):
                    bank = (SA if half == 0 else SB)[i % 2]
                    t.matmul(bank[:, :], ksrc(hp, half, w, i),
                             qf[half * 64:half * 64 + 64, hp,
                                w * TGW:(w + 1) * TGW],
                             start=True, stop=False, skip_group_check=True)
                    last = t.matmul(bank[:, :], identb[:, :],
                                    msk[:, wcl, i, :], start=False, stop=True,
                                    skip_group_check=True)
                    if half == 1:
                        inc(pr, last, "lq")
                if i >= 1:
                    emit_pv(J, i - 1)
            emit_pv(J, 7)
            t.wait_ge(pr.s("lv"), lv0 + 6 * J + 2)
            t.matmul(BC[0:128, 0:TGW], al.ones_f32[0:1, :], rden[0:1, 0:TGW],
                     start=True, stop=True)
            inc(pr, t.matmul(BC2[0:128, 0:TGW], al.ones_f32[0:1, :],
                             rden[0:1, TGW:2 * TGW], start=True, stop=True),
                "lq")
    pr.blk.tensor(tb)

    def ab(a):
        a.wait_ge(pr.s("lv"), lv_b + 2)
        for J, (hp, w) in enumerate(jobs):
            for i in range(8):
                a.wait_ge(pr.s("lq"), lq0 + 10 * J + i + 1)
                inc(pr, a.activation(pbf[:, i % 2, 0, :], SA[i % 2][:, :],
                                     AF.Exp, scale=8.0, bias=cbias[:, 0:1]),
                    "la")
                inc(pr, a.activation(pbf[:, i % 2, 1, :], SB[i % 2][:, :],
                                     AF.Exp, scale=8.0, bias=cbias[:, 0:1]),
                    "la")
    pr.blk.scalar(ab)

    def vbod(e):
        e.wait_ge(pr.s("xin"), xin_done)
        e.reduce_max(rden[0:1, 0:1], qkr[0:1, 0:64], mybir.AxisListType.X)
        ser(pr, e, e.reduce_max(rden[0:1, 1:2], qkr[0:1, 64:128],
                                mybir.AxisListType.X))
        inc(pr, e.tensor_tensor(rden[0:1, 64:65], rden[0:1, 0:1],
                                rden[0:1, 1:2], ALU.mult), "lv")
        e.wait_ge(pr.s("lq"), lq_b + 1)
        inc(pr, e.tensor_scalar_mul(cbias[:, 0:1], BC[0:128, 0:1], -8.5), "lv")
        for J, (hp, w) in enumerate(jobs):
            e.wait_ge(pr.s("lq"), lq0 + 10 * J + 9)
            inc(pr, e.reciprocal(rden[0:1, 0:TGW], PA[64:65, :]), "lv")
            inc(pr, e.reciprocal(rden[0:1, TGW:2 * TGW], PB[0:1, :]), "lv")
            e.wait_ge(pr.s("lq"), lq0 + 10 * J + 10)
            inc(pr, e.tensor_copy(rbcs[0:64, 0, :], BC[0:64, :]), "lv")
            inc(pr, e.tensor_copy(rbcs[64:128, 1, :], BC2[64:128, :]), "lv")
            inc(pr, e.tensor_tensor(of[0:64, hp, w * TGW:(w + 1) * TGW],
                                    PA[0:64, :], rbcs[0:64, 0, :], ALU.mult),
                "lv")
            inc(pr, e.tensor_tensor(of[64:128, hp, w * TGW:(w + 1) * TGW],
                                    PB[64:128, :], rbcs[64:128, 1, :],
                                    ALU.mult), "lv")
    pr.blk.vector(vbod)
    of_done_lv = pr.cnt["lv"]

    pr.blk.tensor(lambda t: t.wait_ge(pr.s("lv"), of_done_lv))
    pr.s("ev")
    def ev_fn(e, rt, rw, tg, ps):
        xs = zstage[:, rt * T + tg * TGW: rt * T + (tg + 1) * TGW]
        return e.tensor_tensor(xs, xs, ps[0:128, :], ALU.add)
    emit_proj_fm(pr, al, banks[:4],
                 rhs_tile=lambda dc, tg: of[:, dc, tg * TGW:(tg + 1) * TGW],
                 wT=io("wfoT"), n_in=512, n_out=D, ev_fn=ev_fn, tag="wo")
    z2_done = pr.cnt["ev"]
    extra = []
    if ZL_WRITES:
        zs0 = zstage[:, T - 1:T]
        zlap = bass.AP(zs0.tensor, zs0.offset, [list(zs0.ap[0]), [T, NDC]])
        zlc2 = al.t("zlcol2", [128, NDC], F32)
        pr.s("zlc")

        def evzl(e):
            e.wait_ge(pr.s("ev"), z2_done)
            inc(pr, e.tensor_copy(zlc2[:, :], zlap), "zlc")
        pr.blk.vector(evzl)
        extra = [(io("z2last"), zlc2[:, :], [("zlc", pr.cnt["zlc"])])]
    if extra:
        dma_seq(pr, "sync", extra, "dout")


def stage_ffn_ffn(pr, al, banks, io, final=False):
    cstv = load_consts(pr, al, io)
    zstage = al.t("xstage", [128, NDC * T], F32)
    xw = al.t("xw", [128, NDC * T], BF16)
    h = al.t("moe_h", [128, NK, TGW], BF16)
    arena1 = al.t("arena1", [128, 24576], BF16)
    arena2 = al.t("arena2", [128, 16384], BF16)
    zsh32 = al.overlay("zsh32", [128, 4, T], F32, "arena1", 0)
    al.overlay("lnsq_lnf", [128, 2, T], BF16, "arena1", 32768)
    al.overlay("lnxb_lnf", [128, 2, T], BF16, "arena1", 40960)
    wgu = al.overlay("ff_wgu", [128, NDC, 2816], BF16, "arena1", 0)
    al.overlay("lnbc_lnf", [128, 2 * T], F32, "arena2", 0)
    al.overlay("lnnrm_lnf", [128, 4, TGW], F32, "arena2", 16384)
    al.overlay("lnrows_lnf", [1, 4 * TGW], F32, "arena2", 24576)
    al.overlay("lnnrm2_lnf", [128, 4, TGW], F32, "moe_h", 0)
    wgd = al.overlay("ff_wgd", [128, NK, 1024], BF16, "arena2", 0)
    zg32 = al.t("zg32", [128, NDC], F32)
    lncols = al.t("lncols", [128, 2 * NDC], F32)
    b1c = al.t("fb1c", [128, 2, 22], F32)
    b2c = al.t("fb2c", [128, NDC], F32)
    h = al.t("moe_h", [128, NK, TGW], BF16)
    gbf = al.t("moe_g", [128, 2, TGW], BF16)
    dma_seq(pr, "sync", [(zg32[:, :], io("zg32"), []),
                         (lncols[:, :], io("lncols"), []),
                         (b1c[:, :, :], io("fb1cols"), []),
                         (b2c[:, :], io("fb2cols"), [])], "xin")
    xin_done = pr.cnt["xin"]
    pr.s("ev"); pr.s("ac"); pr.s("gp"); pr.s("mm")
    def evsh(e):
        e.wait_ge(pr.s("xin"), xin_done)
        for dch in range(4):
            e.tensor_copy(zsh32[:, dch, 1:T],
                          zstage[:, (4 + dch) * T:(5 + dch) * T - 1])
            inc(pr, e.tensor_copy(zsh32[:, dch, 0:1], zg32[:, 4 + dch:5 + dch]),
                "ev")
    pr.blk.vector(evsh)
    sh_done = pr.cnt["ev"]
    pr.blk.scalar(lambda a: a.wait_ge(pr.s("ev"), sh_done))
    pr.blk.vector(lambda e: e.wait_ge(pr.s("ev"), sh_done))
    def x_dc(dc):
        if dc < 4:
            return zstage[:, dc * T:(dc + 1) * T]
        return zsh32[:, dc - 4, 0:T]
    emit_ln(pr, al, banks, x_dc=x_dc,
            out_tile=lambda dc, tg: xw[:, dc * T + tg * TGW:
                                       dc * T + (tg + 1) * TGW],
            tag="lnf", cst_wait=cstv, pass1_eng="vector",
            affine_cols=(lncols[:, 0:NDC], lncols[:, NDC:2 * NDC]))
    ln_done = pr.cnt["ev"]
    if DBG_FF and not final:
        bc_t = al._cache["lnbc_lnf"]
        d1 = pr.nc.dram_tensor("dbgff_bc", [128, 2 * T], F32,
                               kind="ExternalOutput").ap()
        d2 = pr.nc.dram_tensor("dbgff_xw", [128, NDC * T], BF16,
                               kind="ExternalOutput").ap()
        d3 = pr.nc.dram_tensor("dbgff_zsh", [128, 4 * T], F32,
                               kind="ExternalOutput").ap()
        dma_seq(pr, "sync", [
            (d1, bc_t[:, :], [("ev", ln_done)]),
            (d2, xw[:, :], [("ev", ln_done)]),
            (d3, zsh32[:, :, :].rearrange("p a t -> p (a t)"),
             [("ev", ln_done)]),
        ], "dout")

    for s in ("mg", "mu", "md", "ag", "eu", "ed"):
        pr.s(s)
    mg0 = pr.cnt["mg"]; mu0 = pr.cnt["mu"]; md0 = pr.cnt["md"]
    ag0 = pr.cnt["ag"]; eu0 = pr.cnt["eu"]; ed0 = pr.cnt["ed"]
    guT = io("fw1T")
    gdT = io("fw2T")
    pr.s("mgu"); pr.s("mgd")
    mgu0 = pr.cnt["mgu"]; mgd0 = pr.cnt["mgd"]
    itemsA, itemsB = [], []
    for e in range(2):
        wu = [("ev", ln_done)] if e == 0 else [("mu", mu0 + NTG * NK * e)]
        wd = [("ev", ln_done)] if e == 0 else [("md", md0 + 32 * e)]
        gre = guT[e].rearrange("(dc p) o -> p dc o", p=128)
        dre = gdT[e].rearrange("(kc p) o -> p kc o", p=128)
        itemsA.append((wgu[:, :, 0:1408], gre[:, :, 0:1408], list(wu), "mgu"))
        itemsB.append((wgu[:, :, 1408:2816], gre[:, :, 1408:2816],
                       list(wu), "mgu"))
        itemsA.append((wgd[:, :, 0:512], dre[:, :, 0:512], list(wd), "mgd"))
        itemsB.append((wgd[:, :, 512:1024], dre[:, :, 512:1024],
                       list(wd), "mgd"))
    if MOE_OLD:
        inter = []
        for i in range(len(itemsA)):
            inter.append(itemsA[i]); inter.append(itemsB[i])
        dma_seq(pr, "gpsimd", inter, "mgu_unusedA")
    else:
        dma_seq(pr, "gpsimd", itemsA, "mgu_unusedA")
        dma_seq(pr, "sync", itemsB, "mgu_unusedB")
    tmpu = al.t("moe_tmpu", [128, TGW], BF16)

    def xlt(dc, tg):
        return xw[:, dc * T + tg * TGW: dc * T + (tg + 1) * TGW]

    def tmoe(t):
        t.wait_ge(pr.s("ev"), ln_done)
        for e in range(2):
            t.wait_ge(pr.s("mgu"), mgu0 + 32 * (e + 1))
            t.wait_ge(pr.s("mgd"), mgd0 + 32 * (e + 1))
            for tg in range(NTG):
                for k in range(NK):
                    for half in range(2):
                        jidx = ((e * NTG + tg) * NK + k) * 2 + half
                        bank = banks[jidx % 4]
                        pj = jidx - 4
                        if pj >= 0:
                            if pj % 2 == 0:
                                t.wait_ge(pr.s("eu"), eu0 + pj // 2 + 1)
                            else:
                                t.wait_ge(pr.s("ag"), ag0 + pj // 2 + 1)
                        last = None
                        for dc in range(NDC):
                            last = t.matmul(
                                bank[0:128, 0:TGW],
                                wgu[:, dc, (2 * k + half) * 128:
                                    (2 * k + half + 1) * 128],
                                xlt(dc, tg),
                                start=(dc == 0), stop=(dc == NDC - 1))
                        inc(pr, last, "mg" if half == 0 else "mu")
                for rt in range(NDC):
                    didx = (e * NTG + tg) * NDC + rt
                    bank = banks[4 + didx % 2]
                    if didx >= 2:
                        t.wait_ge(pr.s("ed"), ed0 + didx - 1)
                    t.wait_ge(pr.s("eu"), eu0 + (e * NTG + tg + 1) * NK)
                    last = None
                    for kc in range(NK):
                        last = t.matmul(bank[0:128, 0:TGW],
                                        wgd[:, kc, rt * 128:(rt + 1) * 128],
                                        h[:, kc, :],
                                        start=(kc == 0), stop=(kc == NK - 1))
                    inc(pr, last, "md")
    pr.blk.tensor(tmoe)

    def amoe(a):
        for e in range(2):
            for tg in range(NTG):
                for k in range(NK):
                    gidx = (e * NTG + tg) * NK + k
                    jidx = gidx * 2 + 1
                    a.wait_ge(pr.s("mu"), mu0 + gidx + 1)
                    if gidx >= 2:
                        a.wait_ge(pr.s("eu"), eu0 + gidx - 1)
                    inc(pr, a.activation(gbf[:, k % 2, :],
                                         banks[jidx % 4][0:128, 0:TGW],
                                         AF.Gelu, bias=b1c[:, e, 2 * k + 1:
                                                          2 * k + 2]), "ag")
    pr.blk.scalar(amoe)

    def vmoe(e_):
        for e in range(2):
            for tg in range(NTG):
                for k in range(NK):
                    gidx = (e * NTG + tg) * NK + k
                    jidx = gidx * 2
                    if k == 0 and (e, tg) != (0, 0):
                        e_.wait_ge(pr.s("md"), md0 + (e * NTG + tg) * NDC)
                    e_.wait_ge(pr.s("mg"), mg0 + gidx + 1)
                    e_.tensor_scalar_add(tmpu[:, :], banks[jidx % 4][0:128, 0:TGW],
                                         b1c[:, e, 2 * k:2 * k + 1])
                    e_.wait_ge(pr.s("ag"), ag0 + gidx + 1)
                    inc(pr, e_.tensor_tensor(h[:, k, :], tmpu[:, :],
                                             gbf[:, k % 2, :], ALU.mult), "eu")
                for rt in range(NDC):
                    didx = (e * NTG + tg) * NDC + rt
                    e_.wait_ge(pr.s("md"), md0 + didx + 1)
                    xs = zstage[:, rt * T + tg * TGW: rt * T + (tg + 1) * TGW]
                    inc(pr, e_.scalar_tensor_tensor(
                        xs, banks[4 + didx % 2][0:128, 0:TGW],
                        b2c[:, rt:rt + 1] if e == 1 else 0.0,
                        xs, ALU.add, ALU.add), "ed")
    pr.blk.vector(vmoe)
    z3_done = pr.cnt["ed"]

    if final:
        x3t = al.t("x3t", [128, 2, TGW], F32)
        pr.s("fx0"); pr.s("fx1")
        base_fx = [pr.cnt["fx0"], pr.cnt["fx1"]]
        ev_f0 = pr.cnt["ev"]
        items = []
        for j in range(NDC * NTG):
            dc, tg = j // NTG, j % NTG
            w = [("ed", z3_done)]
            if j >= 2:
                w.append(("ev", ev_f0 + j - 1))
            items.append((x3t[:, j % 2, :],
                          io("x3_d")[dc, :, tg * TGW:(tg + 1) * TGW], w,
                          f"fx{j % 2}"))
        dma_seq(pr, "sync", items, "fx_unused")
        def evf(e):
            for j in range(NDC * NTG):
                dc, tg = j // NTG, j % NTG
                e.wait_ge(pr.s(f"fx{j % 2}"), base_fx[j % 2] + 16 * (j // 2 + 1))
                xs = zstage[:, dc * T + tg * TGW: dc * T + (tg + 1) * TGW]
                inc(pr, e.tensor_tensor(xs, xs, x3t[:, j % 2, :], ALU.add), "ev")
        pr.blk.vector(evf)
        out_done = pr.cnt["ev"]
        dma_seq(pr, "sync", [(io("zout")[dc], zstage[:, dc * T:(dc + 1) * T],
                              [("ev", out_done)]) for dc in range(NDC)], "dout")
    else:
        extra = []
        if ZL_WRITES:
            zs0 = zstage[:, T - 1:T]
            zlap = bass.AP(zs0.tensor, zs0.offset, [list(zs0.ap[0]), [T, NDC]])
            zlc2 = al.t("zlcol2", [128, NDC], F32)
            pr.s("zlc")

            def evzl(e):
                e.wait_ge(pr.s("ed"), z3_done)
                inc(pr, e.tensor_copy(zlc2[:, :], zlap), "zlc")
            pr.blk.vector(evzl)
            extra = [(io("z3last"), zlc2[:, :], [("zlc", pr.cnt["zlc"])])]
        if extra:
            dma_seq(pr, "sync", extra, "dout")
